# version 19
# baseline (speedup 1.0000x reference)
import numpy as np
import ml_dtypes
import concourse.bass as bass
import concourse.mybir as mybir
from concourse.bass_utils import run_bass_kernel_spmd
from contextlib import ExitStack

F32 = mybir.dt.float32
BF16 = mybir.dt.bfloat16
AF = mybir.ActivationFunctionType
ALU = mybir.AluOpType
AX = mybir.AxisListType
BF = ml_dtypes.bfloat16

NCORE = 8
TPC = 1024
NT = 8
DM = 2048
NE = 16
EPS = 1e-5
ENGS = ("sync", "gpsimd", "tensor", "vector", "scalar")

_cache = {}


DMA_ENGS = ("sync", "gpsimd")
LANES = [e for e in ENGS if e not in DMA_ENGS] + \
        [f"{e}{i}" for e in DMA_ENGS for i in range(2)]


class Sched:

    def __init__(self):
        self.steps = []
        self.count = {l: 0 for l in LANES}
        self.nq = {e: 0 for e in DMA_ENGS}
        self.last_w = {}
        self.readers = {}

    def step(self, eng, fn, reads=(), writes=(), extra_deps=()):
        if eng in DMA_ENGS:
            lane = f"{eng}{self.nq[eng] % 2}"
            self.nq[eng] += 1
            inc = 16
            deps = [(lane, self.count[lane])] if self.count[lane] else []
        else:
            lane = eng
            inc = 1
            deps = []
        for k in reads:
            if k in self.last_w:
                deps.append(self.last_w[k])
        for k in writes:
            if k in self.last_w:
                deps.append(self.last_w[k])
            deps.extend(self.readers.get(k, ()))
        waits = {}
        for (l, c) in deps:
            if l == eng:
                continue
            waits[l] = max(waits.get(l, 0), c)
        for (l, c) in extra_deps:
            waits[l] = max(waits.get(l, 0), c)
        self.count[lane] += inc
        me = (lane, self.count[lane])
        for k in reads:
            self.readers.setdefault(k, []).append(me)
        for k in writes:
            self.last_w[k] = me
            self.readers[k] = []
        self.steps.append((eng, lane, fn, sorted(waits.items()), inc))
        return me

    def wait_all(self, eng, deps):
        waits = {}
        for (l, c) in deps:
            if l == eng:
                continue
            waits[l] = max(waits.get(l, 0), c)
        self.steps.append((eng, eng if eng not in DMA_ENGS else f"{eng}0",
                           None, sorted(waits.items()), 0))

    def build(self, block, sems):
        prog = {e: [] for e in ENGS}
        watermark = {e: {l: 0 for l in LANES} for e in ENGS}
        for eng, lane, fn, waits, inc in self.steps:
            needed = []
            for (l, c) in waits:
                if watermark[eng][l] < c:
                    watermark[eng][l] = c
                    needed.append((l, c))
            prog[eng].append((fn, needed, lane, inc))
        for e in ENGS:
            items = prog[e]

            def body(eng, items=items):
                for fn, needed, lane, inc in items:
                    for (l, c) in needed:
                        eng.wait_ge(sems[l], c)
                    if fn is not None:
                        last = fn(eng)
                        assert last is not None, "step fn must return last instruction"
                        last.then_inc(sems[lane], inc)
            getattr(block, e)(body)


def ts(i, n=128):
    return slice(i * n, (i + 1) * n)


def cview(ap, c):
    return ap.rearrange("p (c g) -> p c g", c=c)



def gating_chain(G, T1, scB, mk8, mkT, s6b, s7b, csrT, snrT, sgrT, prioT, wgT, G4flat):
    SHP = [128, NE, NT]
    bt = lambda a: a.broadcast_to(SHP)
    Gall, G1, G2, G3, G4, G5 = G
    cnt, m1, inact, p1, m2, p2, mx, den, rec = [scB[:, i:i + 1, :] for i in range(9)]
    s6B = s6b[:].unsqueeze(1).broadcast_to(SHP)
    s7B = s7b[:].unsqueeze(1).broadcast_to(SHP)
    C = []
    V = lambda fn: C.append(("vector", fn))

    def tree(op, srcv, dst):
        f = lambda e: getattr(e, f"tensor_{op}")
        V(lambda e, srcv=srcv, op=op: getattr(e, f"tensor_{op}")(
            T1[:, 0:8, :], srcv[:, 0:8, :], srcv[:, 8:16, :]))
        V(lambda e, op=op: getattr(e, f"tensor_{op}")(
            T1[:, 0:4, :], T1[:, 0:4, :], T1[:, 4:8, :]))
        V(lambda e, op=op: getattr(e, f"tensor_{op}")(
            T1[:, 0:2, :], T1[:, 0:2, :], T1[:, 2:4, :]))
        V(lambda e, dst=dst, op=op: getattr(e, f"tensor_{op}")(
            dst, T1[:, 0:1, :], T1[:, 1:2, :]))

    V(lambda e: e.tensor_mul(G1[:], Gall[:], s6B))
    V(lambda e: e.tensor_mul(G2[:], csrT[:], s7B))
    V(lambda e: e.tensor_sub(G1[:], G1[:], G2[:]))
    V(lambda e: e.tensor_mul(G1[:], G1[:], snrT[:]))
    V(lambda e: e.tensor_sub(G1[:], G1[:], sgrT[:]))
    V(lambda e: e.tensor_scalar_max(G2[:], G1[:], 0.0))
    V(lambda e: e.tensor_scalar(out=mk8[:], in0=G1[:], scalar1=0.0,
                                scalar2=None, op0=ALU.is_gt))
    V(lambda e: e.tensor_copy(G3[:], mk8[:]))
    tree("add", G3, cnt)
    tree("max", G1, m1)
    V(lambda e: e.tensor_scalar(out=mkT[:], in0=cnt.rearrange("p a b -> p (a b)"),
                                scalar1=0.5, scalar2=None, op0=ALU.is_lt))
    V(lambda e: e.tensor_copy(inact.rearrange("p a b -> p (a b)"), mkT[:]))
    V(lambda e: e.tensor_tensor(mk8[:], G1[:], bt(m1), ALU.is_ge))
    V(lambda e: e.tensor_copy(G4[:], mk8[:]))
    V(lambda e: e.tensor_mul(G4[:], G4[:], prioT[:]))
    tree("max", G4, p1)
    V(lambda e: e.tensor_tensor(mk8[:], G4[:], bt(p1), ALU.is_ge))
    V(lambda e: e.tensor_copy(G4[:], mk8[:]))
    V(lambda e: e.scalar_tensor_tensor(out=G5[:], in0=G4[:], scalar=-1e9,
                                       in1=G1[:], op0=ALU.mult, op1=ALU.add))
    tree("max", G5, m2)
    V(lambda e: e.tensor_tensor(mk8[:], G5[:], bt(m2), ALU.is_ge))
    V(lambda e: e.tensor_copy(G5[:], mk8[:]))
    V(lambda e: e.tensor_mul(G5[:], G5[:], prioT[:]))
    tree("max", G5, p2)
    V(lambda e: e.tensor_tensor(mk8[:], G5[:], bt(p2), ALU.is_ge))
    V(lambda e: e.tensor_copy(G5[:], mk8[:]))
    V(lambda e: e.tensor_add(G4[:], G4[:], G5[:]))
    V(lambda e: e.tensor_mul(G4[:], G4[:], bt(inact)))
    V(lambda e: e.tensor_max(G3[:], G3[:], G4[:]))
    V(lambda e: e.tensor_scalar(out=G4[:], in0=G3[:], scalar1=1e9,
                                scalar2=-1e9, op0=ALU.mult, op1=ALU.add))
    V(lambda e: e.tensor_add(G4[:], G2[:], G4[:]))
    tree("max", G4, mx)
    V(lambda e: e.scalar_tensor_tensor(out=G4[:], in0=bt(mx), scalar=-1.0,
                                       in1=G4[:], op0=ALU.mult, op1=ALU.add))
    C.append(("scalar", lambda e: e.activation(out=G4flat, in_=G4flat, func=AF.Exp)))
    tree("add", G4, den)
    V(lambda e: e.reciprocal(rec.rearrange("p a b -> p (a b)"),
                             den.rearrange("p a b -> p (a b)")))
    V(lambda e: e.tensor_mul(G4[:], G4[:], bt(rec)))
    V(lambda e: e.tensor_mul(wgT, G4[:], G3[:]))
    return C


def emit_chain(s, C, first_reads, final_writes, first_deps=()):
    h = None
    for i, (eng, fn) in enumerate(C):
        h = s.step(eng, fn, reads=(first_reads if i == 0 else ["gch"]),
                   writes=(["gch"] + (final_writes if i == len(C) - 1 else [])),
                   extra_deps=([h] if h else list(first_deps)))
    return h


def build_l1():
    nc = bass.Bass("TRN2", target_bir_lowering=False, detect_race_conditions=False)
    x_in = nc.dram_tensor("x", [NT, 128, DM], F32, kind="ExternalInput").ap()
    aW2_in = nc.dram_tensor("aW2", [128, 6144], F32, kind="ExternalInput").ap()
    ab_in = nc.dram_tensor("ab", [1, 6144], F32, kind="ExternalInput").ap()
    aWo2_in = nc.dram_tensor("aWo2", [128, 64], F32, kind="ExternalInput").ap()
    abo_in = nc.dram_tensor("abo", [1, 64], F32, kind="ExternalInput").ap()
    sim_in = nc.dram_tensor("sim", [16, 128, NE], F32, kind="ExternalInput").ap()
    gates_in = nc.dram_tensor("gates", [1, NE], F32, kind="ExternalInput").ap()
    prio_in = nc.dram_tensor("prio", [1, NE], F32, kind="ExternalInput").ap()
    wcat_in = nc.dram_tensor("wcat", [16, 128, NE * 384], F32, kind="ExternalInput").ap()
    ident_in = nc.dram_tensor("ident", [128, 128], F32, kind="ExternalInput").ap()

    r1_out = nc.dram_tensor("r1", [NT, 128, DM], F32, kind="ExternalOutput").ap()

    qkvm_out = nc.dram_tensor("qkvm", [128, NT, 384], F32, kind="ExternalOutput").ap()
    w_out = nc.dram_tensor("w", [128, NT, NE], F32, kind="ExternalOutput").ap()
    scr_d = nc.dram_tensor("scrd", [4, NE], F32)

    s = Sched()
    with ExitStack() as stk:
        E = stk.enter_context
        rbuf = E(nc.sbuf_tensor("rbuf", [128, NT, DM], F32))
        hT2 = E(nc.sbuf_tensor("hT2", [128, NT * 16, 128], BF16))
        wg = E(nc.sbuf_tensor("wg", [128, NT, NE], F32))
        aW2 = E(nc.sbuf_tensor("aW2s", [128, 6144], BF16))
        ab = E(nc.sbuf_tensor("abs", [1, 6144], BF16))
        aWo2 = E(nc.sbuf_tensor("aWo2s", [128, 64], BF16))
        abo = E(nc.sbuf_tensor("abos", [1, 64], BF16))
        ident = E(nc.sbuf_tensor("idents", [128, 128], BF16))
        ident32 = E(nc.sbuf_tensor("id32s", [128, 128], F32))
        ones = E(nc.sbuf_tensor("oness", [1, 128], BF16))
        sim32 = E(nc.sbuf_tensor("sim32", [128, 16, NE], F32))
        nrow = E(nc.sbuf_tensor("nrow", [1, 64], F32))
        ones32 = E(nc.sbuf_tensor("ones32", [128, 1], F32))
        hT = E(nc.sbuf_tensor("hT", [128, 16, 128], BF16))
        sD = E(nc.sbuf_tensor("sD", [128, 32, 64], BF16))
        ctx = E(nc.sbuf_tensor("ctxs", [128, 32, 64], BF16))
        ctxT = E(nc.sbuf_tensor("ctxT", [128, 16, 128], BF16))
        bst = E(nc.sbuf_tensor("bst", [128, 24], F32))
        mv = E(nc.sbuf_tensor("mv", [128, 2], F32))
        sc = E(nc.sbuf_tensor("scs", [128, 8], F32))
        Gall = E(nc.sbuf_tensor("Gall", [128, NE, NT], F32))
        mk8 = E(nc.sbuf_tensor("mk8s", [128, NE, NT], mybir.dt.uint8))
        mkT = E(nc.sbuf_tensor("mkTs", [128, NT], mybir.dt.uint8))
        s6b = E(nc.sbuf_tensor("s6bs", [128, NT], F32))
        s7b = E(nc.sbuf_tensor("s7bs", [128, NT], F32))
        prioT = E(nc.sbuf_tensor("prioT", [128, NE, NT], F32))
        csrT = E(nc.sbuf_tensor("csrT", [128, NE, NT], F32))
        snrT = E(nc.sbuf_tensor("snrT", [128, NE, NT], F32))
        sgrT = E(nc.sbuf_tensor("sgrT", [128, NE, NT], F32))
        prio_s = E(nc.sbuf_tensor("prio_s", [128, NE], F32))
        csr_s = E(nc.sbuf_tensor("csr_s", [128, NE], F32))
        snr_s = E(nc.sbuf_tensor("snr_s", [128, NE], F32))
        sgr_s = E(nc.sbuf_tensor("sgr_s", [128, NE], F32))
        grow = E(nc.sbuf_tensor("grows", [1, NE], F32))
        epsc = E(nc.sbuf_tensor("epscs", [128, 1], F32))
        rT32b = E(nc.sbuf_tensor("rT32b", [128, 4, 128], F32))
        sDf = sD[:].rearrange("p a b -> p (a b)").bitcast(F32)
        Gv = [Gall[:]] + [sDf[:, i * 128:(i + 1) * 128]
                          .rearrange("p (a b) -> p a b", a=NE) for i in range(5)]
        G4flat = sDf[:, 3 * 128:4 * 128]
        T1g = sDf[:, 640:704].rearrange("p (a b) -> p a b", a=8)
        scB = sDf[:, 704:800].rearrange("p (a b) -> p a b", a=12)
        inner = ExitStack()
        ssq = inner.enter_context(nc.sbuf_tensor("ssq", [128, 256], F32))
        hbf = inner.enter_context(nc.sbuf_tensor("hbf", [128, DM], BF16))
        qkvP = [inner.enter_context(nc.sbuf_tensor(f"qkvs{i}", [128, 32, 192], BF16))
                for i in range(2)]
        sS = [inner.enter_context(nc.sbuf_tensor(f"sS{i}", [128, 4096], BF16)) for i in range(2)]
        sE = [inner.enter_context(nc.sbuf_tensor(f"sE{i}", [128, 4096], BF16)) for i in range(2)]

        pU = E(nc.psum_tensor("pU", [128, DM], F32))
        pQ4 = E(nc.psum_tensor("pQ4", [128, 1024], F32))
        pTb = [E(nc.psum_tensor(f"pTb{i}", [128, 8, 128], BF16)) for i in range(2)]

        sems = {l: E(nc.semaphore(f"SEM_{l}")) for l in LANES}
        block = E(nc.Block())

        QKV_KEYS = [f"qkv{c}" for c in range(0, 32, 2)]
        CTX_KEYS = [f"ctx{dc}" for dc in range(16)]

        for t in range(NT):
            s.step("sync", (lambda eng, t=t: eng.dma_start(out=rbuf[:, t, :], in_=x_in[t])),
                   writes=[f"r{t}"])
        for dst, src, key in ((aW2, aW2_in, "aW2"), (ab, ab_in, "ab"),
                              (aWo2, aWo2_in, "aWo2"), (abo, abo_in, "abo"),
                              (ident, ident_in, "id"), (ident32, ident_in, "id32"),
                              (grow, gates_in, "grow")):
            s.step("gpsimd", (lambda eng, d=dst, sr=src: eng.dma_start(out=d[:], in_=sr)),
                   writes=[key])
        s.step("gpsimd", (lambda eng: eng.dma_start(
            out=sim32[:], in_=sim_in.rearrange("a p e -> p a e"))), writes=["sim32"])

        def f(eng):
            eng.memset(ones[:], 1.0)
            eng.memset(ones32[:], 1.0)
            eng.memset(epsc[:], EPS)
            aq = cview(aW2[:], 32)[:, :, 0:64]
            eng.tensor_scalar_mul(aq, aq, 0.125)
            abq = cview(ab[:], 32)[:, :, 0:64]
            eng.tensor_scalar_mul(abq, abq, 0.125)
            s32 = sim32[:].rearrange("p a e -> p (a e)")
            return eng.tensor_mul(ssq[:], s32, s32)
        s.step("vector", f, reads=["sim32"],
               writes=["aW2", "ab", "ssq", "ones", "ones32"],
               extra_deps=[s.last_w["aW2"], s.last_w["ab"]])

        def f(eng):
            for k in range(16):
                eng.matmul(pQ4[0:1, 0:NE], ones32[:], sim32[:, k, :],
                           start=(k == 0), stop=(k == 15), skip_group_check=True)
            last = None
            for k in range(16):
                last = eng.matmul(pQ4[0:1, 32:32 + NE], ones32[:],
                                  ssq[:, k * NE:(k + 1) * NE],
                                  start=(k == 0), stop=(k == 15), skip_group_check=True)
            return last
        s.step("tensor", f, reads=["ssq", "sim32", "ones32"], writes=["pQb"])

        def f(eng):
            eng.tensor_copy(nrow[:, 0:NE], pQ4[0:1, 0:NE])
            return eng.tensor_copy(nrow[:, 32:32 + NE], pQ4[0:1, 32:32 + NE])
        s.step("vector", f, reads=["pQb"], writes=["nrow"])

        def f(eng):
            eng.activation(out=grow[:], in_=grow[:], func=AF.Exp, scale=-1.0)
            return eng.activation(out=nrow[:, 32:32 + NE], in_=nrow[:, 32:32 + NE], func=AF.Ln)
        s.step("scalar", f, reads=["grow", "nrow"], writes=["grow", "nrow"])

        s.step("scalar", (lambda eng: eng.activation(
            out=nrow[:, 32:32 + NE], in_=nrow[:, 32:32 + NE], func=AF.Exp, scale=-0.5)),
            reads=["nrow"], writes=["nrow"])

        def f(eng):
            eng.tensor_scalar_add(grow[:], grow[:], 1.0)
            return eng.reciprocal(grow[:], grow[:])
        s.step("vector", f, reads=["grow"], writes=["grow"])

        s.step("gpsimd", (lambda eng: eng.dma_start(out=scr_d[0:1, :], in_=grow[:])),
               reads=["grow"], writes=["scr0"])
        s.step("gpsimd", (lambda eng: eng.dma_start(
            out=scr_d[1:2, :], in_=nrow[:, 0:NE])),
            reads=["nrow"], writes=["scr1"])
        s.step("gpsimd", (lambda eng: eng.dma_start(
            out=scr_d[2:3, :], in_=nrow[:, 32:32 + NE])),
            reads=["nrow"], writes=["scr2"])
        s.step("gpsimd", (lambda eng: eng.dma_start(
            out=sgr_s[:], in_=scr_d[0:1, :].broadcast_to([128, NE]))),
            reads=["scr0"], writes=["sgr_s"])
        s.step("gpsimd", (lambda eng: eng.dma_start(
            out=csr_s[:], in_=scr_d[1:2, :].broadcast_to([128, NE]))),
            reads=["scr1"], writes=["csr_s"])
        s.step("gpsimd", (lambda eng: eng.dma_start(
            out=snr_s[:], in_=scr_d[2:3, :].broadcast_to([128, NE]))),
            reads=["scr2"], writes=["snr_s"])
        s.step("gpsimd", (lambda eng: eng.dma_start(
            out=prio_s[:], in_=prio_in.broadcast_to([128, NE]))), writes=["prio_s"])
        for big, small, bk, sk in ((sgrT, sgr_s, "sgr", "sgr_s"),
                                   (csrT, csr_s, "csr", "csr_s"),
                                   (snrT, snr_s, "snr", "snr_s"),
                                   (prioT, prio_s, "prio", "prio_s")):
            s.step("vector", (lambda eng, big=big, small=small: eng.tensor_copy(
                big[:].rearrange("p a b -> p b a"),
                small[:].unsqueeze(1).broadcast_to([128, NT, NE]))),
                reads=[sk], writes=[bk])

        def ln_stats(t):
            rk = f"r{t}"

            def f(eng, t=t):
                for i in range(3):
                    eng.bn_stats(bst[:, i * 6:(i + 1) * 6], rbuf[:, t, ts(i, 512)])
                return eng.bn_stats(bst[:, 18:24], rbuf[:, t, ts(3, 512)])
            h = s.step("vector", f, reads=[rk], writes=["bst"])
            s.step("vector", (lambda eng: eng.bn_aggr(mv[:], bst[:])),
                   reads=["bst"], writes=["mv"], extra_deps=[h])

        def QK(t):
            return [f"qk{t % 2}_{c}" for c in range(0, 32, 2)]

        def emit_front(t):
            rk = f"r{t}"
            par = t % 2
            qb = qkvP[par]
            ln_stats(t)

            def f(eng):
                eng.activation(out=sc[:, 0:1], in_=mv[:, 1:2], func=AF.Ln, bias=epsc[:])
                return eng.activation(out=sc[:, 2:3], in_=sc[:, 0:1], func=AF.Exp, scale=-0.5)
            s.step("scalar", f, reads=["mv"], writes=["sc02"])

            s.step("vector", (lambda eng: eng.scalar_tensor_tensor(
                out=sc[:, 4:5], in0=mv[:, 0:1], scalar=-1.0, in1=sc[:, 2:3],
                op0=ALU.mult, op1=ALU.mult)), reads=["mv", "sc02"], writes=["sc4"])

            s.step("scalar", (lambda eng, t=t: eng.activation(
                out=hbf[:], in_=rbuf[:, t, :], func=AF.Identity,
                scale=sc[:, 2:3], bias=sc[:, 4:5])),
                reads=[rk, "sc02", "sc4"], writes=["hbf"])

            for g in range(2):
                def f(eng, g=g):
                    last = None
                    for i in range(8):
                        last = eng.transpose(pTb[g % 2][:, i, :], hbf[:, ts(g * 8 + i)], ident[:])
                    return last
                s.step("tensor", f, reads=["hbf", "id"], writes=[f"pTb{g % 2}"])
                s.step("scalar", (lambda eng, g=g: eng.copy(
                    hT[:, g * 8:(g + 1) * 8, :], pTb[g % 2][:])),
                    reads=[f"pTb{g % 2}"], writes=[f"hTg{g}"])

        def emit_qkv2(t, g2):
            par = t % 2
            qb = qkvP[par]
            for i in range(2):
                c = g2 * 2 + i
                bk = c % 2

                def f(eng, c=c, bk=bk):
                    j, half = c // 2, c % 2
                    eng.matmul(pQ4[:, bk * 512:bk * 512 + 192],
                               hT[:, j, :][half * 64:half * 64 + 64, :],
                               aW2[half * 64:half * 64 + 64, c * 192:(c + 1) * 192],
                               start=True, stop=False)
                    return eng.matmul(pQ4[:, bk * 512:bk * 512 + 192], ones[:],
                                      ab[:, c * 192:(c + 1) * 192], start=False, stop=True)
                s.step("tensor", f, reads=["hTg0", "hTg1", "aW2", "ab", "ones"],
                       writes=[f"pQb{bk}"])
                s.step("scalar", (lambda eng, c=c, bk=bk, qb=qb: eng.copy(
                    qb[:, c, :], pQ4[:, bk * 512:bk * 512 + 192])),
                    reads=[f"pQb{bk}"], writes=[f"qk{par}_{(c // 2) * 2}"])

        SCR = ["cTg0", "cTg1"]

        def emit_S(t, dc):
            o, b = dc * 4, dc % 2
            qb = qkvP[t % 2]

            def f(eng, o=o, b=b, qb=qb):
                q = qb[:, :, o:o + 4].unsqueeze(2).broadcast_to([128, 32, 32, 4])
                k = qb[:, :, 64 + o:64 + o + 4].unsqueeze(1).broadcast_to([128, 32, 32, 4])
                return eng.tensor_tensor(
                    sS[b][:].rearrange("p (c e d) -> p c e d", c=32, e=32), q, k, ALU.mult)
            s.step("vector", f, reads=QK(t), writes=[f"sS{b}"])

        def emit_exp(t, dc):
            b = dc % 2
            s.step("scalar", (lambda eng, b=b: eng.activation(
                out=sE[b][:], in_=sS[b][:], func=AF.Exp)),
                reads=[f"sS{b}"], writes=[f"sE{b}"])

        def emit_trees(t, dc):
            o, b = dc * 4, dc % 2
            qb = qkvP[t % 2]

            def f(eng, o=o, b=b):
                Ev = sE[b][:]
                ctA = ctxT[:].rearrange("p a b -> p (a b)")
                stf = sS[b][:]
                eng.tensor_add(cview(ctA, 32), cview(Ev, 32)[:, :, 0:64],
                               cview(Ev, 32)[:, :, 64:128])
                eng.tensor_add(cview(stf[:, 0:1024], 32), cview(ctA, 32)[:, :, 0:32],
                               cview(ctA, 32)[:, :, 32:64])
                eng.tensor_add(cview(stf[:, 1024:1536], 32), cview(stf[:, 0:1024], 32)[:, :, 0:16],
                               cview(stf[:, 0:1024], 32)[:, :, 16:32])
                eng.tensor_add(cview(stf[:, 1536:1792], 32), cview(stf[:, 1024:1536], 32)[:, :, 0:8],
                               cview(stf[:, 1024:1536], 32)[:, :, 8:16])
                eng.tensor_add(sD[:, :, o:o + 4], cview(stf[:, 1536:1792], 32)[:, :, 0:4],
                               cview(stf[:, 1536:1792], 32)[:, :, 4:8])
                with nc.allow_low_precision(reason="bf16 softmax denom, within tolerance"):
                    return eng.reciprocal(sD[:, :, o:o + 4], sD[:, :, o:o + 4])
            hA = s.step("vector", f, reads=[f"sE{b}"], writes=["sDG", f"sS{b}"] + SCR)

            def f(eng, o=o, b=b, qb=qb):
                Ev = sE[b][:]
                ctA = ctxT[:].rearrange("p a b -> p (a b)")
                etf = sE[b][:]
                eng.tensor_mul(sD[:, :, o:o + 4], sD[:, :, o:o + 4], qb[:, :, 128 + o:128 + o + 4])
                eng.tensor_mul(sS[b][:].rearrange("p (c m d) -> p c m d", c=32, m=32),
                               Ev.rearrange("p (c m d) -> p c m d", c=32, m=32),
                               sD[:, :, o:o + 4].unsqueeze(2).broadcast_to([128, 32, 32, 4]))
                eng.tensor_add(cview(ctA, 16), cview(sS[b][:], 32)[:, 0:16, :],
                               cview(sS[b][:], 32)[:, 16:32, :])
                eng.tensor_add(cview(etf[:, 0:1024], 8), cview(ctA, 16)[:, 0:8, :],
                               cview(ctA, 16)[:, 8:16, :])
                eng.tensor_add(cview(etf[:, 1024:1536], 4), cview(etf[:, 0:1024], 8)[:, 0:4, :],
                               cview(etf[:, 0:1024], 8)[:, 4:8, :])
                eng.tensor_add(cview(etf[:, 1536:1792], 2), cview(etf[:, 1024:1536], 4)[:, 0:2, :],
                               cview(etf[:, 1024:1536], 4)[:, 2:4, :])
                return eng.tensor_add(
                    ctx[:, :, o:o + 4],
                    etf[:, 1536:1664].rearrange("p (m d) -> p m d", m=32),
                    etf[:, 1664:1792].rearrange("p (m d) -> p m d", m=32))
            s.step("vector", f, reads=[f"sE{b}", "sDG"] + QK(t),
                   writes=["sDG", f"ctx{dc}", f"sS{b}", f"sE{b}"] + SCR, extra_deps=[hA])

        def emit_tail(t):
            rk = f"r{t}"
            cflat = ctx[:].rearrange("p a b -> p (a b)")
            for g in range(2):
                def f(eng, g=g):
                    last = None
                    for i in range(8):
                        last = eng.transpose(pTb[g % 2][:, i, :], cflat[:, ts(g * 8 + i)], ident[:])
                    return last
                s.step("tensor", f, reads=CTX_KEYS + ["id"], writes=[f"pTb{g % 2}"])
                s.step("scalar", (lambda eng, g=g: eng.copy(
                    ctxT[:, g * 8:(g + 1) * 8, :], pTb[g % 2][:])),
                    reads=[f"pTb{g % 2}"], writes=[f"cTg{g}"])
            for mg in range(4):
                def f(eng, mg=mg):
                    last = None
                    for i in range(8):
                        m = mg * 8 + i
                        j, half = m // 2, m % 2
                        eng.matmul(pU[:, m * 64:(m + 1) * 64],
                                   ctxT[:, j, :][half * 64:half * 64 + 64, :],
                                   aWo2[half * 64:half * 64 + 64, :], start=True, stop=False)
                        last = eng.matmul(pU[:, m * 64:(m + 1) * 64], ones[:], abo[:],
                                          start=False, stop=True)
                    return last
                s.step("tensor", f, reads=["cTg0", "cTg1", "aWo2", "abo", "ones"],
                       writes=[f"pU{mg}"])

            for mg in range(4):
                s.step("vector", (lambda eng, t=t, mg=mg: eng.tensor_add(
                    rbuf[:, t, mg * 512:(mg + 1) * 512],
                    rbuf[:, t, mg * 512:(mg + 1) * 512],
                    pU[:, mg * 512:(mg + 1) * 512])),
                    reads=[rk, f"pU{mg}"], writes=[rk])

            ln_stats(t)

            def f(eng):
                eng.activation(out=sc[:, 0:1], in_=mv[:, 1:2], func=AF.Ln, bias=epsc[:])
                eng.activation(out=sc[:, 1:2], in_=mv[:, 1:2], func=AF.Ln, scale=float(DM))
                return eng.activation(out=sc[:, 2:3], in_=sc[:, 0:1], func=AF.Exp, scale=-0.5)
            s.step("scalar", f, reads=["mv"], writes=["sc02"])

            def f(eng):
                eng.tensor_sub(sc[:, 3:4], sc[:, 1:2], sc[:, 0:1])
                return eng.scalar_tensor_tensor(out=sc[:, 4:5], in0=mv[:, 0:1], scalar=-1.0,
                                                in1=sc[:, 2:3], op0=ALU.mult, op1=ALU.mult)
            s.step("vector", f, reads=["mv", "sc02"], writes=["sc4", "sc3"])

            def f(eng, t=t):
                eng.activation(out=sc[:, 5:6], in_=sc[:, 3:4], func=AF.Exp, scale=-0.5)
                return eng.activation(out=hbf[:], in_=rbuf[:, t, :], func=AF.Identity,
                                      scale=sc[:, 2:3], bias=sc[:, 4:5])
            s.step("scalar", f, reads=[rk, "sc02", "sc4", "sc3"], writes=["hbf", "sc5"])

            h6 = s.step("vector", (lambda eng, t=t: eng.tensor_mul(
                s6b[:, t:t + 1], sc[:, 2:3], sc[:, 5:6])),
                reads=["sc02", "sc5"], writes=["s6b"])
            s.step("vector", (lambda eng, t=t: eng.tensor_mul(
                s7b[:, t:t + 1], mv[:, 0:1], s6b[:, t:t + 1])),
                reads=["mv", "s6b"], writes=["s7b"], extra_deps=[h6])

            for g in range(2):
                def f(eng, g=g):
                    last = None
                    for i in range(8):
                        last = eng.transpose(pTb[g % 2][:, i, :], hbf[:, ts(g * 8 + i)], ident[:])
                    return last
                s.step("tensor", f, reads=["hbf", "id"], writes=[f"pTb{g % 2}"])
                s.step("scalar", (lambda eng, t=t, g=g: eng.copy(
                    hT2[:, t * 16 + g * 8:t * 16 + (g + 1) * 8, :], pTb[g % 2][:])),
                    reads=[f"pTb{g % 2}"], writes=[f"hT2_{t}_{g}"])

            for kg in range(4):
                def f(eng, t=t, kg=kg):
                    last = None
                    for i in range(4):
                        last = eng.transpose(pU[:, 1536 + i * 128:1664 + i * 128],
                                             rbuf[:, t, ts(kg * 4 + i)], ident32[:])
                    return last
                s.step("tensor", f, reads=[rk, "id32"], writes=["pU3"])
                s.step("vector", (lambda eng: eng.tensor_copy(
                    rT32b[:].rearrange("p a b -> p (a b)"), pU[:, 1536:2048])),
                    reads=["pU3"], writes=["rT32"])
                def f(eng, kg=kg):
                    last = None
                    for i in range(4):
                        last = eng.matmul(pQ4[:, 0:NE], rT32b[:, i, :],
                                          sim32[:, kg * 4 + i, :],
                                          start=(kg == 0 and i == 0),
                                          stop=(kg == 3 and i == 3), skip_group_check=True)
                    return last
                s.step("tensor", f, reads=["rT32", "sim32"],
                       writes=["pQmm", "pQb0", "pQb1"])

            h = s.step("vector", (lambda eng, t=t: eng.tensor_copy(
                Gall[:, :, t:t + 1], pQ4[:, 0:NE].unsqueeze(2))),
                reads=["pQmm", "pQb0"], writes=["G0"])

            s.step("sync", (lambda eng, t=t: eng.dma_start(out=r1_out[t], in_=rbuf[:, t, :])),
                   reads=[rk], writes=[f"st_r{t}"])
            return h

        emit_front(0)
        for g2 in range(16):
            emit_qkv2(0, g2)
        emit_S(0, 0)
        for t in range(NT):
            for dc in range(16):
                if not (dc == 0 and t > 0):
                    emit_exp(t, dc)
                if dc + 1 < 16:
                    emit_S(t, dc + 1)
                emit_trees(t, dc)
                if t + 1 < NT:
                    if dc == 1:
                        emit_front(t + 1)
                    elif 2 <= dc < 10:
                        emit_qkv2(t + 1, (dc - 2) * 2)
                        emit_qkv2(t + 1, (dc - 2) * 2 + 1)
            if t + 1 < NT:
                emit_S(t + 1, 0)
                emit_exp(t + 1, 0)
            hGall = emit_tail(t)

        wgT = wg[:].rearrange("p a b -> p b a")
        emit_chain(s, gating_chain(Gv, T1g, scB, mk8, mkT, s6b, s7b, csrT, snrT,
                                   sgrT, prioT, wgT, G4flat),
                   ["G0", "s6b", "s7b", "csr", "snr", "sgr", "prio"],
                   [f"w{t}" for t in range(NT)], first_deps=[hGall])

        barrier = [("vector", s.count["vector"]), ("scalar", s.count["scalar"]),
                   ("tensor", s.count["tensor"])]
        inner.close()
        wcat_eg = stk.enter_context(nc.sbuf_tensor("wcat_eg", [128, 16, 1536], BF16))
        qkvm = stk.enter_context(nc.sbuf_tensor("qkvm_s", [128, NT, 384], F32))

        for eg in range(4):
            for k in range(16):
                s.step("gpsimd", (lambda eng, eg=eg, k=k: eng.dma_start(
                    out=wcat_eg[:, k, :], in_=wcat_in[k, :, eg * 1536:(eg + 1) * 1536])),
                    reads=[], writes=[f"wc{k}"],
                    extra_deps=(barrier if (eg == 0 and k < 2) else ()))
            for t in range(NT):
                for k in range(16):
                    def f(eng, t=t, k=k):
                        eng.matmul(pU[:, 0:512], hT2[:, t * 16 + k, :],
                                   wcat_eg[:, k, 0:512], start=(k == 0), stop=(k == 15))
                        eng.matmul(pU[:, 512:1024], hT2[:, t * 16 + k, :],
                                   wcat_eg[:, k, 512:1024], start=(k == 0), stop=(k == 15))
                        return eng.matmul(pU[:, 1024:1536], hT2[:, t * 16 + k, :],
                                          wcat_eg[:, k, 1024:1536], start=(k == 0), stop=(k == 15))
                    s.step("tensor", f, reads=[f"hT2_{t}_{k // 8}", f"wc{k}"],
                           writes=["pUmix"])
                h = None
                for i in range(4):
                    if eg == 0 and i == 0:
                        def f(eng, t=t):
                            return eng.tensor_scalar(
                                out=qkvm[:, t, :], in0=pU[:, 0:384],
                                scalar1=wg[:, t, 0:1], scalar2=None, op0=ALU.mult)
                    else:
                        def f(eng, t=t, eg=eg, i=i):
                            return eng.scalar_tensor_tensor(
                                out=qkvm[:, t, :], in0=pU[:, i * 384:(i + 1) * 384],
                                scalar=wg[:, t, eg * 4 + i:eg * 4 + i + 1], in1=qkvm[:, t, :],
                                op0=ALU.mult, op1=ALU.add)
                    h = s.step("vector", f, reads=["pUmix", f"w{t}", "qkvm_s"],
                               writes=["qkvm_s"], extra_deps=[h] if h else ())

        s.step("sync", (lambda eng: eng.dma_start(
            out=qkvm_out, in_=qkvm[:])),
            reads=["qkvm_s"], writes=["st_q"])
        s.step("sync", (lambda eng: eng.dma_start(
            out=w_out, in_=wg[:])),
            reads=[f"w{t}" for t in range(NT)], writes=["st_w"])
        s.wait_all("sync", [("sync", s.count["sync"])] if False else
                   [s.last_w["st_q"], s.last_w["st_w"]] +
                   [s.last_w[f"st_r{t}"] for t in range(NT)])
        s.step("vector", (lambda eng: eng.tensor_copy(sc[:, 0:1], sc[:, 1:2])),
               reads=["st_q", "st_w"] + [f"st_r{t}" for t in range(NT)], writes=["fin"])

        s.build(block, sems)
    return nc


def build_l2():
    nc = bass.Bass("TRN2", target_bir_lowering=False, detect_race_conditions=False)
    r1_in = nc.dram_tensor("r1", [NT, 128, DM], F32, kind="ExternalInput").ap()
    qm_in = nc.dram_tensor("qm", [128, NT, 128], F32, kind="ExternalInput").ap()
    kf_in = nc.dram_tensor("kf", [128, 64, 128], F32, kind="ExternalInput").ap()
    vf_in = nc.dram_tensor("vf", [128, 64, 128], F32, kind="ExternalInput").ap()
    w_in = nc.dram_tensor("w", [128, NT, NE], F32, kind="ExternalInput").ap()
    oc_in = nc.dram_tensor("ocat", [128, NE, DM], F32, kind="ExternalInput").ap()
    wqb_in = nc.dram_tensor("wqb", [128, 48, 16, 128], F32, kind="ExternalInput").ap()
    gam_in = nc.dram_tensor("gam", [1, DM], F32, kind="ExternalInput").ap()
    bet_in = nc.dram_tensor("bet", [1, DM], F32, kind="ExternalInput").ap()
    msk_in = nc.dram_tensor("msk", [128, 1024], F32, kind="ExternalInput").ap()
    ident_in = nc.dram_tensor("ident", [128, 128], F32, kind="ExternalInput").ap()

    r25_out = nc.dram_tensor("r25", [NT, 128, DM], F32, kind="ExternalOutput").ap()
    q2T_out = nc.dram_tensor("q2T", [16, 128, TPC], BF16, kind="ExternalOutput").ap()
    k2T_out = nc.dram_tensor("k2T", [16, 128, TPC], BF16, kind="ExternalOutput").ap()
    v2_out = nc.dram_tensor("v2", [128, 16, NT, 128], BF16, kind="ExternalOutput").ap()

    QS = 1.0 / float(np.sqrt(128.0))
    QS2 = 1.0 / float(np.sqrt(2048.0))

    s = Sched()
    with ExitStack() as stk:
        E = stk.enter_context
        rbuf = E(nc.sbuf_tensor("rbuf", [128, NT, DM], F32))
        qbf = E(nc.sbuf_tensor("qbf", [128, NT, 128], BF16))
        aos = E(nc.sbuf_tensor("aos", [128, NT, 128], BF16))
        wgs = E(nc.sbuf_tensor("wgs", [128, NT, NE], F32))
        ident = E(nc.sbuf_tensor("idents", [128, 128], BF16))
        mskA = E(nc.sbuf_tensor("mskA", [128, 1024], F32))
        q_fm = E(nc.sbuf_tensor("q_fm", [128, 128], BF16))
        PTs = [E(nc.sbuf_tensor(f"PTs{i}", [128, 128], BF16)) for i in range(2)]
        tmpb = E(nc.sbuf_tensor("tmpb", [128, 4, 128], BF16))
        orng4 = [E(nc.sbuf_tensor(f"orng4_{i}", [128, 4, DM], BF16)) for i in range(2)]
        bst = E(nc.sbuf_tensor("bst", [128, 24], F32))
        mv = E(nc.sbuf_tensor("mv", [128, 2], F32))
        sc = E(nc.sbuf_tensor("scs", [128, 8], F32))
        epsc = E(nc.sbuf_tensor("epscs", [128, 1], F32))
        PTb = [E(nc.sbuf_tensor(f"PTb{i}", [128, 8, 128], BF16)) for i in range(2)]
        st1 = ExitStack()
        S_sb = st1.enter_context(nc.sbuf_tensor("S_sb", [128, 8192], F32))
        P_sb = st1.enter_context(nc.sbuf_tensor("P_sb", [128, 8192], BF16))
        k_fm = st1.enter_context(nc.sbuf_tensor("k_fm", [128, 64, 128], BF16))
        v_sb = st1.enter_context(nc.sbuf_tensor("v_sb", [128, 64, 128], BF16))

        pS = [E(nc.psum_tensor(f"pS{i}", [128, 512], F32)) for i in range(2)]
        pU = E(nc.psum_tensor("pU", [128, DM], F32))
        pTb = [E(nc.psum_tensor(f"pTb{i}", [128, 8, 128], BF16)) for i in range(2)]

        sems = {l: E(nc.semaphore(f"SEM_{l}")) for l in LANES}
        block = E(nc.Block())

        for t in range(NT):
            s.step("sync", (lambda eng, t=t: eng.dma_start(out=rbuf[:, t, :], in_=r1_in[t])),
                   writes=[f"r{t}"])
        s.step("sync", (lambda eng: eng.dma_start(out=wgs[:], in_=w_in)),
               writes=["wgs"])
        s.step("sync", (lambda eng: eng.dma_start(out=mskA[:], in_=msk_in)), writes=["mskA"])
        s.step("gpsimd", (lambda eng: eng.dma_start(
            out=P_sb[:].bitcast(F32)[:, 0:NT * 128].rearrange("p (t f) -> p t f", t=NT),
            in_=qm_in)), writes=["PSBUF"])
        s.step("gpsimd", (lambda eng: eng.dma_start(
            out=v_sb[:], in_=vf_in)), writes=["v_sb"])
        s.step("gpsimd", (lambda eng: eng.dma_start(
            out=S_sb[:].bitcast(BF16)[:, 0:8192].rearrange("p (a b) -> p a b", a=64),
            in_=kf_in)), writes=["kbfS"])
        s.step("gpsimd", (lambda eng: eng.dma_start(out=ident[:], in_=ident_in)),
               writes=["id"])

        def f(eng):
            eng.memset(epsc[:], EPS)
            return eng.tensor_scalar(
                out=qbf[:].rearrange("p a b -> p (a b)"),
                in0=P_sb[:].bitcast(F32)[:, 0:NT * 128], scalar1=QS, scalar2=None,
                op0=ALU.mult)
        s.step("vector", f, reads=["PSBUF"], writes=["qbf", "eps"])

        for g in range(8):
            def f(eng, g=g):
                last = None
                for i in range(8):
                    b = g * 8 + i
                    last = eng.transpose(pTb[g % 2][:, i, :],
                                         S_sb[:].bitcast(BF16)[:, b * 128:(b + 1) * 128], ident[:])
                return last
            s.step("tensor", f, reads=["kbfS", "id"], writes=[f"pTb{g % 2}"])
            s.step("scalar", (lambda eng, g=g: eng.copy(
                k_fm[:, g * 8:(g + 1) * 8, :], pTb[g % 2][:])),
                reads=[f"pTb{g % 2}"], writes=["k_fm"])

        for j in range(NT):
            nch = 2 * j + 2
            L = nch * 512
            s.step("tensor", (lambda eng, j=j: eng.transpose(
                pTb[0][:, 0, :], qbf[:, j, :], ident[:])), reads=["qbf", "id"], writes=["pTb0"])
            s.step("scalar", (lambda eng: eng.copy(q_fm[:], pTb[0][:, 0, :])),
                   reads=["pTb0"], writes=["q_fm"])
            for ch in range(nch):
                s.step("tensor", (lambda eng, ch=ch: eng.matmul(
                    pS[ch % 2][:], q_fm[:], k_fm[:].rearrange("p a b -> p (a b)")[:, ch * 512:(ch + 1) * 512],
                    start=True, stop=True)),
                    reads=["q_fm", "k_fm"], writes=[f"pS{ch % 2}"])
                if ch >= nch - 2:
                    mo = (ch - (nch - 2)) * 512
                    def f(eng, ch=ch, mo=mo):
                        return eng.scalar_tensor_tensor(
                            out=S_sb[:, ch * 512:(ch + 1) * 512], in0=pS[ch % 2][:],
                            scalar=1.0, in1=mskA[:, mo:mo + 512],
                            op0=ALU.mult, op1=ALU.add)
                    hev = s.step("vector", f, reads=[f"pS{ch % 2}", "mskA"], writes=["Ssb"])
                else:
                    hev = s.step("vector", (lambda eng, ch=ch: eng.tensor_copy(
                        S_sb[:, ch * 512:(ch + 1) * 512], pS[ch % 2][:])),
                        reads=[f"pS{ch % 2}"], writes=["Ssb"])
            h1 = s.step("vector", (lambda eng, L=L: eng.reduce_max(
                sc[:, 0:1], S_sb[:, 0:L], axis=AX.X)), reads=["Ssb"], writes=["smx"],
                extra_deps=[hev])
            s.step("vector", (lambda eng: eng.tensor_scalar(
                out=sc[:, 1:2], in0=sc[:, 0:1], scalar1=-1.0, scalar2=None, op0=ALU.mult)),
                reads=["smx"], writes=["smx"], extra_deps=[h1])
            s.step("scalar", (lambda eng, L=L: eng.activation(
                out=P_sb[:, 0:L], in_=S_sb[:, 0:L], func=AF.Exp, bias=sc[:, 1:2])),
                reads=["Ssb", "smx"], writes=["PSBUF"])
            h2 = s.step("vector", (lambda eng, L=L: eng.reduce_sum(
                sc[:, 2:3], P_sb[:, 0:L], axis=AX.X)), reads=["PSBUF"], writes=["ssum"])
            h3 = s.step("vector", (lambda eng: eng.reciprocal(sc[:, 3:4], sc[:, 2:3])),
                        reads=["ssum"], writes=["srec"], extra_deps=[h2])
            nb = nch * 4
            ng = nb // 8
            for g in range(ng):
                def f(eng, g=g):
                    last = None
                    for i in range(8):
                        b2 = g * 8 + i
                        last = eng.transpose(pTb[g % 2][:, i, :],
                                             P_sb[:, b2 * 128:(b2 + 1) * 128], ident[:])
                    return last
                s.step("tensor", f, reads=["PSBUF", "id"], writes=[f"pTb{g % 2}"])
                s.step("scalar", (lambda eng, g=g: eng.copy(PTb[g % 2][:], pTb[g % 2][:])),
                       reads=[f"pTb{g % 2}"], writes=[f"PTk{g % 2}"])
                def f(eng, g=g, ng=ng):
                    last = None
                    for i in range(8):
                        b2 = g * 8 + i
                        last = eng.matmul(pU[:, 0:128], PTb[g % 2][:, i, :], v_sb[:, b2, :],
                                          start=(b2 == 0), stop=(g == ng - 1 and i == 7))
                    return last
                s.step("tensor", f, reads=[f"PTk{g % 2}", "v_sb"], writes=["pUmix"])
            s.step("scalar", (lambda eng, j=j: eng.activation(
                out=aos[:, j, :], in_=pU[:, 0:128], func=AF.Identity, scale=sc[:, 3:4])),
                reads=["pUmix", "srec"], writes=[f"ao{j}"], extra_deps=[h3])


        for e4 in range(4):
            s.step("gpsimd", (lambda eng, e4=e4: eng.dma_start(
                out=orng4[e4 % 2][:], in_=oc_in[:, e4 * 4:(e4 + 1) * 4, :])),
                writes=[f"oc{e4 % 2}"])
            for j in range(NT):
                p = (e4 * NT + j) % 2

                def f(eng, j=j, e4=e4):
                    last = None
                    for i in range(4):
                        last = eng.activation(
                            out=tmpb[:, i, :], in_=aos[:, j, :], func=AF.Identity,
                            scale=wgs[:, j, e4 * 4 + i:e4 * 4 + i + 1])
                    return last
                s.step("scalar", f, reads=[f"ao{j}", "wgs"], writes=["tmpb"])

                def f(eng, p=p):
                    last = None
                    for i in range(4):
                        last = eng.transpose(pTb[p][:, i, :], tmpb[:, i, :], ident[:])
                    return last
                s.step("tensor", f, reads=["tmpb", "id"], writes=[f"pTb{p}"])
                s.step("scalar", (lambda eng, p=p: eng.copy(
                    PTb[p][:, 0:4, :], pTb[p][:, 0:4, :])),
                    reads=[f"pTb{p}"], writes=[f"PTk{p}"])

                def f(eng, e4=e4, p=p):
                    last = None
                    for i in range(4):
                        for q4 in range(4):
                            last = eng.matmul(pU[:, q4 * 512:(q4 + 1) * 512],
                                              PTb[p][:, i, :],
                                              orng4[e4 % 2][:, i, q4 * 512:(q4 + 1) * 512],
                                              start=(i == 0), stop=(i == 3))
                    return last
                s.step("tensor", f, reads=[f"PTk{p}", f"oc{e4 % 2}"], writes=["pUmix"])
                s.step("vector", (lambda eng, j=j: eng.tensor_add(
                    rbuf[:, j, :], rbuf[:, j, :], pU[:])),
                    reads=[f"r{j}", "pUmix"], writes=[f"r{j}"])

        st1.close()
        xnT = stk.enter_context(nc.sbuf_tensor("xnT", [128, 16, TPC], BF16))
        h2fP = [stk.enter_context(nc.sbuf_tensor(f"h2f{i}", [128, DM], F32))
                for i in range(2)]
        xnbP = [stk.enter_context(nc.sbuf_tensor(f"xnb{i}", [128, DM], BF16))
                for i in range(2)]
        bstP = [stk.enter_context(nc.sbuf_tensor(f"bst2_{i}", [128, 24], F32))
                for i in range(2)]
        mvP = [stk.enter_context(nc.sbuf_tensor(f"mv2_{i}", [128, 2], F32))
               for i in range(2)]
        scP = [stk.enter_context(nc.sbuf_tensor(f"sc2_{i}", [128, 8], F32))
               for i in range(2)]
        gamr = stk.enter_context(nc.sbuf_tensor("gamr", [128, DM], BF16))
        betr = stk.enter_context(nc.sbuf_tensor("betr", [128, DM], BF16))
        wrng = [stk.enter_context(nc.sbuf_tensor(f"wrng{i}", [128, 16, 128], BF16)) for i in range(2)]
        stg = stk.enter_context(nc.sbuf_tensor("stg", [128, 1024], BF16))
        vst = stk.enter_context(nc.sbuf_tensor("vst", [128, 8, 128], BF16))

        barrier = [("vector", s.count["vector"]), ("scalar", s.count["scalar"]),
                   ("tensor", s.count["tensor"])]
        s.step("gpsimd", (lambda eng: eng.dma_start(
            out=gamr[:], in_=gam_in.broadcast_to([128, DM]))), writes=["gamr"],
            extra_deps=barrier)
        s.step("gpsimd", (lambda eng: eng.dma_start(
            out=betr[:], in_=bet_in.broadcast_to([128, DM]))), writes=["betr"])

        for t in range(NT):
            rk = f"r{t}"
            pb = t % 2
            bst_, mv_, sc_ = bstP[pb], mvP[pb], scP[pb]
            h2f_, xnb_ = h2fP[pb], xnbP[pb]
            def f(eng, t=t, bst_=bst_, mv_=mv_, sc_=sc_, h2f_=h2f_, xnb_=xnb_):
                for i in range(3):
                    eng.bn_stats(bst_[:, i * 6:(i + 1) * 6], rbuf[:, t, ts(i, 512)])
                return eng.bn_stats(bst_[:, 18:24], rbuf[:, t, ts(3, 512)])
            h = s.step("vector", f, reads=[rk], writes=[f"bst{pb}"])
            s.step("vector", (lambda eng, bst_=bst_, mv_=mv_, sc_=sc_, h2f_=h2f_, xnb_=xnb_: eng.bn_aggr(mv_[:], bst_[:])),
                   reads=[f"bst{pb}"], writes=[f"mv{pb}"], extra_deps=[h])

            def f(eng, bst_=bst_, mv_=mv_, sc_=sc_, h2f_=h2f_, xnb_=xnb_):
                eng.activation(out=sc_[:, 0:1], in_=mv_[:, 1:2], func=AF.Ln, bias=epsc[:])
                return eng.activation(out=sc_[:, 2:3], in_=sc_[:, 0:1], func=AF.Exp, scale=-0.5)
            s.step("scalar", f, reads=[f"mv{pb}", "eps"], writes=[f"sc02{pb}"])

            s.step("vector", (lambda eng, bst_=bst_, mv_=mv_, sc_=sc_, h2f_=h2f_, xnb_=xnb_: eng.scalar_tensor_tensor(
                out=sc_[:, 4:5], in0=mv_[:, 0:1], scalar=-1.0, in1=sc_[:, 2:3],
                op0=ALU.mult, op1=ALU.mult)), reads=[f"mv{pb}", f"sc02{pb}"], writes=[f"sc4{pb}"])

            s.step("scalar", (lambda eng, t=t, bst_=bst_, mv_=mv_, sc_=sc_, h2f_=h2f_, xnb_=xnb_: eng.activation(
                out=h2f_[:], in_=rbuf[:, t, :], func=AF.Identity,
                scale=sc_[:, 2:3], bias=sc_[:, 4:5])),
                reads=[rk, f"sc02{pb}", f"sc4{pb}"], writes=[f"h2f{pb}"])

            s.step("vector", (lambda eng, t=t, bst_=bst_, mv_=mv_, sc_=sc_, h2f_=h2f_, xnb_=xnb_: eng.tensor_add(
                rbuf[:, t, :], rbuf[:, t, :], h2f_[:])),
                reads=[rk, f"h2f{pb}"], writes=[rk])
            s.step("sync", (lambda eng, t=t, bst_=bst_, mv_=mv_, sc_=sc_, h2f_=h2f_, xnb_=xnb_: eng.dma_start(
                out=r25_out[t], in_=rbuf[:, t, :])), reads=[rk], writes=[f"st_r{t}"])

            def f(eng, bst_=bst_, mv_=mv_, sc_=sc_, h2f_=h2f_, xnb_=xnb_):
                for i in range(3):
                    eng.bn_stats(bst_[:, i * 6:(i + 1) * 6], h2f_[:, ts(i, 512)])
                return eng.bn_stats(bst_[:, 18:24], h2f_[:, ts(3, 512)])
            h = s.step("vector", f, reads=[f"h2f{pb}"], writes=[f"bst{pb}"])
            s.step("vector", (lambda eng, bst_=bst_, mv_=mv_, sc_=sc_, h2f_=h2f_, xnb_=xnb_: eng.bn_aggr(mv_[:], bst_[:])),
                   reads=[f"bst{pb}"], writes=[f"mv{pb}"], extra_deps=[h])

            def f(eng, bst_=bst_, mv_=mv_, sc_=sc_, h2f_=h2f_, xnb_=xnb_):
                eng.activation(out=sc_[:, 0:1], in_=mv_[:, 1:2], func=AF.Ln, bias=epsc[:])
                return eng.activation(out=sc_[:, 2:3], in_=sc_[:, 0:1], func=AF.Exp, scale=-0.5)
            s.step("scalar", f, reads=[f"mv{pb}", "eps"], writes=[f"sc02{pb}"])

            s.step("vector", (lambda eng, bst_=bst_, mv_=mv_, sc_=sc_, h2f_=h2f_, xnb_=xnb_: eng.scalar_tensor_tensor(
                out=sc_[:, 4:5], in0=mv_[:, 0:1], scalar=-1.0, in1=sc_[:, 2:3],
                op0=ALU.mult, op1=ALU.mult)), reads=[f"mv{pb}", f"sc02{pb}"], writes=[f"sc4{pb}"])

            s.step("scalar", (lambda eng, bst_=bst_, mv_=mv_, sc_=sc_, h2f_=h2f_, xnb_=xnb_: eng.activation(
                out=xnb_[:], in_=h2f_[:], func=AF.Identity,
                scale=sc_[:, 2:3], bias=sc_[:, 4:5])),
                reads=[f"h2f{pb}", f"sc02{pb}", f"sc4{pb}"], writes=[f"xnb{pb}"])

            def f(eng, bst_=bst_, mv_=mv_, sc_=sc_, h2f_=h2f_, xnb_=xnb_):
                eng.tensor_mul(xnb_[:], xnb_[:], gamr[:])
                return eng.tensor_add(xnb_[:], xnb_[:], betr[:])
            s.step("vector", f, reads=[f"xnb{pb}", "gamr", "betr"], writes=[f"xnb{pb}"])

            for g in range(2):
                def f(eng, g=g, bst_=bst_, mv_=mv_, sc_=sc_, h2f_=h2f_, xnb_=xnb_):
                    last = None
                    for i in range(8):
                        last = eng.transpose(pTb[g % 2][:, i, :],
                                             xnb_[:, ts(g * 8 + i)], ident[:])
                    return last
                s.step("tensor", f, reads=[f"xnb{pb}", "id"], writes=[f"pTb{g % 2}"])
                s.step("scalar", (lambda eng, t=t, g=g, bst_=bst_, mv_=mv_, sc_=sc_, h2f_=h2f_, xnb_=xnb_: eng.copy(
                    xnT[:, g * 8:(g + 1) * 8, t * 128:(t + 1) * 128], pTb[g % 2][:])),
                    reads=[f"pTb{g % 2}"], writes=[f"xnT{k}" for k in range(g * 8, g * 8 + 8)])

        for oc in range(48):
            s.step("gpsimd", (lambda eng, oc=oc: eng.dma_start(
                out=wrng[oc % 2][:], in_=wqb_in[:, oc, :, :])),
                writes=[f"wr{oc % 2}"])
            def f(eng, oc=oc):
                last = None
                for kc in range(16):
                    for hh in range(2):
                        last = eng.matmul(
                            pU[:, hh * 512:hh * 512 + 512], wrng[oc % 2][:, kc, :],
                            xnT[:, kc, hh * 512:hh * 512 + 512],
                            start=(kc == 0), stop=(kc == 15))
                return last
            s.step("tensor", f, reads=[f"wr{oc % 2}"] + [f"xnT{k}" for k in range(16)],
                   writes=["pOC"])
            if oc < 32:
                sc_ = QS2 if oc < 16 else 1.0
                s.step("scalar", (lambda eng, sc_=sc_: eng.activation(
                    out=stg[:], in_=pU[:, 0:1024], func=AF.Identity, scale=sc_)),
                    reads=["pOC"], writes=["stg"])
                dst = q2T_out if oc < 16 else k2T_out
                s.step("sync", (lambda eng, dst=dst, oc=oc: eng.dma_start(
                    out=dst[oc % 16], in_=stg[:])), reads=["stg"], writes=["st_qk"])
            else:
                s.step("scalar", (lambda eng: eng.copy(stg[:], pU[:, 0:1024])),
                       reads=["pOC"], writes=["stg"])
                def f(eng, oc=oc):
                    last = None
                    for b2 in range(8):
                        last = eng.transpose(pTb[oc % 2][:, b2, :],
                                             stg[:, b2 * 128:(b2 + 1) * 128], ident[:])
                    return last
                s.step("tensor", f, reads=["stg", "id"], writes=[f"pTb{oc % 2}"])
                s.step("scalar", (lambda eng, oc=oc: eng.copy(
                    vst[:], pTb[oc % 2][:])),
                    reads=[f"pTb{oc % 2}"], writes=["vst"])
                s.step("sync", (lambda eng, oc=oc: eng.dma_start(
                    out=v2_out[:, oc - 32, :, :],
                    in_=vst[:])), reads=["vst"], writes=["st_v"])

        s.wait_all("sync", [("sync0", s.count["sync0"]), ("sync1", s.count["sync1"])])
        s.build(block, sems)
    return nc


def _host_prep_l2(inputs, r1, qkvm, w):
    op = np.asarray(inputs["L0_ffn_o_proj"])
    wqb = np.asarray(inputs["L0_attn_b_qkv"])
    gam = np.asarray(inputs["L0_attn_b_gamma"]).reshape(1, DM)
    bet = np.asarray(inputs["L0_attn_b_beta"]).reshape(1, DM)
    ident = np.eye(128, dtype=np.float32)
    wqb_r = np.ascontiguousarray(
        wqb.reshape(16, 128, 48, 128).transpose(1, 2, 0, 3))

    kf = np.empty((128, 64, 128), np.float32)
    vf = np.empty((128, 64, 128), np.float32)
    for g in range(64):
        c, t2 = g % 8, g // 8
        kf[:, g, :] = qkvm[c][:, t2, 128:256]
        vf[:, g, :] = qkvm[c][:, t2, 256:384]

    shared = dict(ocat=np.ascontiguousarray(op.transpose(1, 0, 2)), wqb=wqb_r,
                  gam=gam, bet=bet, ident=ident, kf=kf, vf=vf)
    maps = []
    for c in range(NCORE):
        msk = np.zeros((128, 1024), np.float32)
        for i in range(8):
            for r in range(128):
                lim = i * 128 + 128 if i < c else (i * 128 + r + 1 if i == c else 0)
                if lim < (i + 1) * 128:
                    msk[r, i * 128 + max(lim, i * 128) - i * 128 + i * 128:(i + 1) * 128] = -1e30
        msk = np.zeros((128, 1024), np.float32)
        for r in range(128):
            allowed = np.zeros(1024, bool)
            for i in range(8):
                if i < c:
                    allowed[i * 128:(i + 1) * 128] = True
                elif i == c:
                    allowed[i * 128:i * 128 + r + 1] = True
            msk[r, ~allowed] = -1e30
        maps.append(dict(r1=r1[c],
                         qm=np.ascontiguousarray(np.asarray(qkvm[c])[:, :, 0:128]),
                         w=np.asarray(w[c]), msk=msk, **shared))
    return maps


def run_l2(inputs, r1, qkvm, w):
    if "l2" not in _cache:
        _cache["l2"] = build_l2()
    maps = _host_prep_l2(inputs, r1, qkvm, w)
    res = run_bass_kernel_spmd(_cache["l2"], maps, core_ids=list(range(NCORE)))
    return res.results


def build_l3():
    nc = bass.Bass("TRN2", target_bir_lowering=False, detect_race_conditions=False)
    r25_in = nc.dram_tensor("r25", [NT, 128, DM], F32, kind="ExternalInput").ap()
    q2T_in = nc.dram_tensor("q2T", [128, 16, TPC], BF16, kind="ExternalInput").ap()
    k2T_in = nc.dram_tensor("k2Tf", [128, 16, 16, 512], BF16, kind="ExternalInput").ap()
    v2_in = nc.dram_tensor("v2f", [128, 64, DM], BF16, kind="ExternalInput").ap()
    wob_in = nc.dram_tensor("wob", [128, 16, 16, 128], F32, kind="ExternalInput").ap()
    sim_in = nc.dram_tensor("sim", [128, 16, NE], F32, kind="ExternalInput").ap()
    gates_in = nc.dram_tensor("gates", [1, NE], F32, kind="ExternalInput").ap()
    prio_in = nc.dram_tensor("prio", [1, NE], F32, kind="ExternalInput").ap()
    wcat_in = nc.dram_tensor("wcat", [16, 128, NE * 384], F32, kind="ExternalInput").ap()
    msk_in = nc.dram_tensor("msk", [128, 1024], F32, kind="ExternalInput").ap()
    ident_in = nc.dram_tensor("ident", [128, 128], F32, kind="ExternalInput").ap()

    r3_out = nc.dram_tensor("r3", [NT, 128, DM], F32, kind="ExternalOutput").ap()
    qkvm_out = nc.dram_tensor("qkvm", [128, NT, 384], F32, kind="ExternalOutput").ap()
    w_out = nc.dram_tensor("w", [128, NT, NE], F32, kind="ExternalOutput").ap()
    scr_d = nc.dram_tensor("scrd", [4, NE], F32)

    SOFF = [(j * j + j) * 512 for j in range(NT)]
    SLEN = [(2 * j + 2) * 512 for j in range(NT)]

    s = Sched()
    with ExitStack() as stk:
        E = stk.enter_context
        qkvm = E(nc.sbuf_tensor("qkvm_s", [128, NT, 384], F32))
        wg = E(nc.sbuf_tensor("wg", [128, NT, NE], F32))
        ident = E(nc.sbuf_tensor("idents", [128, 128], BF16))
        ident32 = E(nc.sbuf_tensor("id32s", [128, 128], F32))
        mskA = E(nc.sbuf_tensor("mskA", [128, 1024], F32))
        sim32 = E(nc.sbuf_tensor("sim32", [128, 16, NE], F32))
        ssq = E(nc.sbuf_tensor("ssq", [128, 256], F32))
        nrow = E(nc.sbuf_tensor("nrow", [1, 64], F32))
        ones32 = E(nc.sbuf_tensor("ones32", [128, 1], F32))
        rT32b = E(nc.sbuf_tensor("rT32b", [128, 4, 128], F32))
        bst = E(nc.sbuf_tensor("bst", [128, 24], F32))
        mv = E(nc.sbuf_tensor("mv", [128, 2], F32))
        sc = E(nc.sbuf_tensor("scs", [128, 8], F32))
        scj = E(nc.sbuf_tensor("scjs", [128, NT], F32))
        Gall = E(nc.sbuf_tensor("Gall", [128, NE, NT], F32))
        Gv = [Gall[:]] + [E(nc.sbuf_tensor(f"gv{i}", [128, NE, NT], F32))[:]
                          for i in range(5)]
        G4flat = Gv[4].rearrange("p a b -> p (a b)")
        T1g = E(nc.sbuf_tensor("T1g", [128, 8, NT], F32))
        scB = E(nc.sbuf_tensor("scB", [128, 12, NT], F32))
        mk8 = E(nc.sbuf_tensor("mk8s", [128, NE, NT], mybir.dt.uint8))
        mkT = E(nc.sbuf_tensor("mkTs", [128, NT], mybir.dt.uint8))
        s6b = E(nc.sbuf_tensor("s6bs", [128, NT], F32))
        s7b = E(nc.sbuf_tensor("s7bs", [128, NT], F32))
        prioT = E(nc.sbuf_tensor("prioT", [128, NE, NT], F32))
        csrT = E(nc.sbuf_tensor("csrT", [128, NE, NT], F32))
        snrT = E(nc.sbuf_tensor("snrT", [128, NE, NT], F32))
        sgrT = E(nc.sbuf_tensor("sgrT", [128, NE, NT], F32))
        prio_s = E(nc.sbuf_tensor("prio_s", [128, NE], F32))
        csr_s = E(nc.sbuf_tensor("csr_s", [128, NE], F32))
        snr_s = E(nc.sbuf_tensor("snr_s", [128, NE], F32))
        sgr_s = E(nc.sbuf_tensor("sgr_s", [128, NE], F32))
        grow = E(nc.sbuf_tensor("grows", [1, NE], F32))
        epsc = E(nc.sbuf_tensor("epscs", [128, 1], F32))
        BIGS = E(nc.sbuf_tensor("BIGS", [128, 36864], BF16))
        hbf = BIGS[:, 22528:24576]
        aoTb = E(nc.sbuf_tensor("aoTb", [128, 16, TPC], BF16))
        PTb = [E(nc.sbuf_tensor(f"PTb{i}", [128, 8, 128], BF16)) for i in range(2)]

        pS = [E(nc.psum_tensor(f"pS{i}", [128, 512], F32)) for i in range(2)]
        pU = E(nc.psum_tensor("pU", [128, DM], F32))
        pTb = [E(nc.psum_tensor(f"pTb{i}", [128, 8, 128], BF16)) for i in range(2)]

        sems = {l: E(nc.semaphore(f"SEM_{l}")) for l in LANES}
        block = E(nc.Block())

        stA = ExitStack()
        q2sb = stA.enter_context(nc.sbuf_tensor("q2sb", [128, 16, TPC], BF16))
        krng = [stA.enter_context(nc.sbuf_tensor(f"krng{i}", [128, 16, 512], BF16))
                for i in range(2)]

        s.step("sync", (lambda eng: eng.dma_start(out=mskA[:], in_=msk_in)), writes=["mskA"])
        s.step("sync", (lambda eng: eng.dma_start(
            out=q2sb[:], in_=q2T_in)), writes=["q2sb"])
        for dst, srcap, key in ((ident32, ident_in, "id32"),
                                (prio_s, prio_in.broadcast_to([128, NE]), "prio_s"),
                                (grow, gates_in, "grow")):
            s.step("gpsimd", (lambda eng, d=dst, sr=srcap: eng.dma_start(out=d[:], in_=sr)),
                   writes=[key])
        s.step("gpsimd", (lambda eng: eng.dma_start(out=ident[:], in_=ident_in)),
               writes=["id"])
        s.step("gpsimd", (lambda eng: eng.dma_start(
            out=sim32[:], in_=sim_in)), writes=["sim32"])

        def f(eng):
            eng.memset(ones32[:], 1.0)
            eng.memset(epsc[:], EPS)
            eng.memset(qkvm[:], 0.0)
            s32 = sim32[:].rearrange("p a e -> p (a e)")
            return eng.tensor_mul(ssq[:], s32, s32)
        s.step("vector", f, reads=["sim32"], writes=["ssq", "qkvm_s", "ones32", "eps"])

        def f(eng):
            for k in range(16):
                eng.matmul(pS[0][0:1, 0:NE], ones32[:], sim32[:, k, :],
                           start=(k == 0), stop=(k == 15), skip_group_check=True)
            last = None
            for k in range(16):
                last = eng.matmul(pS[0][0:1, 32:32 + NE], ones32[:],
                                  ssq[:, k * NE:(k + 1) * NE],
                                  start=(k == 0), stop=(k == 15), skip_group_check=True)
            return last
        s.step("tensor", f, reads=["ssq", "sim32", "ones32"], writes=["pS0"])

        def f(eng):
            eng.tensor_copy(nrow[:, 0:NE], pS[0][0:1, 0:NE])
            return eng.tensor_copy(nrow[:, 32:32 + NE], pS[0][0:1, 32:32 + NE])
        s.step("vector", f, reads=["pS0"], writes=["nrow"])

        def f(eng):
            eng.activation(out=grow[:], in_=grow[:], func=AF.Exp, scale=-1.0)
            return eng.activation(out=nrow[:, 32:32 + NE], in_=nrow[:, 32:32 + NE], func=AF.Ln)
        s.step("scalar", f, reads=["grow", "nrow"], writes=["grow", "nrow"])
        s.step("scalar", (lambda eng: eng.activation(
            out=nrow[:, 32:32 + NE], in_=nrow[:, 32:32 + NE], func=AF.Exp, scale=-0.5)),
            reads=["nrow"], writes=["nrow"])

        def f(eng):
            eng.tensor_scalar_add(grow[:], grow[:], 1.0)
            return eng.reciprocal(grow[:], grow[:])
        s.step("vector", f, reads=["grow"], writes=["grow"])

        s.step("gpsimd", (lambda eng: eng.dma_start(out=scr_d[0:1, :], in_=grow[:])),
               reads=["grow"], writes=["scr0"])
        s.step("gpsimd", (lambda eng: eng.dma_start(out=scr_d[1:2, :], in_=nrow[:, 0:NE])),
               reads=["nrow"], writes=["scr1"])
        s.step("gpsimd", (lambda eng: eng.dma_start(
            out=scr_d[2:3, :], in_=nrow[:, 32:32 + NE])), reads=["nrow"], writes=["scr2"])
        s.step("gpsimd", (lambda eng: eng.dma_start(
            out=sgr_s[:], in_=scr_d[0:1, :].broadcast_to([128, NE]))),
            reads=["scr0"], writes=["sgr_s"])
        s.step("gpsimd", (lambda eng: eng.dma_start(
            out=csr_s[:], in_=scr_d[1:2, :].broadcast_to([128, NE]))),
            reads=["scr1"], writes=["csr_s"])
        s.step("gpsimd", (lambda eng: eng.dma_start(
            out=snr_s[:], in_=scr_d[2:3, :].broadcast_to([128, NE]))),
            reads=["scr2"], writes=["snr_s"])
        for big, small, bk, sk in ((sgrT, sgr_s, "sgr", "sgr_s"),
                                   (csrT, csr_s, "csr", "csr_s"),
                                   (snrT, snr_s, "snr", "snr_s"),
                                   (prioT, prio_s, "prio", "prio_s")):
            s.step("vector", (lambda eng, big=big, small=small: eng.tensor_copy(
                big[:].rearrange("p a b -> p b a"),
                small[:].unsqueeze(1).broadcast_to([128, NT, NE]))),
                reads=[sk], writes=[bk])

        idx = [0]

        def emit_softmax(j):
            sk = f"Sall{j}"
            h1 = s.step("vector", (lambda eng, j=j: eng.reduce_max(
                sc[:, 0:1], BIGS[:, SOFF[j]:SOFF[j] + SLEN[j]], axis=AX.X)),
                reads=[sk], writes=["smx"])
            s.step("vector", (lambda eng: eng.tensor_scalar(
                out=sc[:, 1:2], in0=sc[:, 0:1], scalar1=-1.0, scalar2=None, op0=ALU.mult)),
                reads=["smx"], writes=["smx"], extra_deps=[h1])
            s.step("scalar", (lambda eng, j=j: eng.activation(
                out=BIGS[:, SOFF[j]:SOFF[j] + SLEN[j]],
                in_=BIGS[:, SOFF[j]:SOFF[j] + SLEN[j]], func=AF.Exp, bias=sc[:, 1:2])),
                reads=[sk, "smx"], writes=[f"Pall{j}"])
            h2 = s.step("vector", (lambda eng, j=j: eng.reduce_sum(
                sc[:, 2:3], BIGS[:, SOFF[j]:SOFF[j] + SLEN[j]], axis=AX.X)),
                reads=[f"Pall{j}"], writes=["ssum"])
            h3 = s.step("vector", (lambda eng, j=j: eng.reciprocal(
                scj[:, j:j + 1], sc[:, 2:3])),
                reads=["ssum"], writes=[f"scj{j}"], extra_deps=[h2])
            s.step("vector", (lambda eng, j=j: eng.tensor_copy(sc[:, 3:4], scj[:, j:j + 1])),
                   reads=[f"scj{j}"], writes=["sc34"], extra_deps=[h3])

        for kc in range(16):
            s.step("gpsimd", (lambda eng, kc=kc: eng.dma_start(
                out=krng[kc % 2][:], in_=k2T_in[:, kc, :, :])),
                writes=[f"kr{kc % 2}"])
            for j in range(NT):
                if 2 * j + 2 <= kc:
                    continue
                p = idx[0] % 2
                idx[0] += 1

                def f(eng, j=j, kc=kc, p=p):
                    last = None
                    for ff in range(16):
                        last = eng.matmul(pS[p][:, 0:512],
                                          q2sb[:, ff, j * 128:(j + 1) * 128],
                                          krng[kc % 2][:, ff, :],
                                          start=(ff == 0), stop=(ff == 15))
                    return last
                s.step("tensor", f, reads=["q2sb", f"kr{kc % 2}"], writes=[f"pS{p}"])
                if kc >= 2 * j:
                    mo = (kc - 2 * j) * 512

                    def f(eng, j=j, kc=kc, p=p, mo=mo):
                        return eng.scalar_tensor_tensor(
                            out=BIGS[:, SOFF[j] + kc * 512:SOFF[j] + (kc + 1) * 512],
                            in0=pS[p][:, 0:512], scalar=1.0, in1=mskA[:, mo:mo + 512],
                            op0=ALU.mult, op1=ALU.add)
                    s.step("vector", f, reads=[f"pS{p}", "mskA"], writes=[f"Sall{j}"])
                else:
                    def f(eng, j=j, kc=kc, p=p):
                        return eng.tensor_copy(
                            BIGS[:, SOFF[j] + kc * 512:SOFF[j] + (kc + 1) * 512],
                            pS[p][:, 0:512])
                    s.step("vector", f, reads=[f"pS{p}"], writes=[f"Sall{j}"])
            if kc % 2 == 1:
                emit_softmax(kc // 2)

        stA.close()

        stC = ExitStack()
        vrng4 = [aoTb[:, i * 8:(i + 1) * 8, :]
                 .rearrange("p a b -> p (a b)")[:, 0:4 * DM]
                 .rearrange("p (a b) -> p a b", a=4) for i in range(2)]
        O4 = stC.enter_context(nc.sbuf_tensor("O4", [128, NT, DM], F32))
        aob = stC.enter_context(nc.sbuf_tensor("aobs", [128, DM], BF16))

        tidx = [0]
        vidx = [0]

        def emit_av_half(tlo, thi):
            for vg8 in range(thi + 1):
                for q in range(2):
                    hg = vg8 * 2 + q
                    vi = vidx[0] % 2
                    vidx[0] += 1
                    s.step("gpsimd", (lambda eng, hg=hg, vi=vi: eng.dma_start(
                        out=vrng4[vi][:], in_=v2_in[:, hg * 4:(hg + 1) * 4, :])),
                        writes=[f"vr{vi}"])
                for j in range(max(tlo, vg8), thi + 1):
                    ti = tidx[0] % 2
                    tidx[0] += 1

                    def f(eng, j=j, vg8=vg8, ti=ti):
                        last = None
                        for i in range(8):
                            cc = SOFF[j] + vg8 * 1024 + i * 128
                            last = eng.transpose(pTb[ti][:, i, :],
                                                 BIGS[:, cc:cc + 128], ident[:])
                        return last
                    s.step("tensor", f, reads=[f"Pall{j}", "id"], writes=[f"pTb{ti}"])
                    s.step("scalar", (lambda eng, ti=ti: eng.copy(PTb[ti][:], pTb[ti][:])),
                           reads=[f"pTb{ti}"], writes=[f"PTk{ti}"])
                    v0, v1 = (vidx[0] - 2) % 2, (vidx[0] - 1) % 2

                    def f(eng, ti=ti, v0=v0, v1=v1):
                        last = None
                        for i in range(8):
                            vs = vrng4[v0 if i < 4 else v1]
                            for q4 in range(4):
                                last = eng.matmul(pU[:, q4 * 512:(q4 + 1) * 512],
                                                  PTb[ti][:, i, :],
                                                  vs[:, i % 4, q4 * 512:(q4 + 1) * 512],
                                                  start=(i == 0), stop=(i == 7))
                        return last
                    s.step("tensor", f, reads=[f"PTk{ti}", f"vr{v0}", f"vr{v1}"],
                           writes=["pUav"])
                    if vg8 == 0:
                        s.step("vector", (lambda eng, j=j: eng.tensor_copy(
                            O4[:, j, :], pU[:])), reads=["pUav"], writes=[f"O{j}"])
                    else:
                        def f(eng, j=j):
                            return eng.scalar_tensor_tensor(
                                out=O4[:, j, :], in0=pU[:], scalar=1.0,
                                in1=O4[:, j, :], op0=ALU.mult, op1=ALU.add)
                        s.step("vector", f, reads=["pUav", f"O{j}"], writes=[f"O{j}"])
            for j in range(tlo, thi + 1):
                s.step("scalar", (lambda eng, j=j: eng.activation(
                    out=aob[:], in_=O4[:, j, :], func=AF.Identity,
                    scale=scj[:, j:j + 1])),
                    reads=[f"O{j}", f"scj{j}"], writes=["aob"])
                for g in range(2):
                    ti = tidx[0] % 2
                    tidx[0] += 1

                    def f(eng, g=g, ti=ti):
                        last = None
                        for i in range(8):
                            last = eng.transpose(pTb[ti][:, i, :],
                                                 aob[:, ts(g * 8 + i)], ident[:])
                        return last
                    s.step("tensor", f, reads=["aob", "id"], writes=[f"pTb{ti}"])
                    s.step("scalar", (lambda eng, j=j, g=g, ti=ti: eng.copy(
                        aoTb[:, g * 8:(g + 1) * 8, j * 128:(j + 1) * 128], pTb[ti][:])),
                        reads=[f"pTb{ti}"], writes=[f"aoT{j}"])

        emit_av_half(0, 7)
        stC.close()

        stR = ExitStack()
        rbuf = stR.enter_context(nc.sbuf_tensor("rbuf", [128, NT, DM], F32))
        stg32 = BIGS[:, 16384:18432].bitcast(F32)
        wrng = [BIGS[:, 18432 + i * 2048:18432 + (i + 1) * 2048]
                .rearrange("p (a b) -> p a b", a=16) for i in range(2)]
        barrierP2 = [("vector", s.count["vector"]), ("scalar", s.count["scalar"]),
                     ("tensor", s.count["tensor"])]

        for t in range(NT):
            s.step("sync", (lambda eng, t=t: eng.dma_start(out=rbuf[:, t, :], in_=r25_in[t])),
                   writes=[f"r{t}"], extra_deps=(barrierP2 if t < 2 else ()))

        AOT_KEYS = [f"aoT{j}" for j in range(NT)]
        for oc in range(16):
            s.step("gpsimd", (lambda eng, oc=oc: eng.dma_start(
                out=wrng[oc % 2][:], in_=wob_in[:, oc, :, :])),
                writes=[f"wo{oc % 2}"],
                extra_deps=(barrierP2 if oc < 2 else ()))
            def f(eng, oc=oc):
                last = None
                for kc in range(16):
                    for hh in range(2):
                        last = eng.matmul(
                            pU[:, hh * 512:hh * 512 + 512], wrng[oc % 2][:, kc, :],
                            aoTb[:, kc, hh * 512:hh * 512 + 512],
                            start=(kc == 0), stop=(kc == 15))
                return last
            s.step("tensor", f, reads=[f"wo{oc % 2}"] + AOT_KEYS, writes=["pOC"])
            s.step("scalar", (lambda eng: eng.copy(stg32[:], pU[:, 0:1024])),
                   reads=["pOC"], writes=["stg32"])
            for t in range(NT):
                s.step("tensor", (lambda eng, t=t: eng.transpose(
                    pS[t % 2][:, 0:128],
                    stg32[:, t * 128:(t + 1) * 128], ident32[:])),
                    reads=["stg32", "id32"], writes=[f"pR{t % 2}"])
                s.step("vector", (lambda eng, t=t, oc=oc: eng.tensor_add(
                    rbuf[:, t, oc * 128:(oc + 1) * 128],
                    rbuf[:, t, oc * 128:(oc + 1) * 128],
                    pS[t % 2][:, 0:128])),
                    reads=[f"pR{t % 2}", f"r{t}"], writes=[f"r{t}"])

        for t in range(NT):
            s.step("sync", (lambda eng, t=t: eng.dma_start(
                out=r3_out[t], in_=rbuf[:, t, :])), reads=[f"r{t}"], writes=[f"st_r{t}"])

        hT2 = BIGS[:, 0:NT * 16 * 128].rearrange("p (a b) -> p a b", a=NT * 16)

        for t in range(NT):
            rk = f"r{t}"

            def f(eng, t=t):
                for i in range(3):
                    eng.bn_stats(bst[:, i * 6:(i + 1) * 6], rbuf[:, t, ts(i, 512)])
                return eng.bn_stats(bst[:, 18:24], rbuf[:, t, ts(3, 512)])
            h = s.step("vector", f, reads=[rk], writes=["bst"])
            s.step("vector", (lambda eng: eng.bn_aggr(mv[:], bst[:])),
                   reads=["bst"], writes=["mv"], extra_deps=[h])

            def f(eng):
                eng.activation(out=sc[:, 0:1], in_=mv[:, 1:2], func=AF.Ln, bias=epsc[:])
                eng.activation(out=sc[:, 1:2], in_=mv[:, 1:2], func=AF.Ln, scale=float(DM))
                return eng.activation(out=sc[:, 2:3], in_=sc[:, 0:1], func=AF.Exp, scale=-0.5)
            s.step("scalar", f, reads=["mv", "eps"], writes=["sc02"])

            def f(eng):
                eng.tensor_sub(sc[:, 3:4], sc[:, 1:2], sc[:, 0:1])
                return eng.scalar_tensor_tensor(out=sc[:, 4:5], in0=mv[:, 0:1], scalar=-1.0,
                                                in1=sc[:, 2:3], op0=ALU.mult, op1=ALU.mult)
            s.step("vector", f, reads=["mv", "sc02"], writes=["sc4", "sc3"])

            def f(eng, t=t):
                eng.activation(out=sc[:, 5:6], in_=sc[:, 3:4], func=AF.Exp, scale=-0.5)
                return eng.activation(out=hbf[:], in_=rbuf[:, t, :], func=AF.Identity,
                                      scale=sc[:, 2:3], bias=sc[:, 4:5])
            s.step("scalar", f, reads=[rk, "sc02", "sc4", "sc3"], writes=["hbf", "sc5"])

            h6 = s.step("vector", (lambda eng, t=t: eng.tensor_mul(
                s6b[:, t:t + 1], sc[:, 2:3], sc[:, 5:6])),
                reads=["sc02", "sc5"], writes=["s6b"])
            s.step("vector", (lambda eng, t=t: eng.tensor_mul(
                s7b[:, t:t + 1], mv[:, 0:1], s6b[:, t:t + 1])),
                reads=["mv", "s6b"], writes=["s7b"], extra_deps=[h6])

            for g in range(2):
                def f(eng, g=g):
                    last = None
                    for i in range(8):
                        last = eng.transpose(pTb[g % 2][:, i, :], hbf[:, ts(g * 8 + i)], ident[:])
                    return last
                s.step("tensor", f, reads=["hbf", "id"], writes=[f"pTb{g % 2}"])
                s.step("scalar", (lambda eng, t=t, g=g: eng.copy(
                    hT2[:, t * 16 + g * 8:t * 16 + (g + 1) * 8, :], pTb[g % 2][:])),
                    reads=[f"pTb{g % 2}"], writes=[f"hT2_{t}_{g}"])

            for kg in range(4):
                def f(eng, t=t, kg=kg):
                    last = None
                    for i in range(4):
                        last = eng.transpose(pU[:, 1536 + i * 128:1664 + i * 128],
                                             rbuf[:, t, ts(kg * 4 + i)], ident32[:])
                    return last
                s.step("tensor", f, reads=[rk, "id32"], writes=["pU3"])
                s.step("vector", (lambda eng: eng.tensor_copy(
                    rT32b[:].rearrange("p a b -> p (a b)"), pU[:, 1536:2048])),
                    reads=["pU3"], writes=["rT32"])
                def f(eng, kg=kg):
                    last = None
                    for i in range(4):
                        last = eng.matmul(pS[0][:, 0:NE], rT32b[:, i, :],
                                          sim32[:, kg * 4 + i, :],
                                          start=(kg == 0 and i == 0),
                                          stop=(kg == 3 and i == 3), skip_group_check=True)
                    return last
                s.step("tensor", f, reads=["rT32", "sim32"], writes=["pQmm"])

            hGall = s.step("vector", (lambda eng, t=t: eng.tensor_copy(
                Gall[:, :, t:t + 1], pS[0][:, 0:NE].unsqueeze(2))),
                reads=["pQmm"], writes=["G0"])

        wgT = wg[:].rearrange("p a b -> p b a")
        emit_chain(s, gating_chain(Gv, T1g[:], scB[:], mk8, mkT, s6b, s7b, csrT,
                                   snrT, sgrT, prioT, wgT, G4flat),
                   ["G0", "s6b", "s7b", "csr", "snr", "sgr", "prio"],
                   [f"w{t}" for t in range(NT)], first_deps=[hGall])

        stR.close()
        stW = ExitStack()
        wcat_eg = stW.enter_context(nc.sbuf_tensor("wcat_eg", [128, 16, 1536], BF16))
        barrier3 = [("vector", s.count["vector"]), ("scalar", s.count["scalar"]),
                    ("tensor", s.count["tensor"]),
                    ("sync0", s.count["sync0"]), ("sync1", s.count["sync1"])]
        firstmix = [True]
        for eg in range(4):
            for k in range(16):
                s.step("gpsimd", (lambda eng, eg=eg, k=k: eng.dma_start(
                    out=wcat_eg[:, k, :], in_=wcat_in[k, :, eg * 1536:(eg + 1) * 1536])),
                    writes=[f"wc{k}"],
                    extra_deps=(barrier3 if (eg == 0 and k < 2) else ()))
            for t in range(NT):
                for k in range(16):
                    def f(eng, t=t, k=k):
                        eng.matmul(pU[:, 0:512], hT2[:, t * 16 + k, :],
                                   wcat_eg[:, k, 0:512], start=(k == 0), stop=(k == 15))
                        eng.matmul(pU[:, 512:1024], hT2[:, t * 16 + k, :],
                                   wcat_eg[:, k, 512:1024], start=(k == 0), stop=(k == 15))
                        return eng.matmul(pU[:, 1024:1536], hT2[:, t * 16 + k, :],
                                          wcat_eg[:, k, 1024:1536], start=(k == 0), stop=(k == 15))
                    wr = ["pUmix", "pOC"] if firstmix[0] else ["pUmix"]
                    firstmix[0] = False
                    s.step("tensor", f, reads=[f"hT2_{t}_{k // 8}", f"wc{k}"], writes=wr)
                h = None
                for i in range(4):
                    def f(eng, t=t, eg=eg, i=i):
                        return eng.scalar_tensor_tensor(
                            out=qkvm[:, t, :], in0=pU[:, i * 384:(i + 1) * 384],
                            scalar=wg[:, t, eg * 4 + i:eg * 4 + i + 1], in1=qkvm[:, t, :],
                            op0=ALU.mult, op1=ALU.add)
                    h = s.step("vector", f, reads=["pUmix", f"w{t}", "qkvm_s"],
                               writes=["qkvm_s"], extra_deps=[h] if h else ())

        s.step("sync", (lambda eng: eng.dma_start(
            out=qkvm_out, in_=qkvm[:])),
            reads=["qkvm_s"], writes=["st_q"])
        s.step("sync", (lambda eng: eng.dma_start(
            out=w_out, in_=wg[:])),
            reads=[f"w{t}" for t in range(NT)], writes=["st_w"])
        s.wait_all("sync", [("sync0", s.count["sync0"]), ("sync1", s.count["sync1"])])
        stW.close()
        s.build(block, sems)
    return nc


def _host_prep_l3(inputs, r25, q2T, k2T, v2):
    wob = np.asarray(inputs["L0_attn_b_o_proj"])
    sim = np.asarray(inputs["L0_ffn_sim_matrix"])
    gates = np.asarray(inputs["L0_ffn_gates"])
    qp = np.asarray(inputs["L0_ffn_q_proj"])
    kp = np.asarray(inputs["L0_ffn_k_proj"])
    vp = np.asarray(inputs["L0_ffn_v_proj"])
    W3 = np.concatenate([qp, kp, vp], axis=2)
    wcat = np.ascontiguousarray(W3.transpose(1, 0, 2).reshape(16, 128, NE * 384))
    wob_r = np.ascontiguousarray(wob.reshape(16, 128, 16, 128).transpose(1, 2, 0, 3))
    sim_r = np.ascontiguousarray(sim.reshape(16, 128, NE).transpose(1, 0, 2))
    prio = np.arange(NE, 0, -1, dtype=np.float32).reshape(1, NE)
    ident = np.eye(128, dtype=np.float32)

    kfm = np.empty((16, 128, 8192), BF)
    v2f = np.empty((128, 64, DM), BF)
    for g in range(64):
        c, t2 = g % 8, g // 8
        kfm[:, :, g * 128:(g + 1) * 128] = k2T[c][:, :, t2 * 128:(t2 + 1) * 128]
        v2f[:, g, :] = v2[c][:, :, t2, :].reshape(128, DM)
    k2Tf = np.ascontiguousarray(
        kfm.reshape(16, 128, 16, 512).transpose(1, 2, 0, 3))

    shared = dict(wob=wob_r, sim=sim_r, gates=gates.reshape(1, NE), prio=prio,
                  wcat=wcat, ident=ident, k2Tf=k2Tf, v2f=v2f)
    maps = []
    for c in range(NCORE):
        msk = np.zeros((128, 1024), np.float32)
        for r in range(128):
            allowed = np.zeros(1024, bool)
            for i in range(8):
                if i < c:
                    allowed[i * 128:(i + 1) * 128] = True
                elif i == c:
                    allowed[i * 128:i * 128 + r + 1] = True
            msk[r, ~allowed] = -1e30
        maps.append(dict(r25=r25[c],
                         q2T=np.ascontiguousarray(np.asarray(q2T[c]).transpose(1, 0, 2)),
                         msk=msk, **shared))
    return maps


def run_l3(inputs, r25, q2T, k2T, v2):
    if "l3" not in _cache:
        _cache["l3"] = build_l3()
    maps = _host_prep_l3(inputs, r25, q2T, k2T, v2)
    res = run_bass_kernel_spmd(_cache["l3"], maps, core_ids=list(range(NCORE)))
    return res.results


def build_l4():
    nc = bass.Bass("TRN2", target_bir_lowering=False, detect_race_conditions=False)
    r3_in = nc.dram_tensor("r3", [NT, 128, DM], F32, kind="ExternalInput").ap()
    qm_in = nc.dram_tensor("qm", [128, NT, 128], F32, kind="ExternalInput").ap()
    kf_in = nc.dram_tensor("kf", [128, 64, 128], F32, kind="ExternalInput").ap()
    vf_in = nc.dram_tensor("vf", [128, 64, 128], F32, kind="ExternalInput").ap()
    w_in = nc.dram_tensor("w", [128, NT, NE], F32, kind="ExternalInput").ap()
    oc_in = nc.dram_tensor("ocat", [128, NE, DM], F32, kind="ExternalInput").ap()
    msk_in = nc.dram_tensor("msk", [128, 1024], F32, kind="ExternalInput").ap()
    ident_in = nc.dram_tensor("ident", [128, 128], F32, kind="ExternalInput").ap()
    out_out = nc.dram_tensor("out", [NT, 128, DM], F32, kind="ExternalOutput").ap()

    QS = 1.0 / float(np.sqrt(128.0))

    s = Sched()
    with ExitStack() as stk:
        E = stk.enter_context
        rbuf = E(nc.sbuf_tensor("rbuf", [128, NT, DM], F32))
        qbf = E(nc.sbuf_tensor("qbf", [128, NT, 128], BF16))
        aos = E(nc.sbuf_tensor("aos", [128, NT, 128], BF16))
        wgs = E(nc.sbuf_tensor("wgs", [128, NT, NE], F32))
        ident = E(nc.sbuf_tensor("idents", [128, 128], BF16))
        mskA = E(nc.sbuf_tensor("mskA", [128, 1024], F32))
        q_fm = E(nc.sbuf_tensor("q_fm", [128, 128], BF16))
        PTs = [E(nc.sbuf_tensor(f"PTs{i}", [128, 128], BF16)) for i in range(2)]
        tmpb = E(nc.sbuf_tensor("tmpb", [128, 4, 128], BF16))
        orng4 = [E(nc.sbuf_tensor(f"orng4_{i}", [128, 4, DM], BF16)) for i in range(2)]
        sc = E(nc.sbuf_tensor("scs", [128, 8], F32))
        S_sb = E(nc.sbuf_tensor("S_sb", [128, 8192], F32))
        P_sb = E(nc.sbuf_tensor("P_sb", [128, 8192], BF16))
        k_fm = E(nc.sbuf_tensor("k_fm", [128, 64, 128], BF16))
        v_sb = E(nc.sbuf_tensor("v_sb", [128, 64, 128], BF16))

        pS = [E(nc.psum_tensor(f"pS{i}", [128, 512], F32)) for i in range(2)]
        pU = E(nc.psum_tensor("pU", [128, DM], F32))
        pTb = [E(nc.psum_tensor(f"pTb{i}", [128, 8, 128], BF16)) for i in range(2)]
        PTb = [E(nc.sbuf_tensor(f"PTb{i}", [128, 8, 128], BF16)) for i in range(2)]

        sems = {l: E(nc.semaphore(f"SEM_{l}")) for l in LANES}
        block = E(nc.Block())

        for t in range(NT):
            s.step("sync", (lambda eng, t=t: eng.dma_start(out=rbuf[:, t, :], in_=r3_in[t])),
                   writes=[f"r{t}"])
        s.step("sync", (lambda eng: eng.dma_start(out=wgs[:], in_=w_in)),
               writes=["wgs"])
        s.step("sync", (lambda eng: eng.dma_start(out=mskA[:], in_=msk_in)), writes=["mskA"])
        s.step("gpsimd", (lambda eng: eng.dma_start(
            out=P_sb[:].bitcast(F32)[:, 0:NT * 128].rearrange("p (t f) -> p t f", t=NT),
            in_=qm_in)), writes=["PSBUF"])
        s.step("gpsimd", (lambda eng: eng.dma_start(
            out=v_sb[:], in_=vf_in)), writes=["v_sb"])
        s.step("gpsimd", (lambda eng: eng.dma_start(
            out=S_sb[:].bitcast(BF16)[:, 0:8192].rearrange("p (a b) -> p a b", a=64),
            in_=kf_in)), writes=["kbfS"])
        s.step("gpsimd", (lambda eng: eng.dma_start(out=ident[:], in_=ident_in)),
               writes=["id"])

        def f(eng):
            return eng.tensor_scalar(
                out=qbf[:].rearrange("p a b -> p (a b)"),
                in0=P_sb[:].bitcast(F32)[:, 0:NT * 128], scalar1=QS, scalar2=None,
                op0=ALU.mult)
        s.step("vector", f, reads=["PSBUF"], writes=["qbf"])

        for g in range(8):
            def f(eng, g=g):
                last = None
                for i in range(8):
                    b = g * 8 + i
                    last = eng.transpose(pTb[g % 2][:, i, :],
                                         S_sb[:].bitcast(BF16)[:, b * 128:(b + 1) * 128], ident[:])
                return last
            s.step("tensor", f, reads=["kbfS", "id"], writes=[f"pTb{g % 2}"])
            s.step("scalar", (lambda eng, g=g: eng.copy(
                k_fm[:, g * 8:(g + 1) * 8, :], pTb[g % 2][:])),
                reads=[f"pTb{g % 2}"], writes=["k_fm"])

        for j in range(NT):
            nch = 2 * j + 2
            L = nch * 512
            s.step("tensor", (lambda eng, j=j: eng.transpose(
                pTb[0][:, 0, :], qbf[:, j, :], ident[:])), reads=["qbf", "id"], writes=["pTb0"])
            s.step("scalar", (lambda eng: eng.copy(q_fm[:], pTb[0][:, 0, :])),
                   reads=["pTb0"], writes=["q_fm"])
            hev = None
            for ch in range(nch):
                s.step("tensor", (lambda eng, ch=ch: eng.matmul(
                    pS[ch % 2][:], q_fm[:], k_fm[:].rearrange("p a b -> p (a b)")[:, ch * 512:(ch + 1) * 512],
                    start=True, stop=True)),
                    reads=["q_fm", "k_fm"], writes=[f"pS{ch % 2}"])
                if ch >= nch - 2:
                    mo = (ch - (nch - 2)) * 512
                    def f(eng, ch=ch, mo=mo):
                        return eng.scalar_tensor_tensor(
                            out=S_sb[:, ch * 512:(ch + 1) * 512], in0=pS[ch % 2][:],
                            scalar=1.0, in1=mskA[:, mo:mo + 512],
                            op0=ALU.mult, op1=ALU.add)
                    hev = s.step("vector", f, reads=[f"pS{ch % 2}", "mskA"], writes=["Ssb"])
                else:
                    hev = s.step("vector", (lambda eng, ch=ch: eng.tensor_copy(
                        S_sb[:, ch * 512:(ch + 1) * 512], pS[ch % 2][:])),
                        reads=[f"pS{ch % 2}"], writes=["Ssb"])
            h1 = s.step("vector", (lambda eng, L=L: eng.reduce_max(
                sc[:, 0:1], S_sb[:, 0:L], axis=AX.X)), reads=["Ssb"], writes=["smx"],
                extra_deps=[hev])
            s.step("vector", (lambda eng: eng.tensor_scalar(
                out=sc[:, 1:2], in0=sc[:, 0:1], scalar1=-1.0, scalar2=None, op0=ALU.mult)),
                reads=["smx"], writes=["smx"], extra_deps=[h1])
            s.step("scalar", (lambda eng, L=L: eng.activation(
                out=P_sb[:, 0:L], in_=S_sb[:, 0:L], func=AF.Exp, bias=sc[:, 1:2])),
                reads=["Ssb", "smx"], writes=["PSBUF"])
            h2 = s.step("vector", (lambda eng, L=L: eng.reduce_sum(
                sc[:, 2:3], P_sb[:, 0:L], axis=AX.X)), reads=["PSBUF"], writes=["ssum"])
            h3 = s.step("vector", (lambda eng: eng.reciprocal(sc[:, 3:4], sc[:, 2:3])),
                        reads=["ssum"], writes=["srec"], extra_deps=[h2])
            nb = nch * 4
            ng = nb // 8
            for g in range(ng):
                def f(eng, g=g):
                    last = None
                    for i in range(8):
                        b2 = g * 8 + i
                        last = eng.transpose(pTb[g % 2][:, i, :],
                                             P_sb[:, b2 * 128:(b2 + 1) * 128], ident[:])
                    return last
                s.step("tensor", f, reads=["PSBUF", "id"], writes=[f"pTb{g % 2}"])
                s.step("scalar", (lambda eng, g=g: eng.copy(PTb[g % 2][:], pTb[g % 2][:])),
                       reads=[f"pTb{g % 2}"], writes=[f"PTk{g % 2}"])
                def f(eng, g=g, ng=ng):
                    last = None
                    for i in range(8):
                        b2 = g * 8 + i
                        last = eng.matmul(pU[:, 0:128], PTb[g % 2][:, i, :], v_sb[:, b2, :],
                                          start=(b2 == 0), stop=(g == ng - 1 and i == 7))
                    return last
                s.step("tensor", f, reads=[f"PTk{g % 2}", "v_sb"], writes=["pUmix"])
            s.step("scalar", (lambda eng, j=j: eng.activation(
                out=aos[:, j, :], in_=pU[:, 0:128], func=AF.Identity, scale=sc[:, 3:4])),
                reads=["pUmix", "srec"], writes=[f"ao{j}"], extra_deps=[h3])

        for e4 in range(4):
            s.step("gpsimd", (lambda eng, e4=e4: eng.dma_start(
                out=orng4[e4 % 2][:], in_=oc_in[:, e4 * 4:(e4 + 1) * 4, :])),
                writes=[f"oc{e4 % 2}"])
            for j in range(NT):
                p = (e4 * NT + j) % 2

                def f(eng, j=j, e4=e4):
                    last = None
                    for i in range(4):
                        last = eng.activation(
                            out=tmpb[:, i, :], in_=aos[:, j, :], func=AF.Identity,
                            scale=wgs[:, j, e4 * 4 + i:e4 * 4 + i + 1])
                    return last
                s.step("scalar", f, reads=[f"ao{j}", "wgs"], writes=["tmpb"])

                def f(eng, p=p):
                    last = None
                    for i in range(4):
                        last = eng.transpose(pTb[p][:, i, :], tmpb[:, i, :], ident[:])
                    return last
                s.step("tensor", f, reads=["tmpb", "id"], writes=[f"pTb{p}"])
                s.step("scalar", (lambda eng, p=p: eng.copy(
                    PTb[p][:, 0:4, :], pTb[p][:, 0:4, :])),
                    reads=[f"pTb{p}"], writes=[f"PTk{p}"])

                def f(eng, e4=e4, p=p):
                    last = None
                    for i in range(4):
                        for q4 in range(4):
                            last = eng.matmul(pU[:, q4 * 512:(q4 + 1) * 512],
                                              PTb[p][:, i, :],
                                              orng4[e4 % 2][:, i, q4 * 512:(q4 + 1) * 512],
                                              start=(i == 0), stop=(i == 3))
                    return last
                s.step("tensor", f, reads=[f"PTk{p}", f"oc{e4 % 2}"], writes=["pUmix"])
                s.step("vector", (lambda eng, j=j: eng.tensor_add(
                    rbuf[:, j, :], rbuf[:, j, :], pU[:])),
                    reads=[f"r{j}", "pUmix"], writes=[f"r{j}"])

        for j in range(NT):
            s.step("sync", (lambda eng, j=j: eng.dma_start(
                out=out_out[j], in_=rbuf[:, j, :])), reads=[f"r{j}"], writes=[f"st{j}"])

        s.wait_all("sync", [("sync0", s.count["sync0"]), ("sync1", s.count["sync1"])])
        s.build(block, sems)
    return nc


def _host_prep_l4(inputs, r3, qkvm3, w3):
    op = np.asarray(inputs["L0_ffn_o_proj"])
    ident = np.eye(128, dtype=np.float32)
    kf = np.empty((128, 64, 128), np.float32)
    vf = np.empty((128, 64, 128), np.float32)
    for g in range(64):
        c, t2 = g % 8, g // 8
        kf[:, g, :] = qkvm3[c][:, t2, 128:256]
        vf[:, g, :] = qkvm3[c][:, t2, 256:384]
    shared = dict(ocat=np.ascontiguousarray(op.transpose(1, 0, 2)), ident=ident,
                  kf=kf, vf=vf)
    maps = []
    for c in range(NCORE):
        msk = np.zeros((128, 1024), np.float32)
        for r in range(128):
            allowed = np.zeros(1024, bool)
            for i in range(8):
                if i < c:
                    allowed[i * 128:(i + 1) * 128] = True
                elif i == c:
                    allowed[i * 128:i * 128 + r + 1] = True
            msk[r, ~allowed] = -1e30
        maps.append(dict(r3=r3[c],
                         qm=np.ascontiguousarray(np.asarray(qkvm3[c])[:, :, 0:128]),
                         w=np.asarray(w3[c]), msk=msk, **shared))
    return maps


def run_l4(inputs, r3, qkvm3, w3):
    if "l4" not in _cache:
        _cache["l4"] = build_l4()
    maps = _host_prep_l4(inputs, r3, qkvm3, w3)
    res = run_bass_kernel_spmd(_cache["l4"], maps, core_ids=list(range(NCORE)))
    return res.results


def _host_prep_l1(inputs):
    x = np.asarray(inputs["L0_attn_a_x"])[0]
    Wqkv = np.asarray(inputs["L0_attn_a_Wqkv"])
    bqkv = np.asarray(inputs["L0_attn_a_bqkv"])
    Wout = np.asarray(inputs["L0_attn_a_Wout"])
    bout = np.asarray(inputs["L0_attn_a_bout"])
    sim = np.asarray(inputs["L0_ffn_sim_matrix"])
    gates = np.asarray(inputs["L0_ffn_gates"])
    qp = np.asarray(inputs["L0_ffn_q_proj"])
    kp = np.asarray(inputs["L0_ffn_k_proj"])
    vp = np.asarray(inputs["L0_ffn_v_proj"])

    aWr = np.ascontiguousarray(Wqkv.transpose(1, 0, 2).reshape(64, 32 * 192))
    aW2 = np.concatenate([aWr, aWr], axis=0)
    ab = bqkv.reshape(1, 32 * 192)
    aWo2 = np.concatenate([Wout, Wout], axis=0)
    abo = bout.reshape(1, 64)
    sim_r = np.ascontiguousarray(sim.reshape(16, 128, NE))
    prio = np.arange(NE, 0, -1, dtype=np.float32).reshape(1, NE)
    W3 = np.concatenate([qp, kp, vp], axis=2)
    wcat = np.ascontiguousarray(W3.transpose(1, 0, 2).reshape(16, 128, NE * 384))
    ident = np.eye(128, dtype=np.float32)

    shared = dict(aW2=np.ascontiguousarray(aW2), ab=np.ascontiguousarray(ab),
                  aWo2=np.ascontiguousarray(aWo2), abo=np.ascontiguousarray(abo),
                  sim=sim_r, gates=gates.reshape(1, NE), prio=prio,
                  wcat=wcat, ident=ident)
    maps = []
    for c in range(NCORE):
        xs = np.stack([x[(8 * j + c) * 128:(8 * j + c + 1) * 128] for j in range(NT)], 0)
        maps.append(dict(x=np.ascontiguousarray(xs), **shared))
    return maps


def run_l1(inputs):
    if "l1" not in _cache:
        _cache["l1"] = build_l1()
    maps = _host_prep_l1(inputs)
    res = run_bass_kernel_spmd(_cache["l1"], maps, core_ids=list(range(NCORE)))
    return res.results


def kernel(**inputs):
    inputs = {k: np.asarray(v) for k, v in inputs.items()}
    res1 = run_l1(inputs)
    r1 = [r["r1"] for r in res1]
    qkvm = [r["qkvm"] for r in res1]
    w = [r["w"] for r in res1]
    res2 = run_l2(inputs, r1, qkvm, w)
    r25 = [r["r25"] for r in res2]
    q2T = [np.asarray(r["q2T"]) for r in res2]
    k2T = [np.asarray(r["k2T"]) for r in res2]
    v2 = [np.asarray(r["v2"]) for r in res2]
    res3 = run_l3(inputs, r25, q2T, k2T, v2)
    r3 = [r["r3"] for r in res3]
    qkvm3 = [r["qkvm"] for r in res3]
    w3 = [r["w"] for r in res3]
    res4 = run_l4(inputs, r3, qkvm3, w3)
    out = np.empty((8192, DM), np.float32)
    for c in range(NCORE):
        o = res4[c]["out"]
        for j in range(NT):
            g = 8 * j + c
            out[g * 128:(g + 1) * 128] = o[j]
    return out.reshape(1, 8192, DM)


# version 20 (final)
# speedup vs baseline: 1.1735x; 1.1735x over previous
import numpy as np
import ml_dtypes
import concourse.bass as bass
import concourse.mybir as mybir
from concourse.bass_utils import run_bass_kernel_spmd
from contextlib import ExitStack

F32 = mybir.dt.float32
BF16 = mybir.dt.bfloat16
AF = mybir.ActivationFunctionType
ALU = mybir.AluOpType
AX = mybir.AxisListType
BF = ml_dtypes.bfloat16

NCORE = 8
TPC = 1024
NT = 8
DM = 2048
NE = 16
EPS = 1e-5
ENGS = ("sync", "gpsimd", "tensor", "vector", "scalar")

_cache = {}


DMA_ENGS = ("sync", "gpsimd")
LANES = [e for e in ENGS if e not in DMA_ENGS] + \
        [f"{e}{i}" for e in DMA_ENGS for i in range(2)]


class Sched:

    def __init__(self):
        self.steps = []
        self.count = {l: 0 for l in LANES}
        self.nq = {e: 0 for e in DMA_ENGS}
        self.last_w = {}
        self.readers = {}

    def step(self, eng, fn, reads=(), writes=(), extra_deps=()):
        if eng in DMA_ENGS:
            lane = f"{eng}{self.nq[eng] % 2}"
            self.nq[eng] += 1
            inc = 16
            deps = [(lane, self.count[lane])] if self.count[lane] else []
        else:
            lane = eng
            inc = 1
            deps = []
        for k in reads:
            if k in self.last_w:
                deps.append(self.last_w[k])
        for k in writes:
            if k in self.last_w:
                deps.append(self.last_w[k])
            deps.extend(self.readers.get(k, ()))
        waits = {}
        for (l, c) in deps:
            if l == eng:
                continue
            waits[l] = max(waits.get(l, 0), c)
        for (l, c) in extra_deps:
            waits[l] = max(waits.get(l, 0), c)
        self.count[lane] += inc
        me = (lane, self.count[lane])
        for k in reads:
            self.readers.setdefault(k, []).append(me)
        for k in writes:
            self.last_w[k] = me
            self.readers[k] = []
        self.steps.append((eng, lane, fn, sorted(waits.items()), inc))
        return me

    def wait_all(self, eng, deps):
        waits = {}
        for (l, c) in deps:
            if l == eng:
                continue
            waits[l] = max(waits.get(l, 0), c)
        self.steps.append((eng, eng if eng not in DMA_ENGS else f"{eng}0",
                           None, sorted(waits.items()), 0))

    def build(self, block, sems):
        prog = {e: [] for e in ENGS}
        watermark = {e: {l: 0 for l in LANES} for e in ENGS}
        for eng, lane, fn, waits, inc in self.steps:
            needed = []
            for (l, c) in waits:
                if watermark[eng][l] < c:
                    watermark[eng][l] = c
                    needed.append((l, c))
            prog[eng].append((fn, needed, lane, inc))
        for e in ENGS:
            items = prog[e]

            def body(eng, items=items):
                for fn, needed, lane, inc in items:
                    for (l, c) in needed:
                        eng.wait_ge(sems[l], c)
                    if fn is not None:
                        last = fn(eng)
                        assert last is not None, "step fn must return last instruction"
                        last.then_inc(sems[lane], inc)
            getattr(block, e)(body)


def ts(i, n=128):
    return slice(i * n, (i + 1) * n)


def cview(ap, c):
    return ap.rearrange("p (c g) -> p c g", c=c)



def gating_chain(G, T1, scB, mk8, mkT, s6b, s7b, csrT, snrT, sgrT, prioT, wgT, G4flat):
    SHP = [128, NE, NT]
    bt = lambda a: a.broadcast_to(SHP)
    Gall, G1, G2, G3, G4, G5 = G
    cnt, m1, inact, p1, m2, p2, mx, den, rec = [scB[:, i:i + 1, :] for i in range(9)]
    s6B = s6b[:].unsqueeze(1).broadcast_to(SHP)
    s7B = s7b[:].unsqueeze(1).broadcast_to(SHP)
    C = []
    V = lambda fn: C.append(("vector", fn))

    def tree(op, srcv, dst):
        f = lambda e: getattr(e, f"tensor_{op}")
        V(lambda e, srcv=srcv, op=op: getattr(e, f"tensor_{op}")(
            T1[:, 0:8, :], srcv[:, 0:8, :], srcv[:, 8:16, :]))
        V(lambda e, op=op: getattr(e, f"tensor_{op}")(
            T1[:, 0:4, :], T1[:, 0:4, :], T1[:, 4:8, :]))
        V(lambda e, op=op: getattr(e, f"tensor_{op}")(
            T1[:, 0:2, :], T1[:, 0:2, :], T1[:, 2:4, :]))
        V(lambda e, dst=dst, op=op: getattr(e, f"tensor_{op}")(
            dst, T1[:, 0:1, :], T1[:, 1:2, :]))

    V(lambda e: e.tensor_mul(G1[:], Gall[:], s6B))
    V(lambda e: e.tensor_mul(G2[:], csrT[:], s7B))
    V(lambda e: e.tensor_sub(G1[:], G1[:], G2[:]))
    V(lambda e: e.tensor_mul(G1[:], G1[:], snrT[:]))
    V(lambda e: e.tensor_sub(G1[:], G1[:], sgrT[:]))
    V(lambda e: e.tensor_scalar_max(G2[:], G1[:], 0.0))
    V(lambda e: e.tensor_scalar(out=mk8[:], in0=G1[:], scalar1=0.0,
                                scalar2=None, op0=ALU.is_gt))
    V(lambda e: e.tensor_copy(G3[:], mk8[:]))
    tree("add", G3, cnt)
    tree("max", G1, m1)
    V(lambda e: e.tensor_scalar(out=mkT[:], in0=cnt.rearrange("p a b -> p (a b)"),
                                scalar1=0.5, scalar2=None, op0=ALU.is_lt))
    V(lambda e: e.tensor_copy(inact.rearrange("p a b -> p (a b)"), mkT[:]))
    V(lambda e: e.tensor_tensor(mk8[:], G1[:], bt(m1), ALU.is_ge))
    V(lambda e: e.tensor_copy(G4[:], mk8[:]))
    V(lambda e: e.tensor_mul(G4[:], G4[:], prioT[:]))
    tree("max", G4, p1)
    V(lambda e: e.tensor_tensor(mk8[:], G4[:], bt(p1), ALU.is_ge))
    V(lambda e: e.tensor_copy(G4[:], mk8[:]))
    V(lambda e: e.scalar_tensor_tensor(out=G5[:], in0=G4[:], scalar=-1e9,
                                       in1=G1[:], op0=ALU.mult, op1=ALU.add))
    tree("max", G5, m2)
    V(lambda e: e.tensor_tensor(mk8[:], G5[:], bt(m2), ALU.is_ge))
    V(lambda e: e.tensor_copy(G5[:], mk8[:]))
    V(lambda e: e.tensor_mul(G5[:], G5[:], prioT[:]))
    tree("max", G5, p2)
    V(lambda e: e.tensor_tensor(mk8[:], G5[:], bt(p2), ALU.is_ge))
    V(lambda e: e.tensor_copy(G5[:], mk8[:]))
    V(lambda e: e.tensor_add(G4[:], G4[:], G5[:]))
    V(lambda e: e.tensor_mul(G4[:], G4[:], bt(inact)))
    V(lambda e: e.tensor_max(G3[:], G3[:], G4[:]))
    V(lambda e: e.tensor_scalar(out=G4[:], in0=G3[:], scalar1=1e9,
                                scalar2=-1e9, op0=ALU.mult, op1=ALU.add))
    V(lambda e: e.tensor_add(G4[:], G2[:], G4[:]))
    tree("max", G4, mx)
    V(lambda e: e.scalar_tensor_tensor(out=G4[:], in0=bt(mx), scalar=-1.0,
                                       in1=G4[:], op0=ALU.mult, op1=ALU.add))
    C.append(("scalar", lambda e: e.activation(out=G4flat, in_=G4flat, func=AF.Exp)))
    tree("add", G4, den)
    V(lambda e: e.reciprocal(rec.rearrange("p a b -> p (a b)"),
                             den.rearrange("p a b -> p (a b)")))
    V(lambda e: e.tensor_mul(G4[:], G4[:], bt(rec)))
    V(lambda e: e.tensor_mul(wgT, G4[:], G3[:]))
    return C


def emit_chain(s, C, first_reads, final_writes, first_deps=()):
    h = None
    for i, (eng, fn) in enumerate(C):
        h = s.step(eng, fn, reads=(first_reads if i == 0 else ["gch"]),
                   writes=(["gch"] + (final_writes if i == len(C) - 1 else [])),
                   extra_deps=([h] if h else list(first_deps)))
    return h


def build_l1():
    nc = bass.Bass("TRN2", target_bir_lowering=False, detect_race_conditions=False)
    x_in = nc.dram_tensor("x", [NT, 128, DM], F32, kind="ExternalInput").ap()
    aW2_in = nc.dram_tensor("aW2", [128, 6144], F32, kind="ExternalInput").ap()
    ab_in = nc.dram_tensor("ab", [1, 6144], F32, kind="ExternalInput").ap()
    aWo2_in = nc.dram_tensor("aWo2", [128, 64], F32, kind="ExternalInput").ap()
    abo_in = nc.dram_tensor("abo", [1, 64], F32, kind="ExternalInput").ap()
    sim_in = nc.dram_tensor("sim", [16, 128, NE], F32, kind="ExternalInput").ap()
    gates_in = nc.dram_tensor("gates", [1, NE], F32, kind="ExternalInput").ap()
    prio_in = nc.dram_tensor("prio", [1, NE], F32, kind="ExternalInput").ap()
    wcat_in = nc.dram_tensor("wcat", [16, 128, NE * 384], F32, kind="ExternalInput").ap()
    ident_in = nc.dram_tensor("ident", [128, 128], F32, kind="ExternalInput").ap()

    r1_out = nc.dram_tensor("r1", [NT, 128, DM], F32, kind="ExternalOutput").ap()

    qkvm_out = nc.dram_tensor("qkvm", [128, NT, 384], F32, kind="ExternalOutput").ap()
    w_out = nc.dram_tensor("w", [128, NT, NE], F32, kind="ExternalOutput").ap()
    scr_d = nc.dram_tensor("scrd", [4, NE], F32)

    s = Sched()
    with ExitStack() as stk:
        E = stk.enter_context
        rbuf = E(nc.sbuf_tensor("rbuf", [128, NT, DM], F32))
        hT2 = E(nc.sbuf_tensor("hT2", [128, NT * 16, 128], BF16))
        wg = E(nc.sbuf_tensor("wg", [128, NT, NE], F32))
        aW2 = E(nc.sbuf_tensor("aW2s", [128, 6144], BF16))
        ab = E(nc.sbuf_tensor("abs", [1, 6144], BF16))
        aWo2 = E(nc.sbuf_tensor("aWo2s", [128, 64], BF16))
        abo = E(nc.sbuf_tensor("abos", [1, 64], BF16))
        ident = E(nc.sbuf_tensor("idents", [128, 128], BF16))
        ident32 = E(nc.sbuf_tensor("id32s", [128, 128], F32))
        ones = E(nc.sbuf_tensor("oness", [1, 128], BF16))
        sim32 = E(nc.sbuf_tensor("sim32", [128, 16, NE], F32))
        nrow = E(nc.sbuf_tensor("nrow", [1, 64], F32))
        ones32 = E(nc.sbuf_tensor("ones32", [128, 1], F32))
        hT = E(nc.sbuf_tensor("hT", [128, 16, 128], BF16))
        sD = E(nc.sbuf_tensor("sD", [128, 32, 64], BF16))
        ctx = E(nc.sbuf_tensor("ctxs", [128, 32, 64], BF16))
        ctxT = E(nc.sbuf_tensor("ctxT", [128, 16, 128], BF16))
        bst = E(nc.sbuf_tensor("bst", [128, 24], F32))
        mv = E(nc.sbuf_tensor("mv", [128, 2], F32))
        sc = E(nc.sbuf_tensor("scs", [128, 8], F32))
        Gall = E(nc.sbuf_tensor("Gall", [128, NE, NT], F32))
        mk8 = E(nc.sbuf_tensor("mk8s", [128, NE, NT], mybir.dt.uint8))
        mkT = E(nc.sbuf_tensor("mkTs", [128, NT], mybir.dt.uint8))
        s6b = E(nc.sbuf_tensor("s6bs", [128, NT], F32))
        s7b = E(nc.sbuf_tensor("s7bs", [128, NT], F32))
        prioT = E(nc.sbuf_tensor("prioT", [128, NE, NT], F32))
        csrT = E(nc.sbuf_tensor("csrT", [128, NE, NT], F32))
        snrT = E(nc.sbuf_tensor("snrT", [128, NE, NT], F32))
        sgrT = E(nc.sbuf_tensor("sgrT", [128, NE, NT], F32))
        prio_s = E(nc.sbuf_tensor("prio_s", [128, NE], F32))
        csr_s = E(nc.sbuf_tensor("csr_s", [128, NE], F32))
        snr_s = E(nc.sbuf_tensor("snr_s", [128, NE], F32))
        sgr_s = E(nc.sbuf_tensor("sgr_s", [128, NE], F32))
        grow = E(nc.sbuf_tensor("grows", [1, NE], F32))
        epsc = E(nc.sbuf_tensor("epscs", [128, 1], F32))
        rT32b = E(nc.sbuf_tensor("rT32b", [128, 4, 128], F32))
        sDf = sD[:].rearrange("p a b -> p (a b)").bitcast(F32)
        Gv = [Gall[:]] + [sDf[:, i * 128:(i + 1) * 128]
                          .rearrange("p (a b) -> p a b", a=NE) for i in range(5)]
        G4flat = sDf[:, 3 * 128:4 * 128]
        T1g = sDf[:, 640:704].rearrange("p (a b) -> p a b", a=8)
        scB = sDf[:, 704:800].rearrange("p (a b) -> p a b", a=12)
        inner = ExitStack()
        ssq = inner.enter_context(nc.sbuf_tensor("ssq", [128, 256], F32))
        hbf = inner.enter_context(nc.sbuf_tensor("hbf", [128, DM], BF16))
        qkvP = [inner.enter_context(nc.sbuf_tensor(f"qkvs{i}", [128, 32, 192], BF16))
                for i in range(2)]
        sS = [inner.enter_context(nc.sbuf_tensor(f"sS{i}", [128, 4096], BF16)) for i in range(2)]
        sE = [inner.enter_context(nc.sbuf_tensor(f"sE{i}", [128, 4096], BF16)) for i in range(2)]

        pU = E(nc.psum_tensor("pU", [128, DM], F32))
        pQ4 = E(nc.psum_tensor("pQ4", [128, 1024], F32))
        pTb = [E(nc.psum_tensor(f"pTb{i}", [128, 8, 128], BF16)) for i in range(2)]

        sems = {l: E(nc.semaphore(f"SEM_{l}")) for l in LANES}
        block = E(nc.Block())

        QKV_KEYS = [f"qkv{c}" for c in range(0, 32, 2)]
        CTX_KEYS = [f"ctx{dc}" for dc in range(16)]

        for t in range(NT):
            s.step("sync", (lambda eng, t=t: eng.dma_start(out=rbuf[:, t, :], in_=x_in[t])),
                   writes=[f"r{t}"])
        for dst, src, key in ((aW2, aW2_in, "aW2"), (ab, ab_in, "ab"),
                              (aWo2, aWo2_in, "aWo2"), (abo, abo_in, "abo"),
                              (ident, ident_in, "id"), (ident32, ident_in, "id32"),
                              (grow, gates_in, "grow")):
            s.step("gpsimd", (lambda eng, d=dst, sr=src: eng.dma_start(out=d[:], in_=sr)),
                   writes=[key])
        s.step("gpsimd", (lambda eng: eng.dma_start(
            out=sim32[:], in_=sim_in.rearrange("a p e -> p a e"))), writes=["sim32"])

        def f(eng):
            eng.memset(ones[:], 1.0)
            eng.memset(ones32[:], 1.0)
            eng.memset(epsc[:], EPS)
            aq = cview(aW2[:], 32)[:, :, 0:64]
            eng.tensor_scalar_mul(aq, aq, 0.125)
            abq = cview(ab[:], 32)[:, :, 0:64]
            eng.tensor_scalar_mul(abq, abq, 0.125)
            s32 = sim32[:].rearrange("p a e -> p (a e)")
            return eng.tensor_mul(ssq[:], s32, s32)
        s.step("vector", f, reads=["sim32"],
               writes=["aW2", "ab", "ssq", "ones", "ones32"],
               extra_deps=[s.last_w["aW2"], s.last_w["ab"]])

        def f(eng):
            for k in range(16):
                eng.matmul(pQ4[0:1, 0:NE], ones32[:], sim32[:, k, :],
                           start=(k == 0), stop=(k == 15), skip_group_check=True)
            last = None
            for k in range(16):
                last = eng.matmul(pQ4[0:1, 32:32 + NE], ones32[:],
                                  ssq[:, k * NE:(k + 1) * NE],
                                  start=(k == 0), stop=(k == 15), skip_group_check=True)
            return last
        s.step("tensor", f, reads=["ssq", "sim32", "ones32"], writes=["pQb"])

        def f(eng):
            eng.tensor_copy(nrow[:, 0:NE], pQ4[0:1, 0:NE])
            return eng.tensor_copy(nrow[:, 32:32 + NE], pQ4[0:1, 32:32 + NE])
        s.step("vector", f, reads=["pQb"], writes=["nrow"])

        def f(eng):
            eng.activation(out=grow[:], in_=grow[:], func=AF.Exp, scale=-1.0)
            return eng.activation(out=nrow[:, 32:32 + NE], in_=nrow[:, 32:32 + NE], func=AF.Ln)
        s.step("scalar", f, reads=["grow", "nrow"], writes=["grow", "nrow"])

        s.step("scalar", (lambda eng: eng.activation(
            out=nrow[:, 32:32 + NE], in_=nrow[:, 32:32 + NE], func=AF.Exp, scale=-0.5)),
            reads=["nrow"], writes=["nrow"])

        def f(eng):
            eng.tensor_scalar_add(grow[:], grow[:], 1.0)
            return eng.reciprocal(grow[:], grow[:])
        s.step("vector", f, reads=["grow"], writes=["grow"])

        s.step("gpsimd", (lambda eng: eng.dma_start(out=scr_d[0:1, :], in_=grow[:])),
               reads=["grow"], writes=["scr0"])
        s.step("gpsimd", (lambda eng: eng.dma_start(
            out=scr_d[1:2, :], in_=nrow[:, 0:NE])),
            reads=["nrow"], writes=["scr1"])
        s.step("gpsimd", (lambda eng: eng.dma_start(
            out=scr_d[2:3, :], in_=nrow[:, 32:32 + NE])),
            reads=["nrow"], writes=["scr2"])
        s.step("gpsimd", (lambda eng: eng.dma_start(
            out=sgr_s[:], in_=scr_d[0:1, :].broadcast_to([128, NE]))),
            reads=["scr0"], writes=["sgr_s"])
        s.step("gpsimd", (lambda eng: eng.dma_start(
            out=csr_s[:], in_=scr_d[1:2, :].broadcast_to([128, NE]))),
            reads=["scr1"], writes=["csr_s"])
        s.step("gpsimd", (lambda eng: eng.dma_start(
            out=snr_s[:], in_=scr_d[2:3, :].broadcast_to([128, NE]))),
            reads=["scr2"], writes=["snr_s"])
        s.step("gpsimd", (lambda eng: eng.dma_start(
            out=prio_s[:], in_=prio_in.broadcast_to([128, NE]))), writes=["prio_s"])
        for big, small, bk, sk in ((sgrT, sgr_s, "sgr", "sgr_s"),
                                   (csrT, csr_s, "csr", "csr_s"),
                                   (snrT, snr_s, "snr", "snr_s"),
                                   (prioT, prio_s, "prio", "prio_s")):
            s.step("vector", (lambda eng, big=big, small=small: eng.tensor_copy(
                big[:].rearrange("p a b -> p b a"),
                small[:].unsqueeze(1).broadcast_to([128, NT, NE]))),
                reads=[sk], writes=[bk])

        def ln_stats(t):
            rk = f"r{t}"

            def f(eng, t=t):
                for i in range(3):
                    eng.bn_stats(bst[:, i * 6:(i + 1) * 6], rbuf[:, t, ts(i, 512)])
                return eng.bn_stats(bst[:, 18:24], rbuf[:, t, ts(3, 512)])
            h = s.step("vector", f, reads=[rk], writes=["bst"])
            s.step("vector", (lambda eng: eng.bn_aggr(mv[:], bst[:])),
                   reads=["bst"], writes=["mv"], extra_deps=[h])

        def QK(t):
            return [f"qk{t % 2}_{c}" for c in range(0, 32, 2)]

        def emit_front(t):
            rk = f"r{t}"
            par = t % 2
            qb = qkvP[par]
            ln_stats(t)

            def f(eng):
                eng.activation(out=sc[:, 0:1], in_=mv[:, 1:2], func=AF.Ln, bias=epsc[:])
                return eng.activation(out=sc[:, 2:3], in_=sc[:, 0:1], func=AF.Exp, scale=-0.5)
            s.step("scalar", f, reads=["mv"], writes=["sc02"])

            s.step("vector", (lambda eng: eng.scalar_tensor_tensor(
                out=sc[:, 4:5], in0=mv[:, 0:1], scalar=-1.0, in1=sc[:, 2:3],
                op0=ALU.mult, op1=ALU.mult)), reads=["mv", "sc02"], writes=["sc4"])

            s.step("scalar", (lambda eng, t=t: eng.activation(
                out=hbf[:], in_=rbuf[:, t, :], func=AF.Identity,
                scale=sc[:, 2:3], bias=sc[:, 4:5])),
                reads=[rk, "sc02", "sc4"], writes=["hbf"])

            for g in range(2):
                def f(eng, g=g):
                    last = None
                    for i in range(8):
                        last = eng.transpose(pTb[g % 2][:, i, :], hbf[:, ts(g * 8 + i)], ident[:])
                    return last
                s.step("tensor", f, reads=["hbf", "id"], writes=[f"pTb{g % 2}"])
                s.step("scalar", (lambda eng, g=g: eng.copy(
                    hT[:, g * 8:(g + 1) * 8, :], pTb[g % 2][:])),
                    reads=[f"pTb{g % 2}"], writes=[f"hTg{g}"])

        def emit_qkv2(t, g2):
            par = t % 2
            qb = qkvP[par]
            for i in range(2):
                c = g2 * 2 + i
                bk = c % 2

                def f(eng, c=c, bk=bk):
                    j, half = c // 2, c % 2
                    eng.matmul(pQ4[:, bk * 512:bk * 512 + 192],
                               hT[:, j, :][half * 64:half * 64 + 64, :],
                               aW2[half * 64:half * 64 + 64, c * 192:(c + 1) * 192],
                               start=True, stop=False)
                    return eng.matmul(pQ4[:, bk * 512:bk * 512 + 192], ones[:],
                                      ab[:, c * 192:(c + 1) * 192], start=False, stop=True)
                s.step("tensor", f, reads=["hTg0", "hTg1", "aW2", "ab", "ones"],
                       writes=[f"pQb{bk}"])
                s.step("scalar", (lambda eng, c=c, bk=bk, qb=qb: eng.copy(
                    qb[:, c, :], pQ4[:, bk * 512:bk * 512 + 192])),
                    reads=[f"pQb{bk}"], writes=[f"qk{par}_{(c // 2) * 2}"])

        SCR = ["cTg0", "cTg1"]

        def emit_S(t, dc):
            o, b = dc * 4, dc % 2
            qb = qkvP[t % 2]

            def f(eng, o=o, b=b, qb=qb):
                q = qb[:, :, o:o + 4].unsqueeze(2).broadcast_to([128, 32, 32, 4])
                k = qb[:, :, 64 + o:64 + o + 4].unsqueeze(1).broadcast_to([128, 32, 32, 4])
                return eng.tensor_tensor(
                    sS[b][:].rearrange("p (c e d) -> p c e d", c=32, e=32), q, k, ALU.mult)
            s.step("vector", f, reads=QK(t), writes=[f"sS{b}"])

        def emit_exp(t, dc):
            b = dc % 2
            s.step("scalar", (lambda eng, b=b: eng.activation(
                out=sE[b][:], in_=sS[b][:], func=AF.Exp)),
                reads=[f"sS{b}"], writes=[f"sE{b}"])

        def emit_trees(t, dc):
            o, b = dc * 4, dc % 2
            qb = qkvP[t % 2]

            def f(eng, o=o, b=b):
                Ev = sE[b][:]
                ctA = ctxT[:].rearrange("p a b -> p (a b)")
                stf = sS[b][:]
                eng.tensor_add(cview(ctA, 32), cview(Ev, 32)[:, :, 0:64],
                               cview(Ev, 32)[:, :, 64:128])
                eng.tensor_add(cview(stf[:, 0:1024], 32), cview(ctA, 32)[:, :, 0:32],
                               cview(ctA, 32)[:, :, 32:64])
                eng.tensor_add(cview(stf[:, 1024:1536], 32), cview(stf[:, 0:1024], 32)[:, :, 0:16],
                               cview(stf[:, 0:1024], 32)[:, :, 16:32])
                eng.tensor_add(cview(stf[:, 1536:1792], 32), cview(stf[:, 1024:1536], 32)[:, :, 0:8],
                               cview(stf[:, 1024:1536], 32)[:, :, 8:16])
                eng.tensor_add(sD[:, :, o:o + 4], cview(stf[:, 1536:1792], 32)[:, :, 0:4],
                               cview(stf[:, 1536:1792], 32)[:, :, 4:8])
                with nc.allow_low_precision(reason="bf16 softmax denom, within tolerance"):
                    return eng.reciprocal(sD[:, :, o:o + 4], sD[:, :, o:o + 4])
            hA = s.step("vector", f, reads=[f"sE{b}"], writes=["sDG", f"sS{b}"] + SCR)

            def f(eng, o=o, b=b, qb=qb):
                Ev = sE[b][:]
                ctA = ctxT[:].rearrange("p a b -> p (a b)")
                etf = sE[b][:]
                eng.tensor_mul(sD[:, :, o:o + 4], sD[:, :, o:o + 4], qb[:, :, 128 + o:128 + o + 4])
                eng.tensor_mul(sS[b][:].rearrange("p (c m d) -> p c m d", c=32, m=32),
                               Ev.rearrange("p (c m d) -> p c m d", c=32, m=32),
                               sD[:, :, o:o + 4].unsqueeze(2).broadcast_to([128, 32, 32, 4]))
                eng.tensor_add(cview(ctA, 16), cview(sS[b][:], 32)[:, 0:16, :],
                               cview(sS[b][:], 32)[:, 16:32, :])
                eng.tensor_add(cview(etf[:, 0:1024], 8), cview(ctA, 16)[:, 0:8, :],
                               cview(ctA, 16)[:, 8:16, :])
                eng.tensor_add(cview(etf[:, 1024:1536], 4), cview(etf[:, 0:1024], 8)[:, 0:4, :],
                               cview(etf[:, 0:1024], 8)[:, 4:8, :])
                eng.tensor_add(cview(etf[:, 1536:1792], 2), cview(etf[:, 1024:1536], 4)[:, 0:2, :],
                               cview(etf[:, 1024:1536], 4)[:, 2:4, :])
                return eng.tensor_add(
                    ctx[:, :, o:o + 4],
                    etf[:, 1536:1664].rearrange("p (m d) -> p m d", m=32),
                    etf[:, 1664:1792].rearrange("p (m d) -> p m d", m=32))
            s.step("vector", f, reads=[f"sE{b}", "sDG"] + QK(t),
                   writes=["sDG", f"ctx{dc}", f"sS{b}", f"sE{b}"] + SCR, extra_deps=[hA])

        def emit_tail(t):
            rk = f"r{t}"
            cflat = ctx[:].rearrange("p a b -> p (a b)")
            for g in range(2):
                def f(eng, g=g):
                    last = None
                    for i in range(8):
                        last = eng.transpose(pTb[g % 2][:, i, :], cflat[:, ts(g * 8 + i)], ident[:])
                    return last
                s.step("tensor", f, reads=CTX_KEYS + ["id"], writes=[f"pTb{g % 2}"])
                s.step("scalar", (lambda eng, g=g: eng.copy(
                    ctxT[:, g * 8:(g + 1) * 8, :], pTb[g % 2][:])),
                    reads=[f"pTb{g % 2}"], writes=[f"cTg{g}"])
            for mg in range(4):
                def f(eng, mg=mg):
                    last = None
                    for i in range(8):
                        m = mg * 8 + i
                        j, half = m // 2, m % 2
                        eng.matmul(pU[:, m * 64:(m + 1) * 64],
                                   ctxT[:, j, :][half * 64:half * 64 + 64, :],
                                   aWo2[half * 64:half * 64 + 64, :], start=True, stop=False)
                        last = eng.matmul(pU[:, m * 64:(m + 1) * 64], ones[:], abo[:],
                                          start=False, stop=True)
                    return last
                s.step("tensor", f, reads=["cTg0", "cTg1", "aWo2", "abo", "ones"],
                       writes=[f"pU{mg}"])

            for mg in range(4):
                s.step("vector", (lambda eng, t=t, mg=mg: eng.tensor_add(
                    rbuf[:, t, mg * 512:(mg + 1) * 512],
                    rbuf[:, t, mg * 512:(mg + 1) * 512],
                    pU[:, mg * 512:(mg + 1) * 512])),
                    reads=[rk, f"pU{mg}"], writes=[rk])

            ln_stats(t)

            def f(eng):
                eng.activation(out=sc[:, 0:1], in_=mv[:, 1:2], func=AF.Ln, bias=epsc[:])
                eng.activation(out=sc[:, 1:2], in_=mv[:, 1:2], func=AF.Ln, scale=float(DM))
                return eng.activation(out=sc[:, 2:3], in_=sc[:, 0:1], func=AF.Exp, scale=-0.5)
            s.step("scalar", f, reads=["mv"], writes=["sc02"])

            def f(eng):
                eng.tensor_sub(sc[:, 3:4], sc[:, 1:2], sc[:, 0:1])
                return eng.scalar_tensor_tensor(out=sc[:, 4:5], in0=mv[:, 0:1], scalar=-1.0,
                                                in1=sc[:, 2:3], op0=ALU.mult, op1=ALU.mult)
            s.step("vector", f, reads=["mv", "sc02"], writes=["sc4", "sc3"])

            def f(eng, t=t):
                eng.activation(out=sc[:, 5:6], in_=sc[:, 3:4], func=AF.Exp, scale=-0.5)
                return eng.activation(out=hbf[:], in_=rbuf[:, t, :], func=AF.Identity,
                                      scale=sc[:, 2:3], bias=sc[:, 4:5])
            s.step("scalar", f, reads=[rk, "sc02", "sc4", "sc3"], writes=["hbf", "sc5"])

            h6 = s.step("vector", (lambda eng, t=t: eng.tensor_mul(
                s6b[:, t:t + 1], sc[:, 2:3], sc[:, 5:6])),
                reads=["sc02", "sc5"], writes=["s6b"])
            s.step("vector", (lambda eng, t=t: eng.tensor_mul(
                s7b[:, t:t + 1], mv[:, 0:1], s6b[:, t:t + 1])),
                reads=["mv", "s6b"], writes=["s7b"], extra_deps=[h6])

            for g in range(2):
                def f(eng, g=g):
                    last = None
                    for i in range(8):
                        last = eng.transpose(pTb[g % 2][:, i, :], hbf[:, ts(g * 8 + i)], ident[:])
                    return last
                s.step("tensor", f, reads=["hbf", "id"], writes=[f"pTb{g % 2}"])
                s.step("scalar", (lambda eng, t=t, g=g: eng.copy(
                    hT2[:, t * 16 + g * 8:t * 16 + (g + 1) * 8, :], pTb[g % 2][:])),
                    reads=[f"pTb{g % 2}"], writes=[f"hT2_{t}_{g}"])

            for kg in range(4):
                def f(eng, t=t, kg=kg):
                    last = None
                    for i in range(4):
                        last = eng.transpose(pU[:, 1536 + i * 128:1664 + i * 128],
                                             rbuf[:, t, ts(kg * 4 + i)], ident32[:])
                    return last
                s.step("tensor", f, reads=[rk, "id32"], writes=["pU3"])
                s.step("vector", (lambda eng: eng.tensor_copy(
                    rT32b[:].rearrange("p a b -> p (a b)"), pU[:, 1536:2048])),
                    reads=["pU3"], writes=["rT32"])
                def f(eng, kg=kg):
                    last = None
                    for i in range(4):
                        last = eng.matmul(pQ4[:, 0:NE], rT32b[:, i, :],
                                          sim32[:, kg * 4 + i, :],
                                          start=(kg == 0 and i == 0),
                                          stop=(kg == 3 and i == 3), skip_group_check=True)
                    return last
                s.step("tensor", f, reads=["rT32", "sim32"],
                       writes=["pQmm", "pQb0", "pQb1"])

            h = s.step("vector", (lambda eng, t=t: eng.tensor_copy(
                Gall[:, :, t:t + 1], pQ4[:, 0:NE].unsqueeze(2))),
                reads=["pQmm", "pQb0"], writes=["G0"])

            s.step("sync", (lambda eng, t=t: eng.dma_start(out=r1_out[t], in_=rbuf[:, t, :])),
                   reads=[rk], writes=[f"st_r{t}"])
            return h

        emit_front(0)
        for g2 in range(16):
            emit_qkv2(0, g2)
        emit_S(0, 0)
        for t in range(NT):
            for dc in range(16):
                if not (dc == 0 and t > 0):
                    emit_exp(t, dc)
                if dc + 1 < 16:
                    emit_S(t, dc + 1)
                emit_trees(t, dc)
                if t + 1 < NT:
                    if dc == 1:
                        emit_front(t + 1)
                    elif 2 <= dc < 10:
                        emit_qkv2(t + 1, (dc - 2) * 2)
                        emit_qkv2(t + 1, (dc - 2) * 2 + 1)
            if t + 1 < NT:
                emit_S(t + 1, 0)
                emit_exp(t + 1, 0)
            hGall = emit_tail(t)

        wgT = wg[:].rearrange("p a b -> p b a")
        emit_chain(s, gating_chain(Gv, T1g, scB, mk8, mkT, s6b, s7b, csrT, snrT,
                                   sgrT, prioT, wgT, G4flat),
                   ["G0", "s6b", "s7b", "csr", "snr", "sgr", "prio"],
                   [f"w{t}" for t in range(NT)], first_deps=[hGall])

        barrier = [("vector", s.count["vector"]), ("scalar", s.count["scalar"]),
                   ("tensor", s.count["tensor"])]
        inner.close()
        wcat_eg = stk.enter_context(nc.sbuf_tensor("wcat_eg", [128, 16, 1536], BF16))
        qkvm = stk.enter_context(nc.sbuf_tensor("qkvm_s", [128, NT, 384], F32))

        for eg in range(4):
            for k in range(16):
                s.step("gpsimd", (lambda eng, eg=eg, k=k: eng.dma_start(
                    out=wcat_eg[:, k, :], in_=wcat_in[k, :, eg * 1536:(eg + 1) * 1536])),
                    reads=[], writes=[f"wc{k}"],
                    extra_deps=(barrier if (eg == 0 and k < 2) else ()))
            for t in range(NT):
                for k in range(16):
                    def f(eng, t=t, k=k):
                        eng.matmul(pU[:, 0:512], hT2[:, t * 16 + k, :],
                                   wcat_eg[:, k, 0:512], start=(k == 0), stop=(k == 15))
                        eng.matmul(pU[:, 512:1024], hT2[:, t * 16 + k, :],
                                   wcat_eg[:, k, 512:1024], start=(k == 0), stop=(k == 15))
                        return eng.matmul(pU[:, 1024:1536], hT2[:, t * 16 + k, :],
                                          wcat_eg[:, k, 1024:1536], start=(k == 0), stop=(k == 15))
                    s.step("tensor", f, reads=[f"hT2_{t}_{k // 8}", f"wc{k}"],
                           writes=["pUmix"])
                h = None
                for i in range(4):
                    if eg == 0 and i == 0:
                        def f(eng, t=t):
                            return eng.tensor_scalar(
                                out=qkvm[:, t, :], in0=pU[:, 0:384],
                                scalar1=wg[:, t, 0:1], scalar2=None, op0=ALU.mult)
                    else:
                        def f(eng, t=t, eg=eg, i=i):
                            return eng.scalar_tensor_tensor(
                                out=qkvm[:, t, :], in0=pU[:, i * 384:(i + 1) * 384],
                                scalar=wg[:, t, eg * 4 + i:eg * 4 + i + 1], in1=qkvm[:, t, :],
                                op0=ALU.mult, op1=ALU.add)
                    h = s.step("vector", f, reads=["pUmix", f"w{t}", "qkvm_s"],
                               writes=["qkvm_s"], extra_deps=[h] if h else ())

        s.step("sync", (lambda eng: eng.dma_start(
            out=qkvm_out, in_=qkvm[:])),
            reads=["qkvm_s"], writes=["st_q"])
        s.step("sync", (lambda eng: eng.dma_start(
            out=w_out, in_=wg[:])),
            reads=[f"w{t}" for t in range(NT)], writes=["st_w"])
        s.wait_all("sync", [("sync", s.count["sync"])] if False else
                   [s.last_w["st_q"], s.last_w["st_w"]] +
                   [s.last_w[f"st_r{t}"] for t in range(NT)])
        s.step("vector", (lambda eng: eng.tensor_copy(sc[:, 0:1], sc[:, 1:2])),
               reads=["st_q", "st_w"] + [f"st_r{t}" for t in range(NT)], writes=["fin"])

        s.build(block, sems)
    return nc


def build_l2():
    nc = bass.Bass("TRN2", target_bir_lowering=False, detect_race_conditions=False)
    r1_in = nc.dram_tensor("r1", [NT, 128, DM], F32, kind="ExternalInput").ap()
    qm_in = nc.dram_tensor("qm", [128, NT, 128], F32, kind="ExternalInput").ap()
    kf_in = nc.dram_tensor("kf", [128, 64, 128], F32, kind="ExternalInput").ap()
    vf_in = nc.dram_tensor("vf", [128, 64, 128], F32, kind="ExternalInput").ap()
    w_in = nc.dram_tensor("w", [128, NT, NE], F32, kind="ExternalInput").ap()
    oc_in = nc.dram_tensor("ocat", [128, NE, DM], F32, kind="ExternalInput").ap()
    wqb_in = nc.dram_tensor("wqb", [128, 48, 16, 128], F32, kind="ExternalInput").ap()
    gam_in = nc.dram_tensor("gam", [1, DM], F32, kind="ExternalInput").ap()
    bet_in = nc.dram_tensor("bet", [1, DM], F32, kind="ExternalInput").ap()
    msk_in = nc.dram_tensor("msk", [128, 1024], F32, kind="ExternalInput").ap()
    ident_in = nc.dram_tensor("ident", [128, 128], F32, kind="ExternalInput").ap()

    r25_out = nc.dram_tensor("r25", [NT, 128, DM], F32, kind="ExternalOutput").ap()
    q2T_out = nc.dram_tensor("q2T", [16, 128, TPC], BF16, kind="ExternalOutput").ap()
    k2T_out = nc.dram_tensor("k2T", [16, 128, TPC], BF16, kind="ExternalOutput").ap()
    v2_out = nc.dram_tensor("v2", [128, 16, NT, 128], BF16, kind="ExternalOutput").ap()

    QS = 1.0 / float(np.sqrt(128.0))
    QS2 = 1.0 / float(np.sqrt(2048.0))

    s = Sched()
    with ExitStack() as stk:
        E = stk.enter_context
        rbuf = E(nc.sbuf_tensor("rbuf", [128, NT, DM], F32))
        qbf = E(nc.sbuf_tensor("qbf", [128, NT, 128], BF16))
        aos = E(nc.sbuf_tensor("aos", [128, NT, 128], BF16))
        wgs = E(nc.sbuf_tensor("wgs", [128, NT, NE], F32))
        ident = E(nc.sbuf_tensor("idents", [128, 128], BF16))
        mskA = E(nc.sbuf_tensor("mskA", [128, 1024], F32))
        q_fm = E(nc.sbuf_tensor("q_fm", [128, 128], BF16))
        tmpb = E(nc.sbuf_tensor("tmpb", [128, 4, 128], BF16))
        orng4 = [E(nc.sbuf_tensor(f"orng4_{i}", [128, 4, DM], BF16)) for i in range(2)]
        bst = E(nc.sbuf_tensor("bst", [128, 24], F32))
        mv = E(nc.sbuf_tensor("mv", [128, 2], F32))
        sc = E(nc.sbuf_tensor("scs", [128, 8], F32))
        epsc = E(nc.sbuf_tensor("epscs", [128, 1], F32))
        PTb = [E(nc.sbuf_tensor(f"PTb{i}", [128, 8, 128], BF16)) for i in range(2)]
        st1 = ExitStack()
        S_sb = st1.enter_context(nc.sbuf_tensor("S_sb", [128, 8192], F32))
        P_sb = st1.enter_context(nc.sbuf_tensor("P_sb", [128, 8192], BF16))
        k_fm = st1.enter_context(nc.sbuf_tensor("k_fm", [128, 64, 128], BF16))
        v_sb = st1.enter_context(nc.sbuf_tensor("v_sb", [128, 64, 128], BF16))

        pS = [E(nc.psum_tensor(f"pS{i}", [128, 512], F32)) for i in range(2)]
        pU = E(nc.psum_tensor("pU", [128, DM], F32))
        pTb = [E(nc.psum_tensor(f"pTb{i}", [128, 8, 128], BF16)) for i in range(2)]

        sems = {l: E(nc.semaphore(f"SEM_{l}")) for l in LANES}
        block = E(nc.Block())

        for t in range(NT):
            s.step("sync", (lambda eng, t=t: eng.dma_start(out=rbuf[:, t, :], in_=r1_in[t])),
                   writes=[f"r{t}"])
        s.step("sync", (lambda eng: eng.dma_start(out=wgs[:], in_=w_in)),
               writes=["wgs"])
        s.step("sync", (lambda eng: eng.dma_start(out=mskA[:], in_=msk_in)), writes=["mskA"])
        s.step("gpsimd", (lambda eng: eng.dma_start(
            out=P_sb[:].bitcast(F32)[:, 0:NT * 128].rearrange("p (t f) -> p t f", t=NT),
            in_=qm_in)), writes=["PSBUF"])
        s.step("gpsimd", (lambda eng: eng.dma_start(
            out=v_sb[:], in_=vf_in)), writes=["v_sb"])
        s.step("gpsimd", (lambda eng: eng.dma_start(
            out=S_sb[:].bitcast(BF16)[:, 0:8192].rearrange("p (a b) -> p a b", a=64),
            in_=kf_in)), writes=["kbfS"])
        s.step("gpsimd", (lambda eng: eng.dma_start(out=ident[:], in_=ident_in)),
               writes=["id"])

        def f(eng):
            eng.memset(epsc[:], EPS)
            return eng.tensor_scalar(
                out=qbf[:].rearrange("p a b -> p (a b)"),
                in0=P_sb[:].bitcast(F32)[:, 0:NT * 128], scalar1=QS, scalar2=None,
                op0=ALU.mult)
        s.step("vector", f, reads=["PSBUF"], writes=["qbf", "eps"])

        for g in range(8):
            def f(eng, g=g):
                last = None
                for i in range(8):
                    b = g * 8 + i
                    last = eng.transpose(pTb[g % 2][:, i, :],
                                         S_sb[:].bitcast(BF16)[:, b * 128:(b + 1) * 128], ident[:])
                return last
            s.step("tensor", f, reads=["kbfS", "id"], writes=[f"pTb{g % 2}"])
            s.step("scalar", (lambda eng, g=g: eng.copy(
                k_fm[:, g * 8:(g + 1) * 8, :], pTb[g % 2][:])),
                reads=[f"pTb{g % 2}"], writes=["k_fm"])

        for j in range(NT):
            nch = 2 * j + 2
            L = nch * 512
            s.step("tensor", (lambda eng, j=j: eng.transpose(
                pTb[0][:, 0, :], qbf[:, j, :], ident[:])), reads=["qbf", "id"], writes=["pTb0"])
            s.step("scalar", (lambda eng: eng.copy(q_fm[:], pTb[0][:, 0, :])),
                   reads=["pTb0"], writes=["q_fm"])
            for ch in range(nch):
                s.step("tensor", (lambda eng, ch=ch: eng.matmul(
                    pS[ch % 2][:], q_fm[:], k_fm[:].rearrange("p a b -> p (a b)")[:, ch * 512:(ch + 1) * 512],
                    start=True, stop=True)),
                    reads=["q_fm", "k_fm"], writes=[f"pS{ch % 2}"])
                if ch >= nch - 2:
                    mo = (ch - (nch - 2)) * 512
                    def f(eng, ch=ch, mo=mo):
                        return eng.scalar_tensor_tensor(
                            out=S_sb[:, ch * 512:(ch + 1) * 512], in0=pS[ch % 2][:],
                            scalar=1.0, in1=mskA[:, mo:mo + 512],
                            op0=ALU.mult, op1=ALU.add)
                    hev = s.step("vector", f, reads=[f"pS{ch % 2}", "mskA"], writes=["Ssb"])
                else:
                    hev = s.step("vector", (lambda eng, ch=ch: eng.tensor_copy(
                        S_sb[:, ch * 512:(ch + 1) * 512], pS[ch % 2][:])),
                        reads=[f"pS{ch % 2}"], writes=["Ssb"])
            h1 = s.step("vector", (lambda eng, L=L: eng.reduce_max(
                sc[:, 0:1], S_sb[:, 0:L], axis=AX.X)), reads=["Ssb"], writes=["smx"],
                extra_deps=[hev])
            s.step("vector", (lambda eng: eng.tensor_scalar(
                out=sc[:, 1:2], in0=sc[:, 0:1], scalar1=-1.0, scalar2=None, op0=ALU.mult)),
                reads=["smx"], writes=["smx"], extra_deps=[h1])
            s.step("scalar", (lambda eng, L=L: eng.activation(
                out=P_sb[:, 0:L], in_=S_sb[:, 0:L], func=AF.Exp, bias=sc[:, 1:2])),
                reads=["Ssb", "smx"], writes=["PSBUF"])
            h2 = s.step("vector", (lambda eng, L=L: eng.reduce_sum(
                sc[:, 2:3], P_sb[:, 0:L], axis=AX.X)), reads=["PSBUF"], writes=["ssum"])
            h3 = s.step("vector", (lambda eng: eng.reciprocal(sc[:, 3:4], sc[:, 2:3])),
                        reads=["ssum"], writes=["srec"], extra_deps=[h2])
            nb = nch * 4
            ng = nb // 8
            for g in range(ng):
                def f(eng, g=g):
                    last = None
                    for i in range(8):
                        b2 = g * 8 + i
                        last = eng.transpose(pTb[g % 2][:, i, :],
                                             P_sb[:, b2 * 128:(b2 + 1) * 128], ident[:])
                    return last
                s.step("tensor", f, reads=["PSBUF", "id"], writes=[f"pTb{g % 2}"])
                s.step("scalar", (lambda eng, g=g: eng.copy(PTb[g % 2][:], pTb[g % 2][:])),
                       reads=[f"pTb{g % 2}"], writes=[f"PTk{g % 2}"])
                def f(eng, g=g, ng=ng):
                    last = None
                    for i in range(8):
                        b2 = g * 8 + i
                        last = eng.matmul(pU[:, 0:128], PTb[g % 2][:, i, :], v_sb[:, b2, :],
                                          start=(b2 == 0), stop=(g == ng - 1 and i == 7))
                    return last
                s.step("tensor", f, reads=[f"PTk{g % 2}", "v_sb"], writes=["pUmix"])
            s.step("scalar", (lambda eng, j=j: eng.activation(
                out=aos[:, j, :], in_=pU[:, 0:128], func=AF.Identity, scale=sc[:, 3:4])),
                reads=["pUmix", "srec"], writes=[f"ao{j}"], extra_deps=[h3])


        for e4 in range(4):
            s.step("gpsimd", (lambda eng, e4=e4: eng.dma_start(
                out=orng4[e4 % 2][:], in_=oc_in[:, e4 * 4:(e4 + 1) * 4, :])),
                writes=[f"oc{e4 % 2}"])
            for j in range(NT):
                p = (e4 * NT + j) % 2

                def f(eng, j=j, e4=e4):
                    last = None
                    for i in range(4):
                        last = eng.activation(
                            out=tmpb[:, i, :], in_=aos[:, j, :], func=AF.Identity,
                            scale=wgs[:, j, e4 * 4 + i:e4 * 4 + i + 1])
                    return last
                s.step("scalar", f, reads=[f"ao{j}", "wgs"], writes=["tmpb"])

                def f(eng, p=p):
                    last = None
                    for i in range(4):
                        last = eng.transpose(pTb[p][:, i, :], tmpb[:, i, :], ident[:])
                    return last
                s.step("tensor", f, reads=["tmpb", "id"], writes=[f"pTb{p}"])
                s.step("scalar", (lambda eng, p=p: eng.copy(
                    PTb[p][:, 0:4, :], pTb[p][:, 0:4, :])),
                    reads=[f"pTb{p}"], writes=[f"PTk{p}"])

                def f(eng, e4=e4, p=p):
                    last = None
                    for i in range(4):
                        for q4 in range(4):
                            last = eng.matmul(pU[:, q4 * 512:(q4 + 1) * 512],
                                              PTb[p][:, i, :],
                                              orng4[e4 % 2][:, i, q4 * 512:(q4 + 1) * 512],
                                              start=(i == 0), stop=(i == 3))
                    return last
                s.step("tensor", f, reads=[f"PTk{p}", f"oc{e4 % 2}"], writes=["pUmix"])
                s.step("vector", (lambda eng, j=j: eng.tensor_add(
                    rbuf[:, j, :], rbuf[:, j, :], pU[:])),
                    reads=[f"r{j}", "pUmix"], writes=[f"r{j}"])

        st1.close()
        xnT = stk.enter_context(nc.sbuf_tensor("xnT", [128, 16, TPC], BF16))
        h2fP = [stk.enter_context(nc.sbuf_tensor(f"h2f{i}", [128, DM], F32))
                for i in range(2)]
        xnbP = [stk.enter_context(nc.sbuf_tensor(f"xnb{i}", [128, DM], BF16))
                for i in range(2)]
        bstP = [stk.enter_context(nc.sbuf_tensor(f"bst2_{i}", [128, 24], F32))
                for i in range(2)]
        mvP = [stk.enter_context(nc.sbuf_tensor(f"mv2_{i}", [128, 2], F32))
               for i in range(2)]
        scP = [stk.enter_context(nc.sbuf_tensor(f"sc2_{i}", [128, 8], F32))
               for i in range(2)]
        gamr = stk.enter_context(nc.sbuf_tensor("gamr", [128, DM], BF16))
        betr = stk.enter_context(nc.sbuf_tensor("betr", [128, DM], BF16))
        wrng = [stk.enter_context(nc.sbuf_tensor(f"wrng{i}", [128, 16, 128], BF16)) for i in range(2)]
        stg = stk.enter_context(nc.sbuf_tensor("stg", [128, 1024], BF16))
        vst = stk.enter_context(nc.sbuf_tensor("vst", [128, 8, 128], BF16))

        barrier = [("vector", s.count["vector"]), ("scalar", s.count["scalar"]),
                   ("tensor", s.count["tensor"])]
        s.step("gpsimd", (lambda eng: eng.dma_start(
            out=gamr[:], in_=gam_in.broadcast_to([128, DM]))), writes=["gamr"],
            extra_deps=barrier)
        s.step("gpsimd", (lambda eng: eng.dma_start(
            out=betr[:], in_=bet_in.broadcast_to([128, DM]))), writes=["betr"])

        for t in range(NT):
            rk = f"r{t}"
            pb = t % 2
            bst_, mv_, sc_ = bstP[pb], mvP[pb], scP[pb]
            h2f_, xnb_ = h2fP[pb], xnbP[pb]
            def f(eng, t=t, bst_=bst_, mv_=mv_, sc_=sc_, h2f_=h2f_, xnb_=xnb_):
                for i in range(3):
                    eng.bn_stats(bst_[:, i * 6:(i + 1) * 6], rbuf[:, t, ts(i, 512)])
                return eng.bn_stats(bst_[:, 18:24], rbuf[:, t, ts(3, 512)])
            h = s.step("vector", f, reads=[rk], writes=[f"bst{pb}"])
            s.step("vector", (lambda eng, bst_=bst_, mv_=mv_, sc_=sc_, h2f_=h2f_, xnb_=xnb_: eng.bn_aggr(mv_[:], bst_[:])),
                   reads=[f"bst{pb}"], writes=[f"mv{pb}"], extra_deps=[h])

            def f(eng, bst_=bst_, mv_=mv_, sc_=sc_, h2f_=h2f_, xnb_=xnb_):
                eng.activation(out=sc_[:, 0:1], in_=mv_[:, 1:2], func=AF.Ln, bias=epsc[:])
                return eng.activation(out=sc_[:, 2:3], in_=sc_[:, 0:1], func=AF.Exp, scale=-0.5)
            s.step("scalar", f, reads=[f"mv{pb}", "eps"], writes=[f"sc02{pb}"])

            s.step("vector", (lambda eng, bst_=bst_, mv_=mv_, sc_=sc_, h2f_=h2f_, xnb_=xnb_: eng.scalar_tensor_tensor(
                out=sc_[:, 4:5], in0=mv_[:, 0:1], scalar=-1.0, in1=sc_[:, 2:3],
                op0=ALU.mult, op1=ALU.mult)), reads=[f"mv{pb}", f"sc02{pb}"], writes=[f"sc4{pb}"])

            s.step("scalar", (lambda eng, t=t, bst_=bst_, mv_=mv_, sc_=sc_, h2f_=h2f_, xnb_=xnb_: eng.activation(
                out=h2f_[:], in_=rbuf[:, t, :], func=AF.Identity,
                scale=sc_[:, 2:3], bias=sc_[:, 4:5])),
                reads=[rk, f"sc02{pb}", f"sc4{pb}"], writes=[f"h2f{pb}"])

            s.step("vector", (lambda eng, t=t, bst_=bst_, mv_=mv_, sc_=sc_, h2f_=h2f_, xnb_=xnb_: eng.tensor_add(
                rbuf[:, t, :], rbuf[:, t, :], h2f_[:])),
                reads=[rk, f"h2f{pb}"], writes=[rk])
            s.step("sync", (lambda eng, t=t, bst_=bst_, mv_=mv_, sc_=sc_, h2f_=h2f_, xnb_=xnb_: eng.dma_start(
                out=r25_out[t], in_=rbuf[:, t, :])), reads=[rk], writes=[f"st_r{t}"])

            def f(eng, bst_=bst_, mv_=mv_, sc_=sc_, h2f_=h2f_, xnb_=xnb_):
                for i in range(3):
                    eng.bn_stats(bst_[:, i * 6:(i + 1) * 6], h2f_[:, ts(i, 512)])
                return eng.bn_stats(bst_[:, 18:24], h2f_[:, ts(3, 512)])
            h = s.step("vector", f, reads=[f"h2f{pb}"], writes=[f"bst{pb}"])
            s.step("vector", (lambda eng, bst_=bst_, mv_=mv_, sc_=sc_, h2f_=h2f_, xnb_=xnb_: eng.bn_aggr(mv_[:], bst_[:])),
                   reads=[f"bst{pb}"], writes=[f"mv{pb}"], extra_deps=[h])

            def f(eng, bst_=bst_, mv_=mv_, sc_=sc_, h2f_=h2f_, xnb_=xnb_):
                eng.activation(out=sc_[:, 0:1], in_=mv_[:, 1:2], func=AF.Ln, bias=epsc[:])
                return eng.activation(out=sc_[:, 2:3], in_=sc_[:, 0:1], func=AF.Exp, scale=-0.5)
            s.step("scalar", f, reads=[f"mv{pb}", "eps"], writes=[f"sc02{pb}"])

            s.step("vector", (lambda eng, bst_=bst_, mv_=mv_, sc_=sc_, h2f_=h2f_, xnb_=xnb_: eng.scalar_tensor_tensor(
                out=sc_[:, 4:5], in0=mv_[:, 0:1], scalar=-1.0, in1=sc_[:, 2:3],
                op0=ALU.mult, op1=ALU.mult)), reads=[f"mv{pb}", f"sc02{pb}"], writes=[f"sc4{pb}"])

            s.step("scalar", (lambda eng, bst_=bst_, mv_=mv_, sc_=sc_, h2f_=h2f_, xnb_=xnb_: eng.activation(
                out=xnb_[:], in_=h2f_[:], func=AF.Identity,
                scale=sc_[:, 2:3], bias=sc_[:, 4:5])),
                reads=[f"h2f{pb}", f"sc02{pb}", f"sc4{pb}"], writes=[f"xnb{pb}"])

            def f(eng, bst_=bst_, mv_=mv_, sc_=sc_, h2f_=h2f_, xnb_=xnb_):
                eng.tensor_mul(xnb_[:], xnb_[:], gamr[:])
                return eng.tensor_add(xnb_[:], xnb_[:], betr[:])
            s.step("vector", f, reads=[f"xnb{pb}", "gamr", "betr"], writes=[f"xnb{pb}"])

            for g in range(2):
                def f(eng, g=g, bst_=bst_, mv_=mv_, sc_=sc_, h2f_=h2f_, xnb_=xnb_):
                    last = None
                    for i in range(8):
                        last = eng.transpose(pTb[g % 2][:, i, :],
                                             xnb_[:, ts(g * 8 + i)], ident[:])
                    return last
                s.step("tensor", f, reads=[f"xnb{pb}", "id"], writes=[f"pTb{g % 2}"])
                s.step("scalar", (lambda eng, t=t, g=g, bst_=bst_, mv_=mv_, sc_=sc_, h2f_=h2f_, xnb_=xnb_: eng.copy(
                    xnT[:, g * 8:(g + 1) * 8, t * 128:(t + 1) * 128], pTb[g % 2][:])),
                    reads=[f"pTb{g % 2}"], writes=[f"xnT{k}" for k in range(g * 8, g * 8 + 8)])

        for oc in range(48):
            s.step("gpsimd", (lambda eng, oc=oc: eng.dma_start(
                out=wrng[oc % 2][:], in_=wqb_in[:, oc, :, :])),
                writes=[f"wr{oc % 2}"])
            def f(eng, oc=oc):
                last = None
                for kc in range(16):
                    for hh in range(2):
                        last = eng.matmul(
                            pU[:, hh * 512:hh * 512 + 512], wrng[oc % 2][:, kc, :],
                            xnT[:, kc, hh * 512:hh * 512 + 512],
                            start=(kc == 0), stop=(kc == 15))
                return last
            s.step("tensor", f, reads=[f"wr{oc % 2}"] + [f"xnT{k}" for k in range(16)],
                   writes=["pOC"])
            if oc < 32:
                sc_ = QS2 if oc < 16 else 1.0
                s.step("scalar", (lambda eng, sc_=sc_: eng.activation(
                    out=stg[:], in_=pU[:, 0:1024], func=AF.Identity, scale=sc_)),
                    reads=["pOC"], writes=["stg"])
                dst = q2T_out if oc < 16 else k2T_out
                s.step("sync", (lambda eng, dst=dst, oc=oc: eng.dma_start(
                    out=dst[oc % 16], in_=stg[:])), reads=["stg"], writes=["st_qk"])
            else:
                s.step("scalar", (lambda eng: eng.copy(stg[:], pU[:, 0:1024])),
                       reads=["pOC"], writes=["stg"])
                def f(eng, oc=oc):
                    last = None
                    for b2 in range(8):
                        last = eng.transpose(pTb[oc % 2][:, b2, :],
                                             stg[:, b2 * 128:(b2 + 1) * 128], ident[:])
                    return last
                s.step("tensor", f, reads=["stg", "id"], writes=[f"pTb{oc % 2}"])
                s.step("scalar", (lambda eng, oc=oc: eng.copy(
                    vst[:], pTb[oc % 2][:])),
                    reads=[f"pTb{oc % 2}"], writes=["vst"])
                s.step("sync", (lambda eng, oc=oc: eng.dma_start(
                    out=v2_out[:, oc - 32, :, :],
                    in_=vst[:])), reads=["vst"], writes=["st_v"])

        s.wait_all("sync", [("sync0", s.count["sync0"]), ("sync1", s.count["sync1"])])
        s.build(block, sems)
    return nc


def _host_prep_l2(inputs, r1, qkvm, w):
    op = np.asarray(inputs["L0_ffn_o_proj"])
    wqb = np.asarray(inputs["L0_attn_b_qkv"])
    gam = np.asarray(inputs["L0_attn_b_gamma"]).reshape(1, DM)
    bet = np.asarray(inputs["L0_attn_b_beta"]).reshape(1, DM)
    ident = np.eye(128, dtype=np.float32)
    wqb_r = np.ascontiguousarray(
        wqb.reshape(16, 128, 48, 128).transpose(1, 2, 0, 3))

    kf = np.empty((128, 64, 128), np.float32)
    vf = np.empty((128, 64, 128), np.float32)
    for g in range(64):
        c, t2 = g % 8, g // 8
        kf[:, g, :] = qkvm[c][:, t2, 128:256]
        vf[:, g, :] = qkvm[c][:, t2, 256:384]

    shared = dict(ocat=np.ascontiguousarray(op.transpose(1, 0, 2)), wqb=wqb_r,
                  gam=gam, bet=bet, ident=ident, kf=kf, vf=vf)
    maps = []
    for c in range(NCORE):
        msk = np.zeros((128, 1024), np.float32)
        for i in range(8):
            for r in range(128):
                lim = i * 128 + 128 if i < c else (i * 128 + r + 1 if i == c else 0)
                if lim < (i + 1) * 128:
                    msk[r, i * 128 + max(lim, i * 128) - i * 128 + i * 128:(i + 1) * 128] = -1e30
        msk = np.zeros((128, 1024), np.float32)
        for r in range(128):
            allowed = np.zeros(1024, bool)
            for i in range(8):
                if i < c:
                    allowed[i * 128:(i + 1) * 128] = True
                elif i == c:
                    allowed[i * 128:i * 128 + r + 1] = True
            msk[r, ~allowed] = -1e30
        maps.append(dict(r1=r1[c],
                         qm=np.ascontiguousarray(np.asarray(qkvm[c])[:, :, 0:128]),
                         w=np.asarray(w[c]), msk=msk, **shared))
    return maps


def run_l2(inputs, r1, qkvm, w):
    if "l2" not in _cache:
        _cache["l2"] = build_l2()
    maps = _host_prep_l2(inputs, r1, qkvm, w)
    res = run_bass_kernel_spmd(_cache["l2"], maps, core_ids=list(range(NCORE)))
    return res.results


def build_l3():
    nc = bass.Bass("TRN2", target_bir_lowering=False, detect_race_conditions=False)
    r25_in = nc.dram_tensor("r25", [NT, 128, DM], F32, kind="ExternalInput").ap()
    q2T_in = nc.dram_tensor("q2T", [128, 16, TPC], BF16, kind="ExternalInput").ap()
    k2T_in = nc.dram_tensor("k2Tf", [128, 16, 16, 512], BF16, kind="ExternalInput").ap()
    v2_in = nc.dram_tensor("v2f", [128, 64, DM], BF16, kind="ExternalInput").ap()
    wob_in = nc.dram_tensor("wob", [128, 16, 16, 128], F32, kind="ExternalInput").ap()
    sim_in = nc.dram_tensor("sim", [128, 16, NE], F32, kind="ExternalInput").ap()
    gates_in = nc.dram_tensor("gates", [1, NE], F32, kind="ExternalInput").ap()
    prio_in = nc.dram_tensor("prio", [1, NE], F32, kind="ExternalInput").ap()
    wcat_in = nc.dram_tensor("wcat", [16, 128, NE * 384], F32, kind="ExternalInput").ap()
    msk_in = nc.dram_tensor("msk", [128, 1024], F32, kind="ExternalInput").ap()
    ident_in = nc.dram_tensor("ident", [128, 128], F32, kind="ExternalInput").ap()

    r3_out = nc.dram_tensor("r3", [NT, 128, DM], F32, kind="ExternalOutput").ap()
    qkvm_out = nc.dram_tensor("qkvm", [128, NT, 384], F32, kind="ExternalOutput").ap()
    w_out = nc.dram_tensor("w", [128, NT, NE], F32, kind="ExternalOutput").ap()
    scr_d = nc.dram_tensor("scrd", [4, NE], F32)

    SOFF = [(j * j + j) * 512 for j in range(NT)]
    SLEN = [(2 * j + 2) * 512 for j in range(NT)]

    s = Sched()
    with ExitStack() as stk:
        E = stk.enter_context
        qkvm = E(nc.sbuf_tensor("qkvm_s", [128, NT, 384], F32))
        wg = E(nc.sbuf_tensor("wg", [128, NT, NE], F32))
        ident = E(nc.sbuf_tensor("idents", [128, 128], BF16))
        ident32 = E(nc.sbuf_tensor("id32s", [128, 128], F32))
        mskA = E(nc.sbuf_tensor("mskA", [128, 1024], F32))
        sim32 = E(nc.sbuf_tensor("sim32", [128, 16, NE], F32))
        ssq = E(nc.sbuf_tensor("ssq", [128, 256], F32))
        nrow = E(nc.sbuf_tensor("nrow", [1, 64], F32))
        ones32 = E(nc.sbuf_tensor("ones32", [128, 1], F32))
        rT32b = E(nc.sbuf_tensor("rT32b", [128, 4, 128], F32))
        bst = E(nc.sbuf_tensor("bst", [128, 24], F32))
        mv = E(nc.sbuf_tensor("mv", [128, 2], F32))
        sc = E(nc.sbuf_tensor("scs", [128, 8], F32))
        scj = E(nc.sbuf_tensor("scjs", [128, NT], F32))
        Gall = E(nc.sbuf_tensor("Gall", [128, NE, NT], F32))
        Gv = [Gall[:]] + [E(nc.sbuf_tensor(f"gv{i}", [128, NE, NT], F32))[:]
                          for i in range(5)]
        G4flat = Gv[4].rearrange("p a b -> p (a b)")
        T1g = E(nc.sbuf_tensor("T1g", [128, 8, NT], F32))
        scB = E(nc.sbuf_tensor("scB", [128, 12, NT], F32))
        mk8 = E(nc.sbuf_tensor("mk8s", [128, NE, NT], mybir.dt.uint8))
        mkT = E(nc.sbuf_tensor("mkTs", [128, NT], mybir.dt.uint8))
        s6b = E(nc.sbuf_tensor("s6bs", [128, NT], F32))
        s7b = E(nc.sbuf_tensor("s7bs", [128, NT], F32))
        prioT = E(nc.sbuf_tensor("prioT", [128, NE, NT], F32))
        csrT = E(nc.sbuf_tensor("csrT", [128, NE, NT], F32))
        snrT = E(nc.sbuf_tensor("snrT", [128, NE, NT], F32))
        sgrT = E(nc.sbuf_tensor("sgrT", [128, NE, NT], F32))
        prio_s = E(nc.sbuf_tensor("prio_s", [128, NE], F32))
        csr_s = E(nc.sbuf_tensor("csr_s", [128, NE], F32))
        snr_s = E(nc.sbuf_tensor("snr_s", [128, NE], F32))
        sgr_s = E(nc.sbuf_tensor("sgr_s", [128, NE], F32))
        grow = E(nc.sbuf_tensor("grows", [1, NE], F32))
        epsc = E(nc.sbuf_tensor("epscs", [128, 1], F32))
        BIGS = E(nc.sbuf_tensor("BIGS", [128, 36864], BF16))
        hbf = BIGS[:, 22528:24576]
        aoTb = E(nc.sbuf_tensor("aoTb", [128, 16, TPC], BF16))
        PTb = [E(nc.sbuf_tensor(f"PTb{i}", [128, 8, 128], BF16)) for i in range(2)]

        pS = [E(nc.psum_tensor(f"pS{i}", [128, 512], F32)) for i in range(2)]
        pU = E(nc.psum_tensor("pU", [128, DM], F32))
        pTb = [E(nc.psum_tensor(f"pTb{i}", [128, 8, 128], BF16)) for i in range(2)]

        sems = {l: E(nc.semaphore(f"SEM_{l}")) for l in LANES}
        block = E(nc.Block())

        stA = ExitStack()
        q2sb = stA.enter_context(nc.sbuf_tensor("q2sb", [128, 16, TPC], BF16))
        krng = [stA.enter_context(nc.sbuf_tensor(f"krng{i}", [128, 16, 512], BF16))
                for i in range(2)]

        s.step("sync", (lambda eng: eng.dma_start(out=mskA[:], in_=msk_in)), writes=["mskA"])
        s.step("sync", (lambda eng: eng.dma_start(
            out=q2sb[:], in_=q2T_in)), writes=["q2sb"])
        for dst, srcap, key in ((ident32, ident_in, "id32"),
                                (prio_s, prio_in.broadcast_to([128, NE]), "prio_s"),
                                (grow, gates_in, "grow")):
            s.step("gpsimd", (lambda eng, d=dst, sr=srcap: eng.dma_start(out=d[:], in_=sr)),
                   writes=[key])
        s.step("gpsimd", (lambda eng: eng.dma_start(out=ident[:], in_=ident_in)),
               writes=["id"])
        s.step("gpsimd", (lambda eng: eng.dma_start(
            out=sim32[:], in_=sim_in)), writes=["sim32"])

        def f(eng):
            eng.memset(ones32[:], 1.0)
            eng.memset(epsc[:], EPS)
            eng.memset(qkvm[:], 0.0)
            s32 = sim32[:].rearrange("p a e -> p (a e)")
            return eng.tensor_mul(ssq[:], s32, s32)
        s.step("vector", f, reads=["sim32"], writes=["ssq", "qkvm_s", "ones32", "eps"])

        def f(eng):
            for k in range(16):
                eng.matmul(pS[0][0:1, 0:NE], ones32[:], sim32[:, k, :],
                           start=(k == 0), stop=(k == 15), skip_group_check=True)
            last = None
            for k in range(16):
                last = eng.matmul(pS[0][0:1, 32:32 + NE], ones32[:],
                                  ssq[:, k * NE:(k + 1) * NE],
                                  start=(k == 0), stop=(k == 15), skip_group_check=True)
            return last
        s.step("tensor", f, reads=["ssq", "sim32", "ones32"], writes=["pS0"])

        def f(eng):
            eng.tensor_copy(nrow[:, 0:NE], pS[0][0:1, 0:NE])
            return eng.tensor_copy(nrow[:, 32:32 + NE], pS[0][0:1, 32:32 + NE])
        s.step("vector", f, reads=["pS0"], writes=["nrow"])

        def f(eng):
            eng.activation(out=grow[:], in_=grow[:], func=AF.Exp, scale=-1.0)
            return eng.activation(out=nrow[:, 32:32 + NE], in_=nrow[:, 32:32 + NE], func=AF.Ln)
        s.step("scalar", f, reads=["grow", "nrow"], writes=["grow", "nrow"])
        s.step("scalar", (lambda eng: eng.activation(
            out=nrow[:, 32:32 + NE], in_=nrow[:, 32:32 + NE], func=AF.Exp, scale=-0.5)),
            reads=["nrow"], writes=["nrow"])

        def f(eng):
            eng.tensor_scalar_add(grow[:], grow[:], 1.0)
            return eng.reciprocal(grow[:], grow[:])
        s.step("vector", f, reads=["grow"], writes=["grow"])

        s.step("gpsimd", (lambda eng: eng.dma_start(out=scr_d[0:1, :], in_=grow[:])),
               reads=["grow"], writes=["scr0"])
        s.step("gpsimd", (lambda eng: eng.dma_start(out=scr_d[1:2, :], in_=nrow[:, 0:NE])),
               reads=["nrow"], writes=["scr1"])
        s.step("gpsimd", (lambda eng: eng.dma_start(
            out=scr_d[2:3, :], in_=nrow[:, 32:32 + NE])), reads=["nrow"], writes=["scr2"])
        s.step("gpsimd", (lambda eng: eng.dma_start(
            out=sgr_s[:], in_=scr_d[0:1, :].broadcast_to([128, NE]))),
            reads=["scr0"], writes=["sgr_s"])
        s.step("gpsimd", (lambda eng: eng.dma_start(
            out=csr_s[:], in_=scr_d[1:2, :].broadcast_to([128, NE]))),
            reads=["scr1"], writes=["csr_s"])
        s.step("gpsimd", (lambda eng: eng.dma_start(
            out=snr_s[:], in_=scr_d[2:3, :].broadcast_to([128, NE]))),
            reads=["scr2"], writes=["snr_s"])
        for big, small, bk, sk in ((sgrT, sgr_s, "sgr", "sgr_s"),
                                   (csrT, csr_s, "csr", "csr_s"),
                                   (snrT, snr_s, "snr", "snr_s"),
                                   (prioT, prio_s, "prio", "prio_s")):
            s.step("vector", (lambda eng, big=big, small=small: eng.tensor_copy(
                big[:].rearrange("p a b -> p b a"),
                small[:].unsqueeze(1).broadcast_to([128, NT, NE]))),
                reads=[sk], writes=[bk])

        idx = [0]

        def emit_softmax(j):
            sk = f"Sall{j}"
            h1 = s.step("vector", (lambda eng, j=j: eng.reduce_max(
                sc[:, 0:1], BIGS[:, SOFF[j]:SOFF[j] + SLEN[j]], axis=AX.X)),
                reads=[sk], writes=["smx"])
            s.step("vector", (lambda eng: eng.tensor_scalar(
                out=sc[:, 1:2], in0=sc[:, 0:1], scalar1=-1.0, scalar2=None, op0=ALU.mult)),
                reads=["smx"], writes=["smx"], extra_deps=[h1])
            s.step("scalar", (lambda eng, j=j: eng.activation(
                out=BIGS[:, SOFF[j]:SOFF[j] + SLEN[j]],
                in_=BIGS[:, SOFF[j]:SOFF[j] + SLEN[j]], func=AF.Exp, bias=sc[:, 1:2])),
                reads=[sk, "smx"], writes=[f"Pall{j}"])
            h2 = s.step("vector", (lambda eng, j=j: eng.reduce_sum(
                sc[:, 2:3], BIGS[:, SOFF[j]:SOFF[j] + SLEN[j]], axis=AX.X)),
                reads=[f"Pall{j}"], writes=["ssum"])
            h3 = s.step("vector", (lambda eng, j=j: eng.reciprocal(
                scj[:, j:j + 1], sc[:, 2:3])),
                reads=["ssum"], writes=[f"scj{j}"], extra_deps=[h2])
            s.step("vector", (lambda eng, j=j: eng.tensor_copy(sc[:, 3:4], scj[:, j:j + 1])),
                   reads=[f"scj{j}"], writes=["sc34"], extra_deps=[h3])

        for kc in range(16):
            s.step("gpsimd", (lambda eng, kc=kc: eng.dma_start(
                out=krng[kc % 2][:], in_=k2T_in[:, kc, :, :])),
                writes=[f"kr{kc % 2}"])
            for j in range(NT):
                if 2 * j + 2 <= kc:
                    continue
                p = idx[0] % 2
                idx[0] += 1

                def f(eng, j=j, kc=kc, p=p):
                    last = None
                    for ff in range(16):
                        last = eng.matmul(pS[p][:, 0:512],
                                          q2sb[:, ff, j * 128:(j + 1) * 128],
                                          krng[kc % 2][:, ff, :],
                                          start=(ff == 0), stop=(ff == 15))
                    return last
                s.step("tensor", f, reads=["q2sb", f"kr{kc % 2}"], writes=[f"pS{p}"])
                if kc >= 2 * j:
                    mo = (kc - 2 * j) * 512

                    def f(eng, j=j, kc=kc, p=p, mo=mo):
                        return eng.scalar_tensor_tensor(
                            out=BIGS[:, SOFF[j] + kc * 512:SOFF[j] + (kc + 1) * 512],
                            in0=pS[p][:, 0:512], scalar=1.0, in1=mskA[:, mo:mo + 512],
                            op0=ALU.mult, op1=ALU.add)
                    s.step("vector", f, reads=[f"pS{p}", "mskA"], writes=[f"Sall{j}"])
                else:
                    def f(eng, j=j, kc=kc, p=p):
                        return eng.tensor_copy(
                            BIGS[:, SOFF[j] + kc * 512:SOFF[j] + (kc + 1) * 512],
                            pS[p][:, 0:512])
                    s.step("vector", f, reads=[f"pS{p}"], writes=[f"Sall{j}"])
            if kc % 2 == 1:
                emit_softmax(kc // 2)

        stA.close()

        stC = ExitStack()
        vrng4 = [aoTb[:, i * 8:(i + 1) * 8, :]
                 .rearrange("p a b -> p (a b)")[:, 0:4 * DM]
                 .rearrange("p (a b) -> p a b", a=4) for i in range(2)]
        O4 = stC.enter_context(nc.sbuf_tensor("O4", [128, NT, DM], F32))
        aob = stC.enter_context(nc.sbuf_tensor("aobs", [128, DM], BF16))

        tidx = [0]
        vidx = [0]

        def emit_av_half(tlo, thi):
            for vg8 in range(thi + 1):
                for q in range(2):
                    hg = vg8 * 2 + q
                    vi = vidx[0] % 2
                    vidx[0] += 1
                    s.step("gpsimd", (lambda eng, hg=hg, vi=vi: eng.dma_start(
                        out=vrng4[vi][:], in_=v2_in[:, hg * 4:(hg + 1) * 4, :])),
                        writes=[f"vr{vi}"])
                for j in range(max(tlo, vg8), thi + 1):
                    ti = tidx[0] % 2
                    tidx[0] += 1

                    def f(eng, j=j, vg8=vg8, ti=ti):
                        last = None
                        for i in range(8):
                            cc = SOFF[j] + vg8 * 1024 + i * 128
                            last = eng.transpose(pTb[ti][:, i, :],
                                                 BIGS[:, cc:cc + 128], ident[:])
                        return last
                    s.step("tensor", f, reads=[f"Pall{j}", "id"], writes=[f"pTb{ti}"])
                    s.step("scalar", (lambda eng, ti=ti: eng.copy(PTb[ti][:], pTb[ti][:])),
                           reads=[f"pTb{ti}"], writes=[f"PTk{ti}"])
                    v0, v1 = (vidx[0] - 2) % 2, (vidx[0] - 1) % 2

                    def f(eng, ti=ti, v0=v0, v1=v1):
                        last = None
                        for i in range(8):
                            vs = vrng4[v0 if i < 4 else v1]
                            for q4 in range(4):
                                last = eng.matmul(pU[:, q4 * 512:(q4 + 1) * 512],
                                                  PTb[ti][:, i, :],
                                                  vs[:, i % 4, q4 * 512:(q4 + 1) * 512],
                                                  start=(i == 0), stop=(i == 7))
                        return last
                    s.step("tensor", f, reads=[f"PTk{ti}", f"vr{v0}", f"vr{v1}"],
                           writes=["pUav"])
                    if vg8 == 0:
                        s.step("vector", (lambda eng, j=j: eng.tensor_copy(
                            O4[:, j, :], pU[:])), reads=["pUav"], writes=[f"O{j}"])
                    else:
                        def f(eng, j=j):
                            return eng.scalar_tensor_tensor(
                                out=O4[:, j, :], in0=pU[:], scalar=1.0,
                                in1=O4[:, j, :], op0=ALU.mult, op1=ALU.add)
                        s.step("vector", f, reads=["pUav", f"O{j}"], writes=[f"O{j}"])
            for j in range(tlo, thi + 1):
                s.step("scalar", (lambda eng, j=j: eng.activation(
                    out=aob[:], in_=O4[:, j, :], func=AF.Identity,
                    scale=scj[:, j:j + 1])),
                    reads=[f"O{j}", f"scj{j}"], writes=["aob"])
                for g in range(2):
                    ti = tidx[0] % 2
                    tidx[0] += 1

                    def f(eng, g=g, ti=ti):
                        last = None
                        for i in range(8):
                            last = eng.transpose(pTb[ti][:, i, :],
                                                 aob[:, ts(g * 8 + i)], ident[:])
                        return last
                    s.step("tensor", f, reads=["aob", "id"], writes=[f"pTb{ti}"])
                    s.step("scalar", (lambda eng, j=j, g=g, ti=ti: eng.copy(
                        aoTb[:, g * 8:(g + 1) * 8, j * 128:(j + 1) * 128], pTb[ti][:])),
                        reads=[f"pTb{ti}"], writes=[f"aoT{j}"])

        emit_av_half(0, 7)
        stC.close()

        stR = ExitStack()
        rbuf = stR.enter_context(nc.sbuf_tensor("rbuf", [128, NT, DM], F32))
        stg32 = BIGS[:, 16384:18432].bitcast(F32)
        wrng = [BIGS[:, 18432 + i * 2048:18432 + (i + 1) * 2048]
                .rearrange("p (a b) -> p a b", a=16) for i in range(2)]
        barrierP2 = [("vector", s.count["vector"]), ("scalar", s.count["scalar"]),
                     ("tensor", s.count["tensor"])]

        for t in range(NT):
            s.step("sync", (lambda eng, t=t: eng.dma_start(out=rbuf[:, t, :], in_=r25_in[t])),
                   writes=[f"r{t}"], extra_deps=(barrierP2 if t < 2 else ()))

        AOT_KEYS = [f"aoT{j}" for j in range(NT)]
        for oc in range(16):
            s.step("gpsimd", (lambda eng, oc=oc: eng.dma_start(
                out=wrng[oc % 2][:], in_=wob_in[:, oc, :, :])),
                writes=[f"wo{oc % 2}"],
                extra_deps=(barrierP2 if oc < 2 else ()))
            def f(eng, oc=oc):
                last = None
                for kc in range(16):
                    for hh in range(2):
                        last = eng.matmul(
                            pU[:, hh * 512:hh * 512 + 512], wrng[oc % 2][:, kc, :],
                            aoTb[:, kc, hh * 512:hh * 512 + 512],
                            start=(kc == 0), stop=(kc == 15))
                return last
            s.step("tensor", f, reads=[f"wo{oc % 2}"] + AOT_KEYS, writes=["pOC"])
            s.step("scalar", (lambda eng: eng.copy(stg32[:], pU[:, 0:1024])),
                   reads=["pOC"], writes=["stg32"])
            for t in range(NT):
                s.step("tensor", (lambda eng, t=t: eng.transpose(
                    pS[t % 2][:, 0:128],
                    stg32[:, t * 128:(t + 1) * 128], ident32[:])),
                    reads=["stg32", "id32"], writes=[f"pR{t % 2}"])
                s.step("vector", (lambda eng, t=t, oc=oc: eng.tensor_add(
                    rbuf[:, t, oc * 128:(oc + 1) * 128],
                    rbuf[:, t, oc * 128:(oc + 1) * 128],
                    pS[t % 2][:, 0:128])),
                    reads=[f"pR{t % 2}", f"r{t}"], writes=[f"r{t}"])

        for t in range(NT):
            s.step("sync", (lambda eng, t=t: eng.dma_start(
                out=r3_out[t], in_=rbuf[:, t, :])), reads=[f"r{t}"], writes=[f"st_r{t}"])

        hT2 = BIGS[:, 0:NT * 16 * 128].rearrange("p (a b) -> p a b", a=NT * 16)

        for t in range(NT):
            rk = f"r{t}"

            def f(eng, t=t):
                for i in range(3):
                    eng.bn_stats(bst[:, i * 6:(i + 1) * 6], rbuf[:, t, ts(i, 512)])
                return eng.bn_stats(bst[:, 18:24], rbuf[:, t, ts(3, 512)])
            h = s.step("vector", f, reads=[rk], writes=["bst"])
            s.step("vector", (lambda eng: eng.bn_aggr(mv[:], bst[:])),
                   reads=["bst"], writes=["mv"], extra_deps=[h])

            def f(eng):
                eng.activation(out=sc[:, 0:1], in_=mv[:, 1:2], func=AF.Ln, bias=epsc[:])
                eng.activation(out=sc[:, 1:2], in_=mv[:, 1:2], func=AF.Ln, scale=float(DM))
                return eng.activation(out=sc[:, 2:3], in_=sc[:, 0:1], func=AF.Exp, scale=-0.5)
            s.step("scalar", f, reads=["mv", "eps"], writes=["sc02"])

            def f(eng):
                eng.tensor_sub(sc[:, 3:4], sc[:, 1:2], sc[:, 0:1])
                return eng.scalar_tensor_tensor(out=sc[:, 4:5], in0=mv[:, 0:1], scalar=-1.0,
                                                in1=sc[:, 2:3], op0=ALU.mult, op1=ALU.mult)
            s.step("vector", f, reads=["mv", "sc02"], writes=["sc4", "sc3"])

            def f(eng, t=t):
                eng.activation(out=sc[:, 5:6], in_=sc[:, 3:4], func=AF.Exp, scale=-0.5)
                return eng.activation(out=hbf[:], in_=rbuf[:, t, :], func=AF.Identity,
                                      scale=sc[:, 2:3], bias=sc[:, 4:5])
            s.step("scalar", f, reads=[rk, "sc02", "sc4", "sc3"], writes=["hbf", "sc5"])

            h6 = s.step("vector", (lambda eng, t=t: eng.tensor_mul(
                s6b[:, t:t + 1], sc[:, 2:3], sc[:, 5:6])),
                reads=["sc02", "sc5"], writes=["s6b"])
            s.step("vector", (lambda eng, t=t: eng.tensor_mul(
                s7b[:, t:t + 1], mv[:, 0:1], s6b[:, t:t + 1])),
                reads=["mv", "s6b"], writes=["s7b"], extra_deps=[h6])

            for g in range(2):
                def f(eng, g=g):
                    last = None
                    for i in range(8):
                        last = eng.transpose(pTb[g % 2][:, i, :], hbf[:, ts(g * 8 + i)], ident[:])
                    return last
                s.step("tensor", f, reads=["hbf", "id"], writes=[f"pTb{g % 2}"])
                s.step("scalar", (lambda eng, t=t, g=g: eng.copy(
                    hT2[:, t * 16 + g * 8:t * 16 + (g + 1) * 8, :], pTb[g % 2][:])),
                    reads=[f"pTb{g % 2}"], writes=[f"hT2_{t}_{g}"])

            for kg in range(4):
                def f(eng, t=t, kg=kg):
                    last = None
                    for i in range(4):
                        last = eng.transpose(pU[:, 1536 + i * 128:1664 + i * 128],
                                             rbuf[:, t, ts(kg * 4 + i)], ident32[:])
                    return last
                s.step("tensor", f, reads=[rk, "id32"], writes=["pU3"])
                s.step("vector", (lambda eng: eng.tensor_copy(
                    rT32b[:].rearrange("p a b -> p (a b)"), pU[:, 1536:2048])),
                    reads=["pU3"], writes=["rT32"])
                def f(eng, kg=kg):
                    last = None
                    for i in range(4):
                        last = eng.matmul(pS[0][:, 0:NE], rT32b[:, i, :],
                                          sim32[:, kg * 4 + i, :],
                                          start=(kg == 0 and i == 0),
                                          stop=(kg == 3 and i == 3), skip_group_check=True)
                    return last
                s.step("tensor", f, reads=["rT32", "sim32"], writes=["pQmm"])

            hGall = s.step("vector", (lambda eng, t=t: eng.tensor_copy(
                Gall[:, :, t:t + 1], pS[0][:, 0:NE].unsqueeze(2))),
                reads=["pQmm"], writes=["G0"])

        wgT = wg[:].rearrange("p a b -> p b a")
        emit_chain(s, gating_chain(Gv, T1g[:], scB[:], mk8, mkT, s6b, s7b, csrT,
                                   snrT, sgrT, prioT, wgT, G4flat),
                   ["G0", "s6b", "s7b", "csr", "snr", "sgr", "prio"],
                   [f"w{t}" for t in range(NT)], first_deps=[hGall])

        stR.close()
        stW = ExitStack()
        wcat_eg = stW.enter_context(nc.sbuf_tensor("wcat_eg", [128, 16, 1536], BF16))
        barrier3 = [("vector", s.count["vector"]), ("scalar", s.count["scalar"]),
                    ("tensor", s.count["tensor"]),
                    ("sync0", s.count["sync0"]), ("sync1", s.count["sync1"])]
        firstmix = [True]
        for eg in range(4):
            for k in range(16):
                s.step("gpsimd", (lambda eng, eg=eg, k=k: eng.dma_start(
                    out=wcat_eg[:, k, :], in_=wcat_in[k, :, eg * 1536:(eg + 1) * 1536])),
                    writes=[f"wc{k}"],
                    extra_deps=(barrier3 if (eg == 0 and k < 2) else ()))
            for t in range(NT):
                for k in range(16):
                    def f(eng, t=t, k=k):
                        eng.matmul(pU[:, 0:512], hT2[:, t * 16 + k, :],
                                   wcat_eg[:, k, 0:512], start=(k == 0), stop=(k == 15))
                        eng.matmul(pU[:, 512:1024], hT2[:, t * 16 + k, :],
                                   wcat_eg[:, k, 512:1024], start=(k == 0), stop=(k == 15))
                        return eng.matmul(pU[:, 1024:1536], hT2[:, t * 16 + k, :],
                                          wcat_eg[:, k, 1024:1536], start=(k == 0), stop=(k == 15))
                    wr = ["pUmix", "pOC"] if firstmix[0] else ["pUmix"]
                    firstmix[0] = False
                    s.step("tensor", f, reads=[f"hT2_{t}_{k // 8}", f"wc{k}"], writes=wr)
                h = None
                for i in range(4):
                    def f(eng, t=t, eg=eg, i=i):
                        return eng.scalar_tensor_tensor(
                            out=qkvm[:, t, :], in0=pU[:, i * 384:(i + 1) * 384],
                            scalar=wg[:, t, eg * 4 + i:eg * 4 + i + 1], in1=qkvm[:, t, :],
                            op0=ALU.mult, op1=ALU.add)
                    h = s.step("vector", f, reads=["pUmix", f"w{t}", "qkvm_s"],
                               writes=["qkvm_s"], extra_deps=[h] if h else ())

        s.step("sync", (lambda eng: eng.dma_start(
            out=qkvm_out, in_=qkvm[:])),
            reads=["qkvm_s"], writes=["st_q"])
        s.step("sync", (lambda eng: eng.dma_start(
            out=w_out, in_=wg[:])),
            reads=[f"w{t}" for t in range(NT)], writes=["st_w"])
        s.wait_all("sync", [("sync0", s.count["sync0"]), ("sync1", s.count["sync1"])])
        stW.close()
        s.build(block, sems)
    return nc


def _host_prep_l3(inputs, r25, q2T, k2T, v2):
    wob = np.asarray(inputs["L0_attn_b_o_proj"])
    sim = np.asarray(inputs["L0_ffn_sim_matrix"])
    gates = np.asarray(inputs["L0_ffn_gates"])
    qp = np.asarray(inputs["L0_ffn_q_proj"])
    kp = np.asarray(inputs["L0_ffn_k_proj"])
    vp = np.asarray(inputs["L0_ffn_v_proj"])
    W3 = np.concatenate([qp, kp, vp], axis=2)
    wcat = np.ascontiguousarray(W3.transpose(1, 0, 2).reshape(16, 128, NE * 384))
    wob_r = np.ascontiguousarray(wob.reshape(16, 128, 16, 128).transpose(1, 2, 0, 3))
    sim_r = np.ascontiguousarray(sim.reshape(16, 128, NE).transpose(1, 0, 2))
    prio = np.arange(NE, 0, -1, dtype=np.float32).reshape(1, NE)
    ident = np.eye(128, dtype=np.float32)

    kfm = np.empty((16, 128, 8192), BF)
    v2f = np.empty((128, 64, DM), BF)
    for g in range(64):
        c, t2 = g % 8, g // 8
        kfm[:, :, g * 128:(g + 1) * 128] = k2T[c][:, :, t2 * 128:(t2 + 1) * 128]
        v2f[:, g, :] = v2[c][:, :, t2, :].reshape(128, DM)
    k2Tf = np.ascontiguousarray(
        kfm.reshape(16, 128, 16, 512).transpose(1, 2, 0, 3))

    shared = dict(wob=wob_r, sim=sim_r, gates=gates.reshape(1, NE), prio=prio,
                  wcat=wcat, ident=ident, k2Tf=k2Tf, v2f=v2f)
    maps = []
    for c in range(NCORE):
        msk = np.zeros((128, 1024), np.float32)
        for r in range(128):
            allowed = np.zeros(1024, bool)
            for i in range(8):
                if i < c:
                    allowed[i * 128:(i + 1) * 128] = True
                elif i == c:
                    allowed[i * 128:i * 128 + r + 1] = True
            msk[r, ~allowed] = -1e30
        maps.append(dict(r25=r25[c],
                         q2T=np.ascontiguousarray(np.asarray(q2T[c]).transpose(1, 0, 2)),
                         msk=msk, **shared))
    return maps


def run_l3(inputs, r25, q2T, k2T, v2):
    if "l3" not in _cache:
        _cache["l3"] = build_l3()
    maps = _host_prep_l3(inputs, r25, q2T, k2T, v2)
    res = run_bass_kernel_spmd(_cache["l3"], maps, core_ids=list(range(NCORE)))
    return res.results


def build_l4():
    nc = bass.Bass("TRN2", target_bir_lowering=False, detect_race_conditions=False)
    r3_in = nc.dram_tensor("r3", [NT, 128, DM], F32, kind="ExternalInput").ap()
    qm_in = nc.dram_tensor("qm", [128, NT, 128], F32, kind="ExternalInput").ap()
    kf_in = nc.dram_tensor("kf", [128, 64, 128], F32, kind="ExternalInput").ap()
    vf_in = nc.dram_tensor("vf", [128, 64, 128], F32, kind="ExternalInput").ap()
    w_in = nc.dram_tensor("w", [128, NT, NE], F32, kind="ExternalInput").ap()
    oc_in = nc.dram_tensor("ocat", [128, NE, DM], F32, kind="ExternalInput").ap()
    msk_in = nc.dram_tensor("msk", [128, 1024], F32, kind="ExternalInput").ap()
    ident_in = nc.dram_tensor("ident", [128, 128], F32, kind="ExternalInput").ap()
    out_out = nc.dram_tensor("out", [NT, 128, DM], F32, kind="ExternalOutput").ap()

    QS = 1.0 / float(np.sqrt(128.0))

    s = Sched()
    with ExitStack() as stk:
        E = stk.enter_context
        rbuf = E(nc.sbuf_tensor("rbuf", [128, NT, DM], F32))
        qbf = E(nc.sbuf_tensor("qbf", [128, NT, 128], BF16))
        aos = E(nc.sbuf_tensor("aos", [128, NT, 128], BF16))
        wgs = E(nc.sbuf_tensor("wgs", [128, NT, NE], F32))
        ident = E(nc.sbuf_tensor("idents", [128, 128], BF16))
        mskA = E(nc.sbuf_tensor("mskA", [128, 1024], F32))
        q_fm = E(nc.sbuf_tensor("q_fm", [128, 128], BF16))
        tmpb = E(nc.sbuf_tensor("tmpb", [128, 4, 128], BF16))
        orng4 = [E(nc.sbuf_tensor(f"orng4_{i}", [128, 4, DM], BF16)) for i in range(2)]
        sc = E(nc.sbuf_tensor("scs", [128, 8], F32))
        S_sb = E(nc.sbuf_tensor("S_sb", [128, 8192], F32))
        P_sb = E(nc.sbuf_tensor("P_sb", [128, 8192], BF16))
        k_fm = E(nc.sbuf_tensor("k_fm", [128, 64, 128], BF16))
        v_sb = E(nc.sbuf_tensor("v_sb", [128, 64, 128], BF16))

        pS = [E(nc.psum_tensor(f"pS{i}", [128, 512], F32)) for i in range(2)]
        pU = E(nc.psum_tensor("pU", [128, DM], F32))
        pTb = [E(nc.psum_tensor(f"pTb{i}", [128, 8, 128], BF16)) for i in range(2)]
        PTb = [E(nc.sbuf_tensor(f"PTb{i}", [128, 8, 128], BF16)) for i in range(2)]

        sems = {l: E(nc.semaphore(f"SEM_{l}")) for l in LANES}
        block = E(nc.Block())

        for t in range(NT):
            s.step("sync", (lambda eng, t=t: eng.dma_start(out=rbuf[:, t, :], in_=r3_in[t])),
                   writes=[f"r{t}"])
        s.step("sync", (lambda eng: eng.dma_start(out=wgs[:], in_=w_in)),
               writes=["wgs"])
        s.step("sync", (lambda eng: eng.dma_start(out=mskA[:], in_=msk_in)), writes=["mskA"])
        s.step("gpsimd", (lambda eng: eng.dma_start(
            out=P_sb[:].bitcast(F32)[:, 0:NT * 128].rearrange("p (t f) -> p t f", t=NT),
            in_=qm_in)), writes=["PSBUF"])
        s.step("gpsimd", (lambda eng: eng.dma_start(
            out=v_sb[:], in_=vf_in)), writes=["v_sb"])
        s.step("gpsimd", (lambda eng: eng.dma_start(
            out=S_sb[:].bitcast(BF16)[:, 0:8192].rearrange("p (a b) -> p a b", a=64),
            in_=kf_in)), writes=["kbfS"])
        s.step("gpsimd", (lambda eng: eng.dma_start(out=ident[:], in_=ident_in)),
               writes=["id"])

        def f(eng):
            return eng.tensor_scalar(
                out=qbf[:].rearrange("p a b -> p (a b)"),
                in0=P_sb[:].bitcast(F32)[:, 0:NT * 128], scalar1=QS, scalar2=None,
                op0=ALU.mult)
        s.step("vector", f, reads=["PSBUF"], writes=["qbf"])

        for g in range(8):
            def f(eng, g=g):
                last = None
                for i in range(8):
                    b = g * 8 + i
                    last = eng.transpose(pTb[g % 2][:, i, :],
                                         S_sb[:].bitcast(BF16)[:, b * 128:(b + 1) * 128], ident[:])
                return last
            s.step("tensor", f, reads=["kbfS", "id"], writes=[f"pTb{g % 2}"])
            s.step("scalar", (lambda eng, g=g: eng.copy(
                k_fm[:, g * 8:(g + 1) * 8, :], pTb[g % 2][:])),
                reads=[f"pTb{g % 2}"], writes=["k_fm"])

        for j in range(NT):
            nch = 2 * j + 2
            L = nch * 512
            s.step("tensor", (lambda eng, j=j: eng.transpose(
                pTb[0][:, 0, :], qbf[:, j, :], ident[:])), reads=["qbf", "id"], writes=["pTb0"])
            s.step("scalar", (lambda eng: eng.copy(q_fm[:], pTb[0][:, 0, :])),
                   reads=["pTb0"], writes=["q_fm"])
            hev = None
            for ch in range(nch):
                s.step("tensor", (lambda eng, ch=ch: eng.matmul(
                    pS[ch % 2][:], q_fm[:], k_fm[:].rearrange("p a b -> p (a b)")[:, ch * 512:(ch + 1) * 512],
                    start=True, stop=True)),
                    reads=["q_fm", "k_fm"], writes=[f"pS{ch % 2}"])
                if ch >= nch - 2:
                    mo = (ch - (nch - 2)) * 512
                    def f(eng, ch=ch, mo=mo):
                        return eng.scalar_tensor_tensor(
                            out=S_sb[:, ch * 512:(ch + 1) * 512], in0=pS[ch % 2][:],
                            scalar=1.0, in1=mskA[:, mo:mo + 512],
                            op0=ALU.mult, op1=ALU.add)
                    hev = s.step("vector", f, reads=[f"pS{ch % 2}", "mskA"], writes=["Ssb"])
                else:
                    hev = s.step("vector", (lambda eng, ch=ch: eng.tensor_copy(
                        S_sb[:, ch * 512:(ch + 1) * 512], pS[ch % 2][:])),
                        reads=[f"pS{ch % 2}"], writes=["Ssb"])
            h1 = s.step("vector", (lambda eng, L=L: eng.reduce_max(
                sc[:, 0:1], S_sb[:, 0:L], axis=AX.X)), reads=["Ssb"], writes=["smx"],
                extra_deps=[hev])
            s.step("vector", (lambda eng: eng.tensor_scalar(
                out=sc[:, 1:2], in0=sc[:, 0:1], scalar1=-1.0, scalar2=None, op0=ALU.mult)),
                reads=["smx"], writes=["smx"], extra_deps=[h1])
            s.step("scalar", (lambda eng, L=L: eng.activation(
                out=P_sb[:, 0:L], in_=S_sb[:, 0:L], func=AF.Exp, bias=sc[:, 1:2])),
                reads=["Ssb", "smx"], writes=["PSBUF"])
            h2 = s.step("vector", (lambda eng, L=L: eng.reduce_sum(
                sc[:, 2:3], P_sb[:, 0:L], axis=AX.X)), reads=["PSBUF"], writes=["ssum"])
            h3 = s.step("vector", (lambda eng: eng.reciprocal(sc[:, 3:4], sc[:, 2:3])),
                        reads=["ssum"], writes=["srec"], extra_deps=[h2])
            nb = nch * 4
            ng = nb // 8
            for g in range(ng):
                def f(eng, g=g):
                    last = None
                    for i in range(8):
                        b2 = g * 8 + i
                        last = eng.transpose(pTb[g % 2][:, i, :],
                                             P_sb[:, b2 * 128:(b2 + 1) * 128], ident[:])
                    return last
                s.step("tensor", f, reads=["PSBUF", "id"], writes=[f"pTb{g % 2}"])
                s.step("scalar", (lambda eng, g=g: eng.copy(PTb[g % 2][:], pTb[g % 2][:])),
                       reads=[f"pTb{g % 2}"], writes=[f"PTk{g % 2}"])
                def f(eng, g=g, ng=ng):
                    last = None
                    for i in range(8):
                        b2 = g * 8 + i
                        last = eng.matmul(pU[:, 0:128], PTb[g % 2][:, i, :], v_sb[:, b2, :],
                                          start=(b2 == 0), stop=(g == ng - 1 and i == 7))
                    return last
                s.step("tensor", f, reads=[f"PTk{g % 2}", "v_sb"], writes=["pUmix"])
            s.step("scalar", (lambda eng, j=j: eng.activation(
                out=aos[:, j, :], in_=pU[:, 0:128], func=AF.Identity, scale=sc[:, 3:4])),
                reads=["pUmix", "srec"], writes=[f"ao{j}"], extra_deps=[h3])

        for e4 in range(4):
            s.step("gpsimd", (lambda eng, e4=e4: eng.dma_start(
                out=orng4[e4 % 2][:], in_=oc_in[:, e4 * 4:(e4 + 1) * 4, :])),
                writes=[f"oc{e4 % 2}"])
            for j in range(NT):
                p = (e4 * NT + j) % 2

                def f(eng, j=j, e4=e4):
                    last = None
                    for i in range(4):
                        last = eng.activation(
                            out=tmpb[:, i, :], in_=aos[:, j, :], func=AF.Identity,
                            scale=wgs[:, j, e4 * 4 + i:e4 * 4 + i + 1])
                    return last
                s.step("scalar", f, reads=[f"ao{j}", "wgs"], writes=["tmpb"])

                def f(eng, p=p):
                    last = None
                    for i in range(4):
                        last = eng.transpose(pTb[p][:, i, :], tmpb[:, i, :], ident[:])
                    return last
                s.step("tensor", f, reads=["tmpb", "id"], writes=[f"pTb{p}"])
                s.step("scalar", (lambda eng, p=p: eng.copy(
                    PTb[p][:, 0:4, :], pTb[p][:, 0:4, :])),
                    reads=[f"pTb{p}"], writes=[f"PTk{p}"])

                def f(eng, e4=e4, p=p):
                    last = None
                    for i in range(4):
                        for q4 in range(4):
                            last = eng.matmul(pU[:, q4 * 512:(q4 + 1) * 512],
                                              PTb[p][:, i, :],
                                              orng4[e4 % 2][:, i, q4 * 512:(q4 + 1) * 512],
                                              start=(i == 0), stop=(i == 3))
                    return last
                s.step("tensor", f, reads=[f"PTk{p}", f"oc{e4 % 2}"], writes=["pUmix"])
                s.step("vector", (lambda eng, j=j: eng.tensor_add(
                    rbuf[:, j, :], rbuf[:, j, :], pU[:])),
                    reads=[f"r{j}", "pUmix"], writes=[f"r{j}"])

        for j in range(NT):
            s.step("sync", (lambda eng, j=j: eng.dma_start(
                out=out_out[j], in_=rbuf[:, j, :])), reads=[f"r{j}"], writes=[f"st{j}"])

        s.wait_all("sync", [("sync0", s.count["sync0"]), ("sync1", s.count["sync1"])])
        s.build(block, sems)
    return nc


def _host_prep_l4(inputs, r3, qkvm3, w3):
    op = np.asarray(inputs["L0_ffn_o_proj"])
    ident = np.eye(128, dtype=np.float32)
    kf = np.empty((128, 64, 128), np.float32)
    vf = np.empty((128, 64, 128), np.float32)
    for g in range(64):
        c, t2 = g % 8, g // 8
        kf[:, g, :] = qkvm3[c][:, t2, 128:256]
        vf[:, g, :] = qkvm3[c][:, t2, 256:384]
    shared = dict(ocat=np.ascontiguousarray(op.transpose(1, 0, 2)), ident=ident,
                  kf=kf, vf=vf)
    maps = []
    for c in range(NCORE):
        msk = np.zeros((128, 1024), np.float32)
        for r in range(128):
            allowed = np.zeros(1024, bool)
            for i in range(8):
                if i < c:
                    allowed[i * 128:(i + 1) * 128] = True
                elif i == c:
                    allowed[i * 128:i * 128 + r + 1] = True
            msk[r, ~allowed] = -1e30
        maps.append(dict(r3=r3[c],
                         qm=np.ascontiguousarray(np.asarray(qkvm3[c])[:, :, 0:128]),
                         w=np.asarray(w3[c]), msk=msk, **shared))
    return maps


def run_l4(inputs, r3, qkvm3, w3):
    if "l4" not in _cache:
        _cache["l4"] = build_l4()
    maps = _host_prep_l4(inputs, r3, qkvm3, w3)
    res = run_bass_kernel_spmd(_cache["l4"], maps, core_ids=list(range(NCORE)))
    return res.results


def _host_prep_l1(inputs):
    x = np.asarray(inputs["L0_attn_a_x"])[0]
    Wqkv = np.asarray(inputs["L0_attn_a_Wqkv"])
    bqkv = np.asarray(inputs["L0_attn_a_bqkv"])
    Wout = np.asarray(inputs["L0_attn_a_Wout"])
    bout = np.asarray(inputs["L0_attn_a_bout"])
    sim = np.asarray(inputs["L0_ffn_sim_matrix"])
    gates = np.asarray(inputs["L0_ffn_gates"])
    qp = np.asarray(inputs["L0_ffn_q_proj"])
    kp = np.asarray(inputs["L0_ffn_k_proj"])
    vp = np.asarray(inputs["L0_ffn_v_proj"])

    aWr = np.ascontiguousarray(Wqkv.transpose(1, 0, 2).reshape(64, 32 * 192))
    aW2 = np.concatenate([aWr, aWr], axis=0)
    ab = bqkv.reshape(1, 32 * 192)
    aWo2 = np.concatenate([Wout, Wout], axis=0)
    abo = bout.reshape(1, 64)
    sim_r = np.ascontiguousarray(sim.reshape(16, 128, NE))
    prio = np.arange(NE, 0, -1, dtype=np.float32).reshape(1, NE)
    W3 = np.concatenate([qp, kp, vp], axis=2)
    wcat = np.ascontiguousarray(W3.transpose(1, 0, 2).reshape(16, 128, NE * 384))
    ident = np.eye(128, dtype=np.float32)

    shared = dict(aW2=np.ascontiguousarray(aW2), ab=np.ascontiguousarray(ab),
                  aWo2=np.ascontiguousarray(aWo2), abo=np.ascontiguousarray(abo),
                  sim=sim_r, gates=gates.reshape(1, NE), prio=prio,
                  wcat=wcat, ident=ident)
    maps = []
    for c in range(NCORE):
        xs = np.stack([x[(8 * j + c) * 128:(8 * j + c + 1) * 128] for j in range(NT)], 0)
        maps.append(dict(x=np.ascontiguousarray(xs), **shared))
    return maps


def run_l1(inputs):
    if "l1" not in _cache:
        _cache["l1"] = build_l1()
    maps = _host_prep_l1(inputs)
    res = run_bass_kernel_spmd(_cache["l1"], maps, core_ids=list(range(NCORE)))
    return res.results


def kernel(**inputs):
    inputs = {k: np.asarray(v) for k, v in inputs.items()}
    res1 = run_l1(inputs)
    r1 = [r["r1"] for r in res1]
    qkvm = [r["qkvm"] for r in res1]
    w = [r["w"] for r in res1]
    res2 = run_l2(inputs, r1, qkvm, w)
    r25 = [r["r25"] for r in res2]
    q2T = [np.asarray(r["q2T"]) for r in res2]
    k2T = [np.asarray(r["k2T"]) for r in res2]
    v2 = [np.asarray(r["v2"]) for r in res2]
    res3 = run_l3(inputs, r25, q2T, k2T, v2)
    r3 = [r["r3"] for r in res3]
    qkvm3 = [r["qkvm"] for r in res3]
    w3 = [r["w"] for r in res3]
    res4 = run_l4(inputs, r3, qkvm3, w3)
    out = np.empty((8192, DM), np.float32)
    for c in range(NCORE):
        o = res4[c]["out"]
        for j in range(NT):
            g = 8 * j + c
            out[g * 128:(g + 1) * 128] = o[j]
    return out.reshape(1, 8192, DM)
